# version 06
# baseline (speedup 1.0000x reference)
import contextlib
from collections import defaultdict
import numpy as np
import ml_dtypes
import concourse.bass as bass
import concourse.mybir as mybir

F32 = mybir.dt.float32
BF16 = mybir.dt.bfloat16
F32R = mybir.dt.float32r
AF = mybir.ActivationFunctionType
OP = mybir.AluOpType

DM, DI, NS, DC, DTR = 1024, 2048, 16, 4, 64
EPS = 1e-5
NKD = DM // 128
NG = DI // 128
T = 512
NSLOT = 2

ENGS = ("sync", "dve", "act", "pe", "gp")


class Prog:

    def __init__(self, nc, sems):
        self.nc = nc
        self.sems = sems
        self.q = {e: [] for e in ENGS}
        self.cnt = defaultdict(int)
        self.trace = []

    def w(self, eng, **waits):
        for s, v in waits.items():
            if v <= 0:
                continue
            sem = self.sems[s]
            self.trace.append(("wait", eng, s, v))
            self.q[eng].append(lambda e, sem=sem, v=v: e.wait_ge(sem, v))

    def w_lazy(self, eng, s, cell):
        sem = self.sems[s]
        self.trace.append(("waitlazy", eng, s, cell))
        def fn(e, sem=sem, cell=cell):
            if cell[0] > 0:
                e.wait_ge(sem, cell[0])
        self.q[eng].append(fn)

    def e(self, eng, fn, inc=None, n=1):
        if inc is None:
            self.trace.append(("op", eng, None, 0))
            self.q[eng].append(lambda e, fn=fn: fn(e))
            return None
        sem = self.sems[inc]
        self.cnt[inc] += n
        self.trace.append(("op", eng, inc, n))
        self.q[eng].append(lambda e, fn=fn, sem=sem, n=n: fn(e).then_inc(sem, n))
        return self.cnt[inc]

    def barrier_snapshot(self):
        return {k: self.cnt[k] for k in ("pe", "dve", "act")}

    def wait_barrier(self, eng, snap):
        self.w(eng, **{k: v for k, v in snap.items() if k != _ENG2SEM.get(eng)})


_ENG2SEM = {"pe": "pe", "dve": "dve", "act": "act"}


def host_prep(inp, L):
    bf = ml_dtypes.bfloat16
    f32 = np.float32
    c = np.ascontiguousarray
    w = {}
    w["win_t"] = c(inp["L0_attn_in_proj_w"].T)
    cw = inp["L0_attn_conv_w"]
    CW = np.zeros((128, DC * NG * 128), f32)
    ar = np.arange(128)
    for k in range(DC):
        for g in range(NG):
            CW[ar, (k * NG + g) * 128 + ar] = cw[g * 128:(g + 1) * 128, 0, k]
    w["cdiag"] = CW
    w["convb"] = c(inp["L0_attn_conv_b"].reshape(NG, 128).T)
    w["xprojt"] = c(inp["L0_attn_x_proj_w"].T.reshape(NG, 128, 96)
                    .transpose(1, 0, 2).reshape(128, NG * 96))
    dtp = inp["L0_attn_dt_proj_w"].T
    w["dtprojt"] = np.concatenate([dtp[:, :1024], dtp[:, 1024:]], 0)
    w["dtb"] = c(inp["L0_attn_dt_proj_b"].reshape(NG, 128).T)
    w["alogt"] = c(inp["L0_attn_A_log"].reshape(NG, 128, NS)
                   .transpose(1, 0, 2).reshape(128, NG * NS))
    w["dcol"] = c(inp["L0_attn_D"].reshape(NG, 128).T)
    w["outprojt"] = c(inp["L0_attn_out_proj_w"].T)
    w["w1t"] = c(inp["L0_attn_ffn_w1"].T)
    w["b1c"] = c(inp["L0_attn_ffn_b1"].reshape(32, 128).T)
    w["w2t"] = c(inp["L0_attn_ffn_w2"].T)
    w["b2c"] = c(inp["L0_attn_ffn_b2"].reshape(8, 128).T)
    w["wfct"] = c(inp["L0_ffn_W_fc"].T)
    w["wprojt"] = c(inp["L0_ffn_W_proj"].T)
    sel = np.zeros((32, 32 * 128), bf)
    for j in range(32):
        sel[j, j * 128:(j + 1) * 128] = 1
    w["selbc"] = sel
    w["id128"] = np.eye(128, dtype=bf)
    w["ones1"] = np.ones((1, 128), f32)
    w["onesk"] = np.ones((128, 1), f32)
    cst = np.zeros((128, 4), f32)
    cst[:, 0] = -1.0
    cst[:, 1] = -0.5
    cst[:, 2] = EPS
    w["cst"] = cst
    maps = []
    x = inp["L0_attn_seq_embeddings"]
    for b in range(8):
        m = dict(w)
        m["x_t"] = c(x[b, :L, :].T)
        maps.append(m)
    return maps


def build(L, taps=()):
    NCH = L // T
    nc = bass.Bass("TRN2", target_bir_lowering=False)
    D = {}

    def din(name, shape, dt=F32):
        D[name] = nc.dram_tensor(name, list(shape), dt, kind="ExternalInput").ap()
        return D[name]

    x_t = din("x_t", (DM, L))
    win_t = din("win_t", (DM, 4096))
    cdiag = din("cdiag", (128, 8192))
    convb = din("convb", (128, NG))
    xprojt = din("xprojt", (128, NG * 96))
    dtprojt = din("dtprojt", (128, 1024))
    dtb = din("dtb", (128, NG))
    alogt = din("alogt", (128, NG * NS))
    dcol = din("dcol", (128, NG))
    outprojt = din("outprojt", (DI, DM))
    w1t = din("w1t", (DM, 4096))
    b1c = din("b1c", (128, 32))
    w2t = din("w2t", (4096, DM))
    b2c = din("b2c", (128, 8))
    wfct = din("wfct", (DM, 4096))
    wprojt = din("wprojt", (4096, DM))
    selbc_d = din("selbc", (32, 4096), BF16)
    id128_d = din("id128", (128, 128), BF16)
    ones1_d = din("ones1", (1, 128))
    onesk_d = din("onesk", (128, 1))
    cst_d = din("cst", (128, 4))

    out_t = nc.dram_tensor("out_t", [DM, L], F32, kind="ExternalOutput").ap()
    c_scr = nc.dram_tensor("c_scr", [DM, L], F32).ap()
    x_bf = nc.dram_tensor("x_bf", [DM, L], BF16).ap()
    tap_d = {}
    for tspec in taps:
        tname, tshape = tspec[0], tspec[1]
        tdt = tspec[2] if len(tspec) > 2 else F32
        tap_d[tname] = nc.dram_tensor("tap_" + tname, list(tshape), tdt,
                                      kind="ExternalOutput").ap()

    es = contextlib.ExitStack()
    with es:
        def sb(name, shape, dt):
            return es.enter_context(nc.sbuf_tensor(name, list(shape), dt))

        ring = [sb(f"ring{i}", (128, 8192), BF16) for i in range(NSLOT)]
        A_x = sb("A_x", (128, 4096), F32R)
        A_xn = sb("A_xn", (128, 4096), BF16)
        A_xm = sb("A_xm", (128, NG * 516), BF16)
        A_sres = sb("A_sres", (128, 8192), BF16)
        A_big4 = sb("A_big4", (128, 16384), BF16)
        A_u = sb("A_u", (128, 8192), BF16)
        A_delta = sb("A_delta", (128, 8192), BF16)
        A_dA = sb("A_dA", (128, 4128), BF16)
        A_dBu = sb("A_dBu", (128, 4128), BF16)
        A_h = sb("A_h", (128, 2064), BF16)
        A_tmp = sb("A_tmp", (128, 2560), F32)
        dt_t = sb("dt_t", (128, T), BF16)
        bcb = sb("bcb", (32, T), BF16)
        carry = sb("carry", (128, NG * NS), BF16)
        xm_tail = sb("xm_tail", (128, NG * 3), BF16)
        fx1 = sb("fx1", (128, 4), F32)
        fx2 = sb("fx2", (128, 4), F32)
        stats_c = sb("stats_c", (128, NCH * 8), F32)
        stats_c2 = sb("stats_c2", (128, NCH * 8), F32)
        ra = A_tmp[0:1, T:2 * T]
        rb = A_tmp[0:1, 2 * T:3 * T]
        sc_red = sb("sc_red", (128, 2), F32)
        sc_red_r = sb("sc_red_r", (128, 2), F32R)
        t_s1 = sb("t_s1", (1, 8), F32)
        t_s1r = sb("t_s1r", (1, 2), F32R)
        rstd_col = sb("rstd_col", (128, 1), F32)
        nmr_col = sb("nmr_col", (128, 1), F32)
        s_xproj = sb("s_xproj", (128, NG * 96), BF16)
        s_dtproj = sb("s_dtproj", (128, 1024), BF16)
        s_dcol = sb("s_dcol", (128, NG), F32)
        ud_t = sb("ud_t", (128, T), BF16)
        du_t = sb("du_t", (128, T), BF16)
        s_selbc = sb("s_selbc", (32, 4096), BF16)
        s_id = sb("s_id", (128, 128), BF16)
        s_ones1r = sb("s_ones1r", (1, 128), F32R)
        s_oneskr = sb("s_oneskr", (128, 1), F32R)
        s_asc = sb("s_asc", (128, NG * NS), F32)
        s_convb = sb("s_convb", (128, NG), F32)
        s_dtb = sb("s_dtb", (128, NG), F32)
        s_b1 = sb("s_b1", (128, 32), F32)
        s_b2 = sb("s_b2", (128, 8), F32)
        s_cst = sb("s_cst", (128, 4), F32)
        s_oneskb = sb("s_oneskb", (128, 1), BF16)
        s_ones1b = sb("s_ones1b", (1, 128), BF16)
        rrb = A_tmp[:].bitcast(BF16)[0:1, 0:2 * T]

        ps = [es.enter_context(nc.psum_tensor(f"ps{i}", [128, T], F32))
              for i in range(8)]

        sems = {n: es.enter_context(nc.semaphore("sem_" + n))
                for n in ("ws0", "ws1", "xv", "cl", "sl", "pe", "dve", "act")}
        block = es.enter_context(nc.Block())
        P = Prog(nc, sems)

        du_bf = None
        m_f32 = A_x[:].bitcast(F32)
        c_f32 = A_u[:].bitcast(F32)
        x2_r = A_sres[:].bitcast(F32)
        ys_f32 = A_delta[:].bitcast(F32)
        mem_f32 = A_xm[:, 0:8192].bitcast(F32)

        def xsl(k):
            return A_delta[:, k * T:(k + 1) * T]

        def xnsl(k):
            return A_xn[:, k * T:(k + 1) * T]

        def usl(g):
            return A_u[:, g * T:(g + 1) * T]

        def dsl(g):
            return A_delta[:, g * T:(g + 1) * T]

        def ysl(g):
            return A_xm[:, g * T:(g + 1) * T]

        def srsl(g):
            return A_sres[:, g * T:(g + 1) * T]

        ring_state = {"fill": 0, "relcell": [[0] for _ in range(NSLOT)]}

        def fill_slot(src_ap):
            s = ring_state["fill"] % NSLOT
            ring_state["fill"] += 1
            P.w_lazy("gp", "pe", ring_state["relcell"][s])
            newcell = [0]
            ring_state["relcell"][s] = newcell
            sname = f"ws{s}"
            wt = P.e("gp", lambda e, dst=ring[s], src=src_ap:
                     e.dma_start(out=dst[:].rearrange("p (k m) -> p k m",
                                                      k=src.shape[1]),
                                 in_=src), inc=sname, n=16)
            return ring[s], newcell, {sname: wt}

        def release_slot(cell):
            cell[0] = P.cnt["pe"]

        P.e("gp", lambda e: e.dma_start(out=s_xproj[:], in_=xprojt[:]), inc="xv", n=16)
        P.e("gp", lambda e: e.dma_start(out=s_dtproj[:], in_=dtprojt[:]), inc="xv", n=16)
        P.e("gp", lambda e: e.dma_start(out=s_dcol[:], in_=dcol[:]), inc="xv", n=16)
        P.e("gp", lambda e: e.dma_start(out=s_ones1r[:], in_=ones1_d[:]), inc="xv", n=16)
        P.e("gp", lambda e: e.dma_start(out=s_oneskr[:], in_=onesk_d[:]), inc="xv", n=16)
        P.e("gp", lambda e: e.dma_start(out=s_ones1b[:], in_=ones1_d[:]), inc="xv", n=16)
        P.e("gp", lambda e: e.dma_start(out=s_oneskb[:], in_=onesk_d[:]), inc="xv", n=16)
        P.e("sync", lambda e: e.dma_start(out=s_selbc[:], in_=selbc_d[:]), inc="sl", n=16)
        P.e("sync", lambda e: e.dma_start(out=s_id[:], in_=id128_d[:]), inc="sl", n=16)
        P.e("gp", lambda e: e.dma_start(out=A_x[:, 0:NG * NS], in_=alogt[:]), inc="xv", n=16)
        P.e("sync", lambda e: e.dma_start(out=s_convb[:], in_=convb[:]), inc="sl", n=16)
        P.e("sync", lambda e: e.dma_start(out=s_dtb[:], in_=dtb[:]), inc="sl", n=16)
        P.e("sync", lambda e: e.dma_start(out=s_b1[:], in_=b1c[:]), inc="sl", n=16)
        P.e("sync", lambda e: e.dma_start(out=s_b2[:], in_=b2c[:]), inc="sl", n=16)
        P.e("sync", lambda e: e.dma_start(out=s_cst[:], in_=cst_d[:]), inc="sl", n=16)
        xbf_done = P.e("gp", lambda e: e.dma_start(
            out=x_bf.rearrange("(k p) t -> p k t", p=128),
            in_=x_t[:].rearrange("(k p) t -> p k t", p=128)), inc="xv", n=16)
        pro_wl, pro_sl = P.cnt["xv"], P.cnt["sl"]
        P.w("gp", xv=pro_wl)

        P.w("act", sl=pro_sl, xv=pro_wl)
        P.e("act", lambda e: e.activation(s_asc[:], A_x[:, 0:NG * NS], AF.Exp),
            inc="act")
        P.w("dve", act=P.cnt["act"])
        P.e("dve", lambda e: e.tensor_scalar(s_asc[:], s_asc[:], -1.0, None,
                                             op0=OP.mult), inc="dve")
        P.e("dve", lambda e: e.memset(A_xm[:], 0.0), inc="dve")
        P.e("dve", lambda e: e.memset(carry[:], 0.0), inc="dve")
        P.e("dve", lambda e: e.memset(A_dA[:], 1.0))
        P.e("dve", lambda e: e.memset(A_dA[:, 0:4128:516], 0.0))
        P.e("dve", lambda e: e.memset(A_dBu[:], 0.0), inc="dve")

        def tap(name, tile_ap, wait_snap):
            if name in tap_d:
                P.wait_barrier("sync", wait_snap)
                tv = P.e("sync", lambda e, t=tile_ap, o=tap_d[name]:
                         e.dma_start(out=o[:], in_=t), inc="sl", n=16)
                for eng in ("dve", "act", "pe", "gp"):
                    P.w(eng, sl=tv)

        cstore_sl = [0]
        fh = {}

        def front_head(c, first):
            if first:
                snapH = P.barrier_snapshot()
                P.wait_barrier("sync", snapH)
                P.w("sync", xv=xbf_done)
                snapL = snapH
            else:
                P.w("sync", dve=fh["du15"], act=fh["exp15"])
                snapL = fh["op_done"]
            xl = P.e("sync", lambda e, c=c: e.dma_start(
                out=A_delta[:, 0:4096].rearrange("p (k t) -> p k t", k=NKD),
                in_=x_bf[:, c * T:(c + 1) * T].rearrange("(k p) t -> p k t", p=128)),
                inc="xv", n=16)
            ln_stats(P, ps, xsl, A_h[:, 0:2 * T], ra, rb, rrb,
                     A_tmp[0:1, 0:T], s_oneskb, s_ones1b, s_cst, xl, snapL)
            mu_sb = A_tmp[:, 0:T]
            rstd_sb = A_tmp[:, T:2 * T]
            P.e("dve", lambda e: e.tensor_copy(mu_sb, ps[6][:]))
            fh["murow"] = P.e("dve", lambda e: e.tensor_copy(rstd_sb, ps[7][:]),
                              inc="dve")
            if not first:
                P.w("dve", pe=P.cnt["pe"])
            xd = None
            for k in range(NKD):
                P.e("dve", lambda e, k=k, m_=mu_sb: e.tensor_tensor(
                    xnsl(k), xsl(k), m_, OP.subtract))
                xd = P.e("dve", lambda e, k=k, r_=rstd_sb: e.tensor_tensor(
                    xnsl(k), xnsl(k), r_, OP.mult),
                    inc="dve" if k == NKD - 1 else None)
            fh["xn_done"] = xd
            tap("xn", A_xn[:], P.barrier_snapshot())

        for c in range(NCH):
            if c == NCH - 1:
                tap("carry", carry[:], P.barrier_snapshot())
            if tap_d:
                for eng in ("dve", "act", "pe", "gp", "sync"):
                    P.w(eng, sl=P.cnt["sl"], wl=P.cnt["wl"])
            if c == 0:
                front_head(0, True)
            ip_slots = []
            for j in range(4):
                srcw = win_t[:, j * 1024:(j + 1) * 1024].rearrange(
                    "(k p) m -> p k m", p=128)
                ip_slots.append(fill_slot(srcw))
            cd_slot = fill_slot(cdiag[:].rearrange("p (k m) -> p k m", k=1))
            op_slots = []
            for j in range(2):
                srcw = outprojt[j * 1024:(j + 1) * 1024, :].rearrange(
                    "(k p) m -> p k m", p=128)
                op_slots.append(fill_slot(srcw))
            w1_slots = []
            for j in range(4):
                srcw = w1t[:, j * 1024:(j + 1) * 1024].rearrange(
                    "(k p) m -> p k m", p=128)
                w1_slots.append(fill_slot(srcw))
            w2_slots = []
            for j in range(4):
                srcw = w2t[j * 1024:(j + 1) * 1024, :].rearrange(
                    "(k p) m -> p k m", p=128)
                w2_slots.append(fill_slot(srcw))
            if c > 0:
                for g in range(NG):
                    P.e("dve", lambda e, g=g: e.tensor_copy(
                        A_xm[:, g * 516:g * 516 + 3],
                        xm_tail[:, g * 3:(g + 1) * 3]), inc="dve")

            evac_cnt = {}
            for j in range(4):
                slot, sidx, wl_t = ip_slots[j]
                for mt in range(8):
                    mabs = j * 8 + mt
                    bank = ps[mt % 2]
                    P.w("pe", dve=P.cnt["dve"] if mabs == 0 else 0, **wl_t)
                    if mabs >= 2:
                        P.w("pe", **{evac_cnt[mabs - 2][0]: evac_cnt[mabs - 2][1]})
                    for k in range(NKD):
                        pe_c = P.e("pe", lambda e, b=bank, s=slot, k=k, mt=mt:
                                   e.matmul(b[:], s[:, k * 1024 + mt * 128:
                                                   k * 1024 + (mt + 1) * 128],
                                            xnsl(k), start=(k == 0), stop=(k == 7)),
                                   inc="pe" if k == 7 else None)
                    if mabs < NG:
                        P.w("dve", pe=pe_c)
                        dc = P.e("dve", lambda e, b=bank, g=mabs: e.tensor_copy(
                            A_xm[:, g * 516 + 3:g * 516 + 515], b[:]), inc="dve")
                        evac_cnt[mabs] = ("dve", dc)
                    else:
                        P.w("act", pe=pe_c)
                        ac = P.e("act", lambda e, b=bank, g=mabs - NG: e.activation(
                            srsl(g), b[:], AF.Silu), inc="act")
                        evac_cnt[mabs] = ("act", ac)
                release_slot(sidx)

            cslot, cs_idx, cwl = cd_slot
            P.w("act", sl=cstore_sl[0])
            prev_u = {}
            for g in range(NG):
                bank = ps[2 + g % 2]
                P.w("pe", **cwl)
                if g == 0:
                    P.w("pe", **{evac_cnt[15][0]: evac_cnt[15][1]})
                    P.w("pe", dve=P.cnt["dve"])
                if g >= 2:
                    P.w("pe", act=prev_u[g - 2])
                for k in range(DC):
                    pe_c = P.e("pe", lambda e, b=bank, s=cslot, k=k, g=g:
                               e.matmul(b[:], s[:, (k * NG + g) * 128:
                                               (k * NG + g + 1) * 128],
                                        A_xm[:, g * 516 + k:g * 516 + k + T],
                                        start=(k == 0), stop=(k == DC - 1)),
                               inc="pe" if k == DC - 1 else None)
                P.w("act", pe=pe_c)
                prev_u[g] = P.e("act", lambda e, b=bank, g=g: e.activation(
                    usl(g), b[:], AF.Silu, bias=s_convb[:, g:g + 1]), inc="act")
            release_slot(cs_idx)
            for g in range(NG):
                P.e("dve", lambda e, g=g: e.tensor_copy(
                    xm_tail[:, g * 3:(g + 1) * 3],
                    A_xm[:, g * 516 + 512:g * 516 + 515]), inc="dve")
            conv_done = P.barrier_snapshot()
            tap("u", A_u[:], conv_done)
            tap("xm", A_xm[:], conv_done)
            tap("sres", A_sres[:], conv_done)
            tap("wslot", ring[0][:], conv_done)

            P.w("pe", act=prev_u[NG - 1])
            for g in range(NG):
                pe_c = P.e("pe", lambda e, g=g: e.matmul(
                    ps[4][0:96, :], s_xproj[:, g * 96:(g + 1) * 96], usl(g),
                    start=(g == 0), stop=(g == NG - 1)),
                    inc="pe" if g == NG - 1 else None)
            P.w("dve", pe=pe_c)
            P.e("dve", lambda e: e.tensor_copy(dt_t[0:64, :], ps[4][0:64, :]))
            P.e("dve", lambda e: e.tensor_copy(dt_t[64:128, :], ps[4][0:64, :]))
            bc_dve = P.e("dve", lambda e: e.tensor_copy(bcb[:], ps[4][64:96, :]),
                         inc="dve")

            tap("bcb", bcb[:], P.barrier_snapshot())
            del_done = {}
            for g in range(NG):
                bank = ps[2 + g % 2]
                P.w("pe", dve=bc_dve)
                if g >= 2:
                    P.w("pe", dve=del_done[g - 2])
                dtw = (s_dtproj[0:64, g * 128:(g + 1) * 128] if g < 8 else
                       s_dtproj[64:128, (g - 8) * 128:(g - 7) * 128])
                dtr = dt_t[0:64, :] if g < 8 else dt_t[64:128, :]
                pe_c = P.e("pe", lambda e, b=bank, w_=dtw, r_=dtr: e.matmul(
                    b[:], w_, r_, start=True, stop=True), inc="pe")
                P.w("act", pe=pe_c)
                if g == 0:
                    P.w("act", dve=fh["xn_done"])
                if g >= 2:
                    P.w("act", dve=del_done[g - 2])
                rt = (0 if g % 2 == 0 else 3) * T
                at = rt + T
                P.e("act", lambda e, b=bank, g=g, rt=rt: e.activation(
                    A_tmp[:, rt:rt + T], b[:], AF.Relu, bias=s_dtb[:, g:g + 1]))
                P.e("act", lambda e, b=bank, g=g, at=at: e.activation(
                    A_tmp[:, at:at + T], b[:], AF.Abs, bias=s_dtb[:, g:g + 1]))
                P.e("act", lambda e, at=at: e.activation(
                    A_tmp[:, 2 * T:3 * T], A_tmp[:, at:at + T], AF.Exp,
                    scale=s_cst[:, 0:1]))
                ac = P.e("act", lambda e, at=at: e.activation(
                    A_tmp[:, at:at + T], A_tmp[:, 2 * T:3 * T], AF.Ln, bias=1.0),
                    inc="act")
                P.w("dve", act=ac)
                del_done[g] = P.e("dve", lambda e, g=g, rt=rt, at=at:
                                  e.tensor_tensor(
                    dsl(g), A_tmp[:, rt:rt + T], A_tmp[:, at:at + T], OP.add),
                    inc="dve")
            dt_done = P.barrier_snapshot()
            tap("delta", A_delta[:], dt_done)

            bc_ev = {}
            for j in range(32):
                bank = ps[4 + j % 2]
                P.w("pe", dve=bc_dve)
                if j >= 2:
                    P.w("pe", act=bc_ev[j - 2])
                pe_c = P.e("pe", lambda e, b=bank, j=j: e.matmul(
                    b[:], s_selbc[:, j * 128:(j + 1) * 128], bcb[:],
                    start=True, stop=True), inc="pe")
                P.w("act", pe=pe_c)
                bc_ev[j] = P.e("act", lambda e, b=bank, j=j: e.activation(
                    A_big4[:, j * T:(j + 1) * T], b[:], AF.Identity), inc="act")

            scan_pre = P.barrier_snapshot()
            idmm_rel = {0: 0, 1: 0}
            yfin = {}
            QT = 4 * T
            for g in range(NG):
                P.w("dve", **{k: v for k, v in scan_pre.items() if k != "dve"})
                du_c = P.e("dve", lambda e, g=g: e.tensor_tensor(
                    du_t[:], dsl(g), usl(g), OP.mult), inc="dve")
                if g == NG - 1:
                    fh["du15"] = du_c
                if g >= 2:
                    P.w("act", pe=idmm_rel[0])
                ud_ac = P.e("act", lambda e, g=g: e.activation(
                    ud_t[:], usl(g), AF.Identity, scale=s_dcol[:, g:g + 1]),
                    inc="act")
                for q in range(4):
                    dab = A_dA[:, (q % 2) * 2064:(q % 2 + 1) * 2064]
                    dbb = A_dBu[:, (q % 2) * 2064:(q % 2 + 1) * 2064]
                    hb = A_h[:, 0:2064]
                    P.w("act", pe=idmm_rel[q % 2])
                    P.w("act", dve=scan_pre["dve"])
                    for n4 in range(4):
                        n = 4 * q + n4
                        ac = P.e("act", lambda e, d=dab, n4=n4, g=g, n=n:
                                 e.activation(d[:, n4 * 516 + 4:(n4 + 1) * 516],
                                              dsl(g), AF.Exp,
                                              scale=s_asc[:, g * NS + n:g * NS + n + 1]),
                                 inc="act" if n4 == 3 else None)
                    if g == NG - 1 and q == 3:
                        fh["exp15"] = ac
                    P.e("dve", lambda e, d=dbb, g=g, q=q: e.tensor_copy(
                        d[:, 0:2064:516],
                        carry[:, g * NS + 4 * q:g * NS + 4 * q + 4]))
                    for n4 in range(4):
                        n = 4 * q + n4
                        P.e("dve", lambda e, d=dbb, n4=n4, n=n:
                            e.tensor_tensor(d[:, n4 * 516 + 4:(n4 + 1) * 516],
                                            du_t[:],
                                            A_big4[:, n * T:(n + 1) * T], OP.mult))
                    if q == 0 and g >= 1:
                        P.w("dve", pe=idmm_rel[1])
                        yfin[g - 1] = P.e("dve", lambda e, b=ps[4 + (g - 1) % 2],
                                          g1=g - 1: e.tensor_tensor(
                            ysl(g1), b[:], srsl(g1), OP.mult), inc="dve")
                    P.w("dve", act=ac)
                    ts = P.e("dve", lambda e, da=dab, db=dbb, hb=hb:
                             e.tensor_tensor_scan(hb, da, db, 0.0, OP.mult, OP.add),
                             inc="dve")
                    P.w("dve", dve=ts)
                    P.e("dve", lambda e, hb=hb, g=g, q=q: e.tensor_copy(
                        carry[:, g * NS + 4 * q:g * NS + 4 * q + 4],
                        hb[:, 515:2064:516]))
                    for n4 in range(4):
                        n = 4 * q + n4
                        dv = P.e("dve", lambda e, hb=hb, d=dab, n4=n4, n=n:
                                 e.tensor_tensor(d[:, n4 * 516 + 4:(n4 + 1) * 516],
                                                 hb[:, n4 * 516 + 4:(n4 + 1) * 516],
                                                 A_big4[:, 8192 + n * T:8192 + (n + 1) * T],
                                                 OP.mult),
                                 inc="dve" if n4 == 3 else None)
                    ybank = ps[4 + g % 2]
                    P.w("pe", dve=dv)
                    if q == 0:
                        if g >= 2:
                            P.w("pe", dve=yfin[g - 2])
                        P.w("pe", act=ud_ac)
                        P.e("pe", lambda e, b=ybank: e.matmul(
                            b[:], s_id[:], ud_t[:], start=True, stop=False))
                    for n4 in range(4):
                        pe_c = P.e("pe", lambda e, b=ybank, d=dab, n4=n4,
                                   last=(q == 3 and n4 == 3):
                                   e.matmul(b[:], s_id[:],
                                            d[:, n4 * 516 + 4:(n4 + 1) * 516],
                                            start=False, stop=last),
                                   inc="pe" if n4 == 3 else None)
                    idmm_rel[q % 2] = pe_c
            P.w("dve", pe=idmm_rel[1])
            yfin[NG - 1] = P.e("dve", lambda e, b=ps[4 + (NG - 1) % 2]:
                               e.tensor_tensor(
                ysl(NG - 1), b[:], srsl(NG - 1), OP.mult), inc="dve")
            scan_done = P.barrier_snapshot()
            tap("y", A_xm[:, 0:8192], scan_done)

            P.wait_barrier("pe", scan_done)
            for j in range(2):
                slot, sidx, wl_t = op_slots[j]
                P.w("pe", **wl_t)
                for k8 in range(8):
                    kabs = j * 8 + k8
                    for mt in range(8):
                        pe_c = P.e("pe", lambda e, b=ps[mt], s=slot, k8=k8, mt=mt,
                                   ka=kabs, st=(kabs == 0), sp=(kabs == 15):
                                   e.matmul(b[:], s[:, k8 * 1024 + mt * 128:
                                                   k8 * 1024 + (mt + 1) * 128],
                                            ysl(ka), start=st, stop=sp),
                                   inc="pe" if (k8 == 7 and mt == 7) else None)
                release_slot(sidx)
            P.w("act", pe=pe_c)
            for mt in range(8):
                a_mt = P.e("act", lambda e, mt=mt: e.activation(
                    m_f32[:, mt * T:(mt + 1) * T], ps[mt][:], AF.Identity),
                    inc="act")
                P.w("dve", act=a_mt)
                P.e("dve", lambda e, mt=mt: e.tensor_copy(
                    xnsl(mt), ps[mt][:]), inc="dve" if mt == 7 else None)
            op_done = P.barrier_snapshot()
            fh["op_done"] = op_done
            tap("m", A_x[:].bitcast(F32), op_done)

            gel_ev = {}
            for j in range(4):
                slot, sidx, wl_t = w1_slots[j]
                for mt in range(8):
                    mabs = j * 8 + mt
                    bank = ps[mt % 2]
                    P.w("pe", **wl_t)
                    if mabs == 0:
                        P.wait_barrier("pe", op_done)
                    if mabs >= 2:
                        P.w("pe", act=gel_ev[mabs - 2])
                    for k in range(NKD):
                        pe_c = P.e("pe", lambda e, b=bank, s=slot, k=k, mt=mt:
                                   e.matmul(b[:], s[:, k * 1024 + mt * 128:
                                                   k * 1024 + (mt + 1) * 128],
                                            xnsl(k), start=(k == 0), stop=(k == 7)),
                                   inc="pe" if k == 7 else None)
                    P.w("act", pe=pe_c)
                    gel_ev[mabs] = P.e("act", lambda e, b=bank, mabs=mabs:
                                       e.activation(A_big4[:, mabs * T:(mabs + 1) * T],
                                                    b[:], AF.Gelu,
                                                    bias=s_b1[:, mabs:mabs + 1]),
                                       inc="act")
                release_slot(sidx)

            if c + 1 < NCH:
                front_head(c + 1, False)

            for j in range(4):
                slot, sidx, wl_t = w2_slots[j]
                P.w("pe", **wl_t)
                if j == 0:
                    P.w("pe", act=gel_ev[31])
                    P.w("pe", dve=fh["murow"])
                for k8 in range(8):
                    kabs = j * 8 + k8
                    for mt in range(8):
                        pe_c = P.e("pe", lambda e, b=ps[mt], s=slot, k8=k8, mt=mt,
                                   ka=kabs, st=(kabs == 0), sp=(kabs == 31):
                                   e.matmul(b[:], s[:, k8 * 1024 + mt * 128:
                                                   k8 * 1024 + (mt + 1) * 128],
                                            A_big4[:, ka * T:(ka + 1) * T],
                                            start=st, stop=sp),
                                   inc="pe" if (k8 == 7 and mt == 7) else None)
                release_slot(sidx)
            P.w("dve", pe=pe_c)
            cdone = None
            for mt in range(8):
                cdone = P.e("dve", lambda e, mt=mt, c=c: e.scalar_tensor_tensor(
                    c_f32[:, mt * T:(mt + 1) * T], ps[mt][:], s_b2[:, mt:mt + 1],
                    m_f32[:, mt * T:(mt + 1) * T], OP.add, OP.add,
                    accum_out=stats_c[:, c * 8 + mt:c * 8 + mt + 1]), inc="dve")
            P.w("act", dve=cdone)
            sq_done = None
            sqd = {}
            for mt in range(8):
                if mt >= 2:
                    P.w("act", dve=sqd[mt - 2])
                sa = P.e("act", lambda e, mt=mt: e.activation(
                    A_tmp[:, (mt % 2) * T:(mt % 2 + 1) * T],
                    c_f32[:, mt * T:(mt + 1) * T], AF.Square), inc="act")
                P.w("dve", act=sa)
                sq_done = P.e("dve", lambda e, mt=mt, c=c: e.tensor_reduce(
                    stats_c2[:, c * 8 + mt:c * 8 + mt + 1],
                    A_tmp[:, (mt % 2) * T:(mt % 2 + 1) * T],
                    axis=mybir.AxisListType.X, op=OP.add), inc="dve")
                sqd[mt] = sq_done
            P.w("sync", dve=cdone)
            cstore_sl[0] = P.e("sync", lambda e, c=c: e.dma_start(
                out=c_scr[:, c * T:(c + 1) * T].rearrange("(m p) t -> p m t", p=128),
                in_=c_f32[:, 0:4096].rearrange("p (m t) -> p m t", m=8)),
                inc="sl", n=16)
            ctap = P.barrier_snapshot()
            tap("c", c_f32[:, 0:4096], ctap)
        snapA = P.barrier_snapshot()
        P.wait_barrier("dve", snapA)

        def dstep(fn):
            v = P.e("dve", fn, inc="dve")
            P.w("dve", dve=v)
            return v

        dstep(lambda e: e.tensor_reduce(sc_red[:, 0:1], stats_c[:],
                                        axis=mybir.AxisListType.X, op=OP.add))
        dstep(lambda e: e.tensor_reduce(sc_red[:, 1:2], stats_c2[:],
                                        axis=mybir.AxisListType.X, op=OP.add))
        rdve = dstep(lambda e: e.tensor_copy(sc_red_r[:], sc_red[:]))
        P.w("pe", dve=rdve)
        P.e("pe", lambda e: e.matmul(ps[1][0:1, 0:2], s_oneskr[:], sc_red_r[:],
                                     start=True, stop=True), inc="pe")
        pe_fin = P.cnt["pe"]
        P.w("dve", pe=pe_fin)
        NEL = float(L * DM)
        dstep(lambda e: e.tensor_scalar(t_s1[0:1, 0:2], ps[1][0:1, 0:2],
                                        1.0 / NEL, None, op0=OP.mult))
        dstep(lambda e: e.tensor_tensor(t_s1[0:1, 2:3], t_s1[0:1, 0:1],
                                        t_s1[0:1, 0:1], OP.mult))
        vdve = dstep(lambda e: e.tensor_tensor(t_s1[0:1, 3:4], t_s1[0:1, 1:2],
                                               t_s1[0:1, 2:3], OP.subtract))
        P.w("act", dve=vdve)
        aa = P.e("act", lambda e: e.activation(t_s1[0:1, 4:5], t_s1[0:1, 3:4],
                                               AF.Ln, bias=s_cst[0:1, 2:3]),
                 inc="act")
        P.w("act", act=aa)
        adone = P.e("act", lambda e: e.activation(t_s1[0:1, 5:6], t_s1[0:1, 4:5],
                                                  AF.Exp, scale=s_cst[0:1, 1:2]),
                    inc="act")
        P.w("dve", act=adone)
        dstep(lambda e: e.tensor_tensor(t_s1[0:1, 6:7], t_s1[0:1, 0:1],
                                        t_s1[0:1, 5:6], OP.mult))
        dstep(lambda e: e.tensor_scalar(t_s1[0:1, 6:7], t_s1[0:1, 6:7], -1.0,
                                        None, op0=OP.mult))
        rdve2 = dstep(lambda e: e.tensor_copy(t_s1r[0:1, 0:2], t_s1[0:1, 5:7]))
        P.w("pe", dve=rdve2)
        P.e("pe", lambda e: e.matmul(ps[2][:, 0:2], s_ones1r[:], t_s1r[0:1, 0:2],
                                     start=True, stop=True), inc="pe")
        pe_b = P.cnt["pe"]
        P.w("dve", pe=pe_b)
        dstep(lambda e: e.tensor_copy(rstd_col[:], ps[2][:, 0:1]))
        dstep(lambda e: e.tensor_copy(nmr_col[:], ps[2][:, 1:2]))
        statsB = P.barrier_snapshot()
        tap("slab", t_s1[0:1, 0:8], statsB)

        ostore_sl = [0]
        ping_f32 = [x2_r, A_xm[:].bitcast(F32)[:, 0:4096]]
        pb = {"pe_stats": P.cnt["pe"], "pe_wfc": P.cnt["pe"],
              "st_done": P.cnt["dve"], "hq31": P.cnt["act"], "x2f": {}}

        def pb_load(c):
            xl = P.e("gp", lambda e, c=c: e.dma_start(
                out=A_x[:].rearrange("p (k t) -> p k t", k=NKD),
                in_=x_t[:, c * T:(c + 1) * T].rearrange("(k p) t -> p k t", p=128)),
                inc="xv", n=16)
            cl = P.e("sync", lambda e, c=c: e.dma_start(
                out=c_f32[:, 0:4096].rearrange("p (m t) -> p m t", m=8),
                in_=c_scr[:, c * T:(c + 1) * T].rearrange("(m p) t -> p m t", p=128)),
                inc="cl", n=16)
            pb[("xl", c)], pb[("cl", c)] = xl, cl

        def pb_front1(c):
            if ("xl", c) not in pb:
                snapB = P.barrier_snapshot()
                P.wait_barrier("gp", snapB)
                P.wait_barrier("sync", snapB)
                pb_load(c)
            xl, cl = pb.pop(("xl", c)), pb.pop(("cl", c))
            fc_slots = []
            for j in range(4):
                srcw = wfct[:, j * 1024:(j + 1) * 1024].rearrange(
                    "(k p) m -> p k m", p=128)
                fc_slots.append(fill_slot(srcw))

            x2f = ping_f32[c % 2]
            pb["x2f"][c] = x2f
            P.w("dve", xv=xl, cl=cl, sl=ostore_sl[0], act=pb["hq31"])
            x2d = None
            for mt in range(8):
                P.e("dve", lambda e, x2f=x2f, mt=mt: e.tensor_scalar(
                    A_tmp[:, 0:T], c_f32[:, mt * T:(mt + 1) * T],
                    rstd_col[:, 0:1], nmr_col[:, 0:1], op0=OP.mult, op1=OP.add))
                x2d = P.e("dve", lambda e, x2f=x2f, mt=mt: e.tensor_tensor(
                    x2f[:, mt * T:(mt + 1) * T], A_tmp[:, 0:T],
                    A_x[:, mt * T:(mt + 1) * T], OP.add),
                    inc="dve" if mt == 7 else None)
            if c + 1 < NCH:
                P.w("gp", dve=x2d)
                P.w("sync", dve=x2d)
                pb_load(c + 1)

            def x2sl(k, x2f=x2f):
                return x2f[:, k * T:(k + 1) * T]
            P.w("dve", pe=pb["pe_stats"])
            for mt in range(8):
                P.e("dve", lambda e, x2f=x2f, mt=mt: e.tensor_copy(
                    A_dA[:, mt * T:(mt + 1) * T],
                    x2f[:, mt * T:(mt + 1) * T]), inc="dve" if mt == 7 else None)

            def x2b(k):
                return A_dA[:, k * T:(k + 1) * T]
            ln_stats(P, ps, x2b, A_h[:, 0:2 * T], ra, rb, rrb,
                     A_tmp[0:1, 0:T], s_oneskb, s_ones1b, s_cst, None,
                     P.barrier_snapshot())
            pb["pe_stats"] = P.cnt["pe"]
            P.w("dve", pe=pb["pe_wfc"])
            for k in range(NKD):
                P.e("dve", lambda e, k=k, x2sl=x2sl: e.tensor_tensor(
                    A_tmp[:, 0:T], x2sl(k), ps[6][:], OP.subtract))
                P.e("dve", lambda e, k=k: e.tensor_tensor(
                    xnsl(k), A_tmp[:, 0:T], ps[7][:], OP.mult), inc="dve")
            ln2_done = P.barrier_snapshot()
            tap("x2", x2f[:, 0:4096], ln2_done)

            hq_ev = {}
            for j in range(4):
                slot, sidx, wl_t = fc_slots[j]
                for mt in range(8):
                    mabs = j * 8 + mt
                    bank = ps[mt % 2]
                    P.w("pe", **wl_t)
                    if mabs == 0:
                        P.wait_barrier("pe", ln2_done)
                    if mabs >= 2:
                        P.w("pe", act=hq_ev[mabs - 2])
                    for k in range(NKD):
                        pe_c = P.e("pe", lambda e, b=bank, s=slot, k=k, mt=mt:
                                   e.matmul(b[:], s[:, k * 1024 + mt * 128:
                                                   k * 1024 + (mt + 1) * 128],
                                            xnsl(k), start=(k == 0), stop=(k == 7)),
                                   inc="pe" if k == 7 else None)
                    P.w("act", pe=pe_c)
                    P.e("act", lambda e, b=bank, mabs=mabs: e.activation(
                        A_tmp[:, (mabs % 2) * T:(mabs % 2 + 1) * T], b[:], AF.Relu))
                    hq_ev[mabs] = P.e("act", lambda e, mabs=mabs: e.activation(
                        A_big4[:, mabs * T:(mabs + 1) * T],
                        A_tmp[:, (mabs % 2) * T:(mabs % 2 + 1) * T], AF.Square),
                        inc="act")
                release_slot(sidx)
            pb["pe_wfc"] = P.cnt["pe"]
            pb["hq31"] = hq_ev[31]
            return hq_ev[31]

        def pb_front2(c, hq31):
            pj_slots = []
            for j in range(4):
                srcw = wprojt[j * 1024:(j + 1) * 1024, :].rearrange(
                    "(k p) m -> p k m", p=128)
                pj_slots.append(fill_slot(srcw))
            pe_c = None
            for j in range(4):
                slot, sidx, wl_t = pj_slots[j]
                P.w("pe", **wl_t)
                if j == 0:
                    P.w("pe", act=hq31)
                for k8 in range(8):
                    kabs = j * 8 + k8
                    for mt in range(8):
                        pe_c = P.e("pe", lambda e, b=ps[mt], s=slot, k8=k8, mt=mt,
                                   st=(kabs == 0), sp=(kabs == 31), ka=kabs:
                                   e.matmul(b[:], s[:, k8 * 1024 + mt * 128:
                                                   k8 * 1024 + (mt + 1) * 128],
                                            A_big4[:, ka * T:(ka + 1) * T],
                                            start=st, stop=sp),
                                   inc="pe" if (k8 == 7 and mt == 7) else None)
                release_slot(sidx)
            P.w("act", pe=pe_c, dve=pb["st_done"])
            ysv = None
            for mt in range(8):
                ysv = P.e("act", lambda e, mt=mt: e.activation(
                    A_delta[:, mt * T:(mt + 1) * T], ps[mt][:], AF.Identity),
                    inc="act" if mt == 7 else None)
            ys_done = P.barrier_snapshot()
            tap("ys", A_delta[:, 0:4096], ys_done)
            pb[("ysv", c)] = ysv

        def pb_spike(c):
            ysv = pb.pop(("ysv", c))
            x2f = pb["x2f"].pop(c)
            P.w("dve", act=ysv)
            st_done = None
            for mt in range(8):
                y_ = A_delta[:, mt * T:(mt + 1) * T]
                spk = A_dBu[:, mt * T:(mt + 1) * T]
                mem = A_delta[:, 4096 + mt * T:4096 + (mt + 1) * T]
                o_ = x2f[:, mt * T:(mt + 1) * T]
                P.e("dve", lambda e, y_=y_, spk=spk: e.tensor_scalar(
                    spk, y_, 1.0, None, op0=OP.is_gt))
                P.e("dve", lambda e, o_=o_, spk=spk: e.scalar_tensor_tensor(
                    o_, spk, 0.2, o_, OP.mult, OP.add))
                P.e("dve", lambda e, y_=y_, mem=mem: e.tensor_scalar(
                    mem, y_, 1.9, None, op0=OP.mult))
                P.e("dve", lambda e, mem=mem, spk=spk: e.tensor_tensor(
                    mem, mem, spk, OP.subtract))
                for it in range(3):
                    P.e("dve", lambda e, mem=mem, spk=spk: e.tensor_scalar(
                        spk, mem, 1.0, None, op0=OP.is_gt))
                    P.e("dve", lambda e, o_=o_, spk=spk: e.scalar_tensor_tensor(
                        o_, spk, 0.2, o_, OP.mult, OP.add))
                    P.e("dve", lambda e, mem=mem: e.tensor_scalar(
                        mem, mem, 0.9, None, op0=OP.mult))
                    P.e("dve", lambda e, mem=mem, y_=y_: e.tensor_tensor(
                        mem, mem, y_, OP.add))
                    P.e("dve", lambda e, mem=mem, spk=spk: e.tensor_tensor(
                        mem, mem, spk, OP.subtract))
                P.e("dve", lambda e, mem=mem, spk=spk: e.tensor_scalar(
                    spk, mem, 1.0, None, op0=OP.is_gt))
                st_done = P.e("dve", lambda e, o_=o_, spk=spk:
                              e.scalar_tensor_tensor(
                    o_, spk, 0.2, o_, OP.mult, OP.add), inc="dve")
            pb["st_done"] = st_done
            P.w("sync", dve=st_done)
            ostore_sl[0] = P.e("sync", lambda e, c=c, x2f=x2f: e.dma_start(
                out=out_t[:, c * T:(c + 1) * T].rearrange("(m p) t -> p m t", p=128),
                in_=x2f[:, 0:4096].rearrange("p (m t) -> p m t", m=8)),
                inc="sl", n=16)

        for c in range(NCH):
            hq31 = pb_front1(c)
            if c > 0:
                pb_spike(c - 1)
            pb_front2(c, hq31)
        pb_spike(NCH - 1)

        P.w("sync", sl=P.cnt["sl"], xv=P.cnt["xv"], cl=P.cnt["cl"],
            ws0=P.cnt["ws0"], ws1=P.cnt["ws1"])

        @block.sync
        def _(e):
            for fn in P.q["sync"]:
                fn(e)

        @block.vector
        def _(e):
            for fn in P.q["dve"]:
                fn(e)

        @block.scalar
        def _(e):
            for fn in P.q["act"]:
                fn(e)

        @block.tensor
        def _(e):
            for fn in P.q["pe"]:
                fn(e)

        @block.gpsimd
        def _(e):
            for fn in P.q["gp"]:
                fn(e)

    return nc


def ln_stats(P, ps, xslfn, sqbuf, ra, rb, rrb, rc,
             s_oneskb, s_ones1b, s_cst, xl, snap):
    if xl is not None:
        P.w("pe", xv=xl)
        P.w("act", xv=xl)
    P.wait_barrier("pe", snap)
    P.wait_barrier("act", snap)
    sq_pe = {}
    for k in range(NKD):
        P.e("pe", lambda e, k=k: e.matmul(ps[6][0:1, :], s_oneskb[:], xslfn(k),
                                          start=(k == 0), stop=(k == NKD - 1)),
            inc="pe" if k == NKD - 1 else None)
    for k in range(NKD):
        buf = sqbuf[:, (k % 2) * T:(k % 2 + 1) * T]
        if k >= 2:
            P.w("act", pe=sq_pe[k - 2])
        a_k = P.e("act", lambda e, b=buf, k=k: e.activation(
            b, xslfn(k), AF.Square), inc="act")
        P.w("pe", act=a_k)
        sq_pe[k] = P.e("pe", lambda e, b=buf, k=k: e.matmul(
            ps[7][0:1, :], s_oneskb[:], b,
            start=(k == 0), stop=(k == NKD - 1)), inc="pe")
    pe_c = P.cnt["pe"]
    P.w("dve", pe=pe_c)
    P.e("dve", lambda e: e.tensor_copy(ra[:], ps[6][0:1, :]))
    P.e("dve", lambda e: e.tensor_copy(rb[:], ps[7][0:1, :]))
    P.e("dve", lambda e: e.tensor_scalar(ra[:], ra[:], 1.0 / DM, None,
                                         op0=OP.mult))
    P.e("dve", lambda e: e.tensor_scalar(rb[:], rb[:], 1.0 / DM, None,
                                         op0=OP.mult))
    P.e("dve", lambda e: e.tensor_tensor(rc, ra[:], ra[:], OP.mult))
    vd = P.e("dve", lambda e: e.tensor_tensor(rb[:], rb[:], rc, OP.subtract),
             inc="dve")
    P.w("act", dve=vd)
    P.e("act", lambda e: e.activation(rc, rb[:], AF.Ln, bias=s_cst[0:1, 2:3]))
    ad = P.e("act", lambda e: e.activation(rb[:], rc, AF.Exp,
                                           scale=s_cst[0:1, 1:2]), inc="act")
    P.w("dve", act=ad)
    P.e("dve", lambda e: e.tensor_copy(rrb[0:1, 0:T], ra[:]))
    cd = P.e("dve", lambda e: e.tensor_copy(rrb[0:1, T:2 * T], rb[:]), inc="dve")
    P.w("pe", dve=cd)
    P.e("pe", lambda e: e.matmul(ps[6][:], s_ones1b[:], rrb[0:1, 0:T],
                                 start=True, stop=True))
    bc = P.e("pe", lambda e: e.matmul(ps[7][:], s_ones1b[:], rrb[0:1, T:2 * T],
                                      start=True, stop=True), inc="pe")
    P.w("dve", pe=bc)



L_FULL = 8192


def kernel(**inputs):
    maps = host_prep(inputs, L_FULL)
    nc = build(L_FULL)
    from concourse.bass_utils import run_bass_kernel_spmd
    res = run_bass_kernel_spmd(nc, maps, core_ids=list(range(8)))
    out = np.stack([np.ascontiguousarray(res.results[b]["out_t"].T)
                    for b in range(8)], axis=0)
    return out


# version 08
# speedup vs baseline: 1.0056x; 1.0056x over previous
import contextlib
from collections import defaultdict
import numpy as np
import ml_dtypes
import concourse.bass as bass
import concourse.mybir as mybir

F32 = mybir.dt.float32
BF16 = mybir.dt.bfloat16
F32R = mybir.dt.float32r
AF = mybir.ActivationFunctionType
OP = mybir.AluOpType

DM, DI, NS, DC, DTR = 1024, 2048, 16, 4, 64
EPS = 1e-5
NKD = DM // 128
NG = DI // 128
T = 512
NSLOT = 2

ENGS = ("sync", "dve", "act", "pe", "gp")


class Prog:

    def __init__(self, nc, sems):
        self.nc = nc
        self.sems = sems
        self.q = {e: [] for e in ENGS}
        self.cnt = defaultdict(int)
        self.trace = []

    def w(self, eng, **waits):
        for s, v in waits.items():
            if v <= 0:
                continue
            sem = self.sems[s]
            self.trace.append(("wait", eng, s, v))
            self.q[eng].append(lambda e, sem=sem, v=v: e.wait_ge(sem, v))

    def w_lazy(self, eng, s, cell):
        sem = self.sems[s]
        self.trace.append(("waitlazy", eng, s, cell))
        def fn(e, sem=sem, cell=cell):
            if cell[0] > 0:
                e.wait_ge(sem, cell[0])
        self.q[eng].append(fn)

    def e(self, eng, fn, inc=None, n=1):
        if inc is None:
            self.trace.append(("op", eng, None, 0))
            self.q[eng].append(lambda e, fn=fn: fn(e))
            return None
        sem = self.sems[inc]
        self.cnt[inc] += n
        self.trace.append(("op", eng, inc, n))
        self.q[eng].append(lambda e, fn=fn, sem=sem, n=n: fn(e).then_inc(sem, n))
        return self.cnt[inc]

    def barrier_snapshot(self):
        return {k: self.cnt[k] for k in ("pe", "dve", "act")}

    def wait_barrier(self, eng, snap):
        self.w(eng, **{k: v for k, v in snap.items() if k != _ENG2SEM.get(eng)})


_ENG2SEM = {"pe": "pe", "dve": "dve", "act": "act"}


def host_prep(inp, L):
    bf = ml_dtypes.bfloat16
    f32 = np.float32
    c = np.ascontiguousarray
    w = {}
    w["win_t"] = c(inp["L0_attn_in_proj_w"].T)
    cw = inp["L0_attn_conv_w"]
    CW = np.zeros((128, DC * NG * 128), f32)
    ar = np.arange(128)
    for k in range(DC):
        for g in range(NG):
            CW[ar, (k * NG + g) * 128 + ar] = cw[g * 128:(g + 1) * 128, 0, k]
    w["cdiag"] = CW
    w["convb"] = c(inp["L0_attn_conv_b"].reshape(NG, 128).T)
    w["xprojt"] = c(inp["L0_attn_x_proj_w"].T.reshape(NG, 128, 96)
                    .transpose(1, 0, 2).reshape(128, NG * 96))
    dtp = inp["L0_attn_dt_proj_w"].T
    w["dtprojt"] = np.concatenate([dtp[:, :1024], dtp[:, 1024:]], 0)
    w["dtb"] = c(inp["L0_attn_dt_proj_b"].reshape(NG, 128).T)
    w["alogt"] = c(inp["L0_attn_A_log"].reshape(NG, 128, NS)
                   .transpose(1, 0, 2).reshape(128, NG * NS))
    w["dcol"] = c(inp["L0_attn_D"].reshape(NG, 128).T)
    w["outprojt"] = c(inp["L0_attn_out_proj_w"].T)
    w["w1t"] = c(inp["L0_attn_ffn_w1"].T)
    w["b1c"] = c(inp["L0_attn_ffn_b1"].reshape(32, 128).T)
    w["w2t"] = c(inp["L0_attn_ffn_w2"].T)
    w["b2c"] = c(inp["L0_attn_ffn_b2"].reshape(8, 128).T)
    w["wfct"] = c(inp["L0_ffn_W_fc"].T)
    w["wprojt"] = c(inp["L0_ffn_W_proj"].T)
    sel = np.zeros((32, 32 * 128), bf)
    for j in range(32):
        sel[j, j * 128:(j + 1) * 128] = 1
    w["selbc"] = sel
    w["id128"] = np.eye(128, dtype=bf)
    w["ones1"] = np.ones((1, 128), f32)
    w["onesk"] = np.ones((128, 1), f32)
    cst = np.zeros((128, 4), f32)
    cst[:, 0] = -1.0
    cst[:, 1] = -0.5
    cst[:, 2] = EPS
    w["cst"] = cst
    maps = []
    x = inp["L0_attn_seq_embeddings"]
    for b in range(8):
        m = dict(w)
        m["x_t"] = c(x[b, :L, :].T)
        maps.append(m)
    return maps


def build(L, taps=()):
    NCH = L // T
    nc = bass.Bass("TRN2", target_bir_lowering=False)
    D = {}

    def din(name, shape, dt=F32):
        D[name] = nc.dram_tensor(name, list(shape), dt, kind="ExternalInput").ap()
        return D[name]

    x_t = din("x_t", (DM, L))
    win_t = din("win_t", (DM, 4096))
    cdiag = din("cdiag", (128, 8192))
    convb = din("convb", (128, NG))
    xprojt = din("xprojt", (128, NG * 96))
    dtprojt = din("dtprojt", (128, 1024))
    dtb = din("dtb", (128, NG))
    alogt = din("alogt", (128, NG * NS))
    dcol = din("dcol", (128, NG))
    outprojt = din("outprojt", (DI, DM))
    w1t = din("w1t", (DM, 4096))
    b1c = din("b1c", (128, 32))
    w2t = din("w2t", (4096, DM))
    b2c = din("b2c", (128, 8))
    wfct = din("wfct", (DM, 4096))
    wprojt = din("wprojt", (4096, DM))
    selbc_d = din("selbc", (32, 4096), BF16)
    id128_d = din("id128", (128, 128), BF16)
    ones1_d = din("ones1", (1, 128))
    onesk_d = din("onesk", (128, 1))
    cst_d = din("cst", (128, 4))

    out_t = nc.dram_tensor("out_t", [DM, L], F32, kind="ExternalOutput").ap()
    c_scr = nc.dram_tensor("c_scr", [DM, L], F32).ap()
    x_bf = nc.dram_tensor("x_bf", [DM, L], BF16).ap()
    tap_d = {}
    for tspec in taps:
        tname, tshape = tspec[0], tspec[1]
        tdt = tspec[2] if len(tspec) > 2 else F32
        tap_d[tname] = nc.dram_tensor("tap_" + tname, list(tshape), tdt,
                                      kind="ExternalOutput").ap()

    es = contextlib.ExitStack()
    with es:
        def sb(name, shape, dt):
            return es.enter_context(nc.sbuf_tensor(name, list(shape), dt))

        ring = [sb(f"ring{i}", (128, 8192), BF16) for i in range(NSLOT)]
        A_x = sb("A_x", (128, 4096), F32R)
        A_xn = sb("A_xn", (128, 4096), BF16)
        A_xm = sb("A_xm", (128, NG * 516), BF16)
        A_sres = sb("A_sres", (128, 8192), BF16)
        A_big4 = sb("A_big4", (128, 16384), BF16)
        A_u = sb("A_u", (128, 8192), BF16)
        A_delta = sb("A_delta", (128, 8192), BF16)
        A_dA = sb("A_dA", (128, 4128), BF16)
        A_dBu = sb("A_dBu", (128, 4128), BF16)
        A_h = sb("A_h", (128, 2064), BF16)
        A_tmp = sb("A_tmp", (128, 2560), F32)
        dt_t = sb("dt_t", (128, T), BF16)
        bcb = sb("bcb", (32, T), BF16)
        carry = sb("carry", (128, NG * NS), BF16)
        xm_tail = sb("xm_tail", (128, NG * 3), BF16)
        fx1 = sb("fx1", (128, 4), F32)
        fx2 = sb("fx2", (128, 4), F32)
        stats_c = sb("stats_c", (128, NCH * 8), F32)
        stats_c2 = sb("stats_c2", (128, NCH * 8), F32)
        ra = A_tmp[0:1, T:2 * T]
        rb = A_tmp[0:1, 2 * T:3 * T]
        sc_red = sb("sc_red", (128, 2), F32)
        sc_red_r = sb("sc_red_r", (128, 2), F32R)
        t_s1 = sb("t_s1", (1, 8), F32)
        t_s1r = sb("t_s1r", (1, 2), F32R)
        rstd_col = sb("rstd_col", (128, 1), F32)
        nmr_col = sb("nmr_col", (128, 1), F32)
        s_xproj = sb("s_xproj", (128, NG * 96), BF16)
        s_dtproj = sb("s_dtproj", (128, 1024), BF16)
        s_dcol = sb("s_dcol", (128, NG), F32)
        ud_t = sb("ud_t", (128, T), BF16)
        du_t = sb("du_t", (128, T), BF16)
        s_selbc = sb("s_selbc", (32, 4096), BF16)
        s_id = sb("s_id", (128, 128), BF16)
        s_ones1r = sb("s_ones1r", (1, 128), F32R)
        s_oneskr = sb("s_oneskr", (128, 1), F32R)
        s_asc = sb("s_asc", (128, NG * NS), F32)
        s_convb = sb("s_convb", (128, NG), F32)
        s_dtb = sb("s_dtb", (128, NG), F32)
        s_b1 = sb("s_b1", (128, 32), F32)
        s_b2 = sb("s_b2", (128, 8), F32)
        s_cst = sb("s_cst", (128, 4), F32)
        s_oneskb = sb("s_oneskb", (128, 1), BF16)
        s_ones1b = sb("s_ones1b", (1, 128), BF16)
        rrb = A_tmp[:].bitcast(BF16)[0:1, 0:2 * T]

        ps = [es.enter_context(nc.psum_tensor(f"ps{i}", [128, T], F32))
              for i in range(8)]

        sems = {n: es.enter_context(nc.semaphore("sem_" + n))
                for n in ("ws0", "ws1", "xv", "cl", "sl", "pe", "dve", "act")}
        block = es.enter_context(nc.Block())
        P = Prog(nc, sems)

        du_bf = None
        m_f32 = A_x[:].bitcast(F32)
        c_f32 = A_u[:].bitcast(F32)
        x2_r = A_sres[:].bitcast(F32)
        ys_f32 = A_delta[:].bitcast(F32)
        mem_f32 = A_xm[:, 0:8192].bitcast(F32)

        def xsl(k):
            return A_delta[:, k * T:(k + 1) * T]

        def xnsl(k):
            return A_xn[:, k * T:(k + 1) * T]

        def usl(g):
            return A_u[:, g * T:(g + 1) * T]

        def dsl(g):
            return A_delta[:, g * T:(g + 1) * T]

        def ysl(g):
            return A_xm[:, g * T:(g + 1) * T]

        def srsl(g):
            return A_sres[:, g * T:(g + 1) * T]

        ring_state = {"fill": 0, "relcell": [[0] for _ in range(NSLOT)]}

        def fill_slot(src_ap):
            s = ring_state["fill"] % NSLOT
            ring_state["fill"] += 1
            P.w_lazy("gp", "pe", ring_state["relcell"][s])
            newcell = [0]
            ring_state["relcell"][s] = newcell
            sname = f"ws{s}"
            wt = P.e("gp", lambda e, dst=ring[s], src=src_ap:
                     e.dma_start(out=dst[:].rearrange("p (k m) -> p k m",
                                                      k=src.shape[1]),
                                 in_=src), inc=sname, n=16)
            return ring[s], newcell, {sname: wt}

        def release_slot(cell):
            cell[0] = P.cnt["pe"]

        P.e("gp", lambda e: e.dma_start(out=s_xproj[:], in_=xprojt[:]), inc="xv", n=16)
        P.e("gp", lambda e: e.dma_start(out=s_dtproj[:], in_=dtprojt[:]), inc="xv", n=16)
        P.e("gp", lambda e: e.dma_start(out=s_dcol[:], in_=dcol[:]), inc="xv", n=16)
        P.e("gp", lambda e: e.dma_start(out=s_ones1r[:], in_=ones1_d[:]), inc="xv", n=16)
        P.e("gp", lambda e: e.dma_start(out=s_oneskr[:], in_=onesk_d[:]), inc="xv", n=16)
        P.e("gp", lambda e: e.dma_start(out=s_ones1b[:], in_=ones1_d[:]), inc="xv", n=16)
        P.e("gp", lambda e: e.dma_start(out=s_oneskb[:], in_=onesk_d[:]), inc="xv", n=16)
        P.e("sync", lambda e: e.dma_start(out=s_selbc[:], in_=selbc_d[:]), inc="sl", n=16)
        P.e("sync", lambda e: e.dma_start(out=s_id[:], in_=id128_d[:]), inc="sl", n=16)
        P.e("gp", lambda e: e.dma_start(out=A_x[:, 0:NG * NS], in_=alogt[:]), inc="xv", n=16)
        P.e("sync", lambda e: e.dma_start(out=s_convb[:], in_=convb[:]), inc="sl", n=16)
        P.e("sync", lambda e: e.dma_start(out=s_dtb[:], in_=dtb[:]), inc="sl", n=16)
        P.e("sync", lambda e: e.dma_start(out=s_b1[:], in_=b1c[:]), inc="sl", n=16)
        P.e("sync", lambda e: e.dma_start(out=s_b2[:], in_=b2c[:]), inc="sl", n=16)
        P.e("sync", lambda e: e.dma_start(out=s_cst[:], in_=cst_d[:]), inc="sl", n=16)
        xbf_done = P.e("gp", lambda e: e.dma_start(
            out=x_bf.rearrange("(k p) t -> p k t", p=128),
            in_=x_t[:].rearrange("(k p) t -> p k t", p=128)), inc="xv", n=16)
        pro_wl, pro_sl = P.cnt["xv"], P.cnt["sl"]
        P.w("gp", xv=pro_wl)

        P.w("act", sl=pro_sl, xv=pro_wl)
        P.e("act", lambda e: e.activation(s_asc[:], A_x[:, 0:NG * NS], AF.Exp),
            inc="act")
        P.w("dve", act=P.cnt["act"])
        P.e("dve", lambda e: e.tensor_scalar(s_asc[:], s_asc[:], -1.0, None,
                                             op0=OP.mult), inc="dve")
        P.e("dve", lambda e: e.memset(A_xm[:], 0.0), inc="dve")
        P.e("dve", lambda e: e.memset(carry[:], 0.0), inc="dve")
        mfc = P.e("dve", lambda e: e.memset(A_dA[:], 1.0), inc="dve")
        P.w("dve", dve=mfc)
        P.e("dve", lambda e: e.memset(A_dA[:, 0:4128:516], 0.0))
        P.e("dve", lambda e: e.memset(A_dBu[:], 0.0), inc="dve")

        def tap(name, tile_ap, wait_snap):
            if name in tap_d:
                P.wait_barrier("sync", wait_snap)
                tv = P.e("sync", lambda e, t=tile_ap, o=tap_d[name]:
                         e.dma_start(out=o[:], in_=t), inc="sl", n=16)
                for eng in ("dve", "act", "pe", "gp"):
                    P.w(eng, sl=tv)

        cstore_sl = [0]
        fh = {}

        def front_head(c, first):
            if first:
                snapH = P.barrier_snapshot()
                P.wait_barrier("sync", snapH)
                P.w("sync", xv=xbf_done)
                snapL = snapH
            else:
                P.w("sync", dve=fh["du15"], act=fh["exp15"])
                snapL = fh["op_done"]
            xl = P.e("sync", lambda e, c=c: e.dma_start(
                out=A_delta[:, 0:4096].rearrange("p (k t) -> p k t", k=NKD),
                in_=x_bf[:, c * T:(c + 1) * T].rearrange("(k p) t -> p k t", p=128)),
                inc="xv", n=16)
            ln_stats(P, ps, xsl, A_h[:, 0:2 * T], ra, rb, rrb,
                     A_tmp[0:1, 0:T], s_oneskb, s_ones1b, s_cst, xl, snapL)
            mu_sb = A_tmp[:, 0:T]
            rstd_sb = A_tmp[:, T:2 * T]
            P.e("dve", lambda e: e.tensor_copy(mu_sb, ps[6][:]))
            fh["murow"] = P.e("dve", lambda e: e.tensor_copy(rstd_sb, ps[7][:]),
                              inc="dve")
            if not first:
                P.w("dve", pe=P.cnt["pe"])
            xd = None
            for k in range(NKD):
                P.e("dve", lambda e, k=k, m_=mu_sb: e.tensor_tensor(
                    xnsl(k), xsl(k), m_, OP.subtract))
                xd = P.e("dve", lambda e, k=k, r_=rstd_sb: e.tensor_tensor(
                    xnsl(k), xnsl(k), r_, OP.mult),
                    inc="dve" if k == NKD - 1 else None)
            fh["xn_done"] = xd
            tap("xn", A_xn[:], P.barrier_snapshot())

        for c in range(NCH):
            if c == NCH - 1:
                tap("carry", carry[:], P.barrier_snapshot())
            if tap_d:
                for eng in ("dve", "act", "pe", "gp", "sync"):
                    P.w(eng, sl=P.cnt["sl"], wl=P.cnt["wl"])
            if c == 0:
                front_head(0, True)
            ip_slots = []
            for j in range(4):
                srcw = win_t[:, j * 1024:(j + 1) * 1024].rearrange(
                    "(k p) m -> p k m", p=128)
                ip_slots.append(fill_slot(srcw))
            cd_slot = fill_slot(cdiag[:].rearrange("p (k m) -> p k m", k=1))
            op_slots = []
            for j in range(2):
                srcw = outprojt[j * 1024:(j + 1) * 1024, :].rearrange(
                    "(k p) m -> p k m", p=128)
                op_slots.append(fill_slot(srcw))
            w1_slots = []
            for j in range(4):
                srcw = w1t[:, j * 1024:(j + 1) * 1024].rearrange(
                    "(k p) m -> p k m", p=128)
                w1_slots.append(fill_slot(srcw))
            w2_slots = []
            for j in range(4):
                srcw = w2t[j * 1024:(j + 1) * 1024, :].rearrange(
                    "(k p) m -> p k m", p=128)
                w2_slots.append(fill_slot(srcw))
            if c > 0:
                for g in range(NG):
                    P.e("dve", lambda e, g=g: e.tensor_copy(
                        A_xm[:, g * 516:g * 516 + 3],
                        xm_tail[:, g * 3:(g + 1) * 3]), inc="dve")

            evac_cnt = {}
            for j in range(4):
                slot, sidx, wl_t = ip_slots[j]
                for mt in range(8):
                    mabs = j * 8 + mt
                    bank = ps[mt % 2]
                    P.w("pe", dve=P.cnt["dve"] if mabs == 0 else 0, **wl_t)
                    if mabs >= 2:
                        P.w("pe", **{evac_cnt[mabs - 2][0]: evac_cnt[mabs - 2][1]})
                    for k in range(NKD):
                        pe_c = P.e("pe", lambda e, b=bank, s=slot, k=k, mt=mt:
                                   e.matmul(b[:], s[:, k * 1024 + mt * 128:
                                                   k * 1024 + (mt + 1) * 128],
                                            xnsl(k), start=(k == 0), stop=(k == 7)),
                                   inc="pe" if k == 7 else None)
                    if mabs < NG:
                        P.w("dve", pe=pe_c)
                        dc = P.e("dve", lambda e, b=bank, g=mabs: e.tensor_copy(
                            A_xm[:, g * 516 + 3:g * 516 + 515], b[:]), inc="dve")
                        evac_cnt[mabs] = ("dve", dc)
                    else:
                        P.w("act", pe=pe_c)
                        ac = P.e("act", lambda e, b=bank, g=mabs - NG: e.activation(
                            srsl(g), b[:], AF.Silu), inc="act")
                        evac_cnt[mabs] = ("act", ac)
                release_slot(sidx)

            cslot, cs_idx, cwl = cd_slot
            P.w("act", sl=cstore_sl[0])
            prev_u = {}
            for g in range(NG):
                bank = ps[2 + g % 2]
                P.w("pe", **cwl)
                if g == 0:
                    P.w("pe", **{evac_cnt[15][0]: evac_cnt[15][1]})
                    P.w("pe", dve=P.cnt["dve"])
                if g >= 2:
                    P.w("pe", act=prev_u[g - 2])
                for k in range(DC):
                    pe_c = P.e("pe", lambda e, b=bank, s=cslot, k=k, g=g:
                               e.matmul(b[:], s[:, (k * NG + g) * 128:
                                               (k * NG + g + 1) * 128],
                                        A_xm[:, g * 516 + k:g * 516 + k + T],
                                        start=(k == 0), stop=(k == DC - 1)),
                               inc="pe" if k == DC - 1 else None)
                P.w("act", pe=pe_c)
                prev_u[g] = P.e("act", lambda e, b=bank, g=g: e.activation(
                    usl(g), b[:], AF.Silu, bias=s_convb[:, g:g + 1]), inc="act")
            release_slot(cs_idx)
            for g in range(NG):
                P.e("dve", lambda e, g=g: e.tensor_copy(
                    xm_tail[:, g * 3:(g + 1) * 3],
                    A_xm[:, g * 516 + 512:g * 516 + 515]), inc="dve")
            conv_done = P.barrier_snapshot()
            tap("u", A_u[:], conv_done)
            tap("xm", A_xm[:], conv_done)
            tap("sres", A_sres[:], conv_done)
            tap("wslot", ring[0][:], conv_done)

            P.w("pe", act=prev_u[NG - 1])
            for g in range(NG):
                pe_c = P.e("pe", lambda e, g=g: e.matmul(
                    ps[4][0:96, :], s_xproj[:, g * 96:(g + 1) * 96], usl(g),
                    start=(g == 0), stop=(g == NG - 1)),
                    inc="pe" if g == NG - 1 else None)
            P.w("dve", pe=pe_c)
            P.e("dve", lambda e: e.tensor_copy(dt_t[0:64, :], ps[4][0:64, :]))
            P.e("dve", lambda e: e.tensor_copy(dt_t[64:128, :], ps[4][0:64, :]))
            bc_dve = P.e("dve", lambda e: e.tensor_copy(bcb[:], ps[4][64:96, :]),
                         inc="dve")

            tap("bcb", bcb[:], P.barrier_snapshot())
            del_done = {}
            for g in range(NG):
                bank = ps[2 + g % 2]
                P.w("pe", dve=bc_dve)
                if g >= 2:
                    P.w("pe", dve=del_done[g - 2])
                dtw = (s_dtproj[0:64, g * 128:(g + 1) * 128] if g < 8 else
                       s_dtproj[64:128, (g - 8) * 128:(g - 7) * 128])
                dtr = dt_t[0:64, :] if g < 8 else dt_t[64:128, :]
                pe_c = P.e("pe", lambda e, b=bank, w_=dtw, r_=dtr: e.matmul(
                    b[:], w_, r_, start=True, stop=True), inc="pe")
                P.w("act", pe=pe_c)
                if g == 0:
                    P.w("act", dve=fh["xn_done"])
                if g >= 2:
                    P.w("act", dve=del_done[g - 2])
                rt = (0 if g % 2 == 0 else 3) * T
                at = rt + T
                P.e("act", lambda e, b=bank, g=g, rt=rt: e.activation(
                    A_tmp[:, rt:rt + T], b[:], AF.Relu, bias=s_dtb[:, g:g + 1]))
                P.e("act", lambda e, b=bank, g=g, at=at: e.activation(
                    A_tmp[:, at:at + T], b[:], AF.Abs, bias=s_dtb[:, g:g + 1]))
                P.e("act", lambda e, at=at: e.activation(
                    A_tmp[:, 2 * T:3 * T], A_tmp[:, at:at + T], AF.Exp,
                    scale=s_cst[:, 0:1]))
                ac = P.e("act", lambda e, at=at: e.activation(
                    A_tmp[:, at:at + T], A_tmp[:, 2 * T:3 * T], AF.Ln, bias=1.0),
                    inc="act")
                P.w("dve", act=ac)
                del_done[g] = P.e("dve", lambda e, g=g, rt=rt, at=at:
                                  e.tensor_tensor(
                    dsl(g), A_tmp[:, rt:rt + T], A_tmp[:, at:at + T], OP.add),
                    inc="dve")
            dt_done = P.barrier_snapshot()
            tap("delta", A_delta[:], dt_done)

            bc_ev = {}
            for j in range(32):
                bank = ps[4 + j % 2]
                P.w("pe", dve=bc_dve)
                if j >= 2:
                    P.w("pe", act=bc_ev[j - 2])
                pe_c = P.e("pe", lambda e, b=bank, j=j: e.matmul(
                    b[:], s_selbc[:, j * 128:(j + 1) * 128], bcb[:],
                    start=True, stop=True), inc="pe")
                P.w("act", pe=pe_c)
                bc_ev[j] = P.e("act", lambda e, b=bank, j=j: e.activation(
                    A_big4[:, j * T:(j + 1) * T], b[:], AF.Identity), inc="act")

            scan_pre = P.barrier_snapshot()
            idmm_rel = {0: 0, 1: 0}
            yfin = {}
            QT = 4 * T
            for g in range(NG):
                P.w("dve", **{k: v for k, v in scan_pre.items() if k != "dve"})
                du_c = P.e("dve", lambda e, g=g: e.tensor_tensor(
                    du_t[:], dsl(g), usl(g), OP.mult), inc="dve")
                if g == NG - 1:
                    fh["du15"] = du_c
                if g >= 2:
                    P.w("act", pe=idmm_rel[0])
                ud_ac = P.e("act", lambda e, g=g: e.activation(
                    ud_t[:], usl(g), AF.Identity, scale=s_dcol[:, g:g + 1]),
                    inc="act")
                for q in range(4):
                    dab = A_dA[:, (q % 2) * 2064:(q % 2 + 1) * 2064]
                    dbb = A_dBu[:, (q % 2) * 2064:(q % 2 + 1) * 2064]
                    hb = A_h[:, 0:2064]
                    P.w("act", pe=idmm_rel[q % 2])
                    P.w("act", dve=scan_pre["dve"])
                    for n4 in range(4):
                        n = 4 * q + n4
                        ac = P.e("act", lambda e, d=dab, n4=n4, g=g, n=n:
                                 e.activation(d[:, n4 * 516 + 4:(n4 + 1) * 516],
                                              dsl(g), AF.Exp,
                                              scale=s_asc[:, g * NS + n:g * NS + n + 1]),
                                 inc="act" if n4 == 3 else None)
                    if g == NG - 1 and q == 3:
                        fh["exp15"] = ac
                    P.e("dve", lambda e, d=dbb, g=g, q=q: e.tensor_copy(
                        d[:, 0:2064:516],
                        carry[:, g * NS + 4 * q:g * NS + 4 * q + 4]))
                    for n4 in range(4):
                        n = 4 * q + n4
                        P.e("dve", lambda e, d=dbb, n4=n4, n=n:
                            e.tensor_tensor(d[:, n4 * 516 + 4:(n4 + 1) * 516],
                                            du_t[:],
                                            A_big4[:, n * T:(n + 1) * T], OP.mult))
                    if q == 0 and g >= 1:
                        P.w("dve", pe=idmm_rel[1])
                        yfin[g - 1] = P.e("dve", lambda e, b=ps[4 + (g - 1) % 2],
                                          g1=g - 1: e.tensor_tensor(
                            ysl(g1), b[:], srsl(g1), OP.mult), inc="dve")
                    P.w("dve", act=ac)
                    ts = P.e("dve", lambda e, da=dab, db=dbb, hb=hb:
                             e.tensor_tensor_scan(hb, da, db, 0.0, OP.mult, OP.add),
                             inc="dve")
                    P.w("dve", dve=ts)
                    P.e("dve", lambda e, hb=hb, g=g, q=q: e.tensor_copy(
                        carry[:, g * NS + 4 * q:g * NS + 4 * q + 4],
                        hb[:, 515:2064:516]))
                    for n4 in range(4):
                        n = 4 * q + n4
                        dv = P.e("dve", lambda e, hb=hb, d=dab, n4=n4, n=n:
                                 e.tensor_tensor(d[:, n4 * 516 + 4:(n4 + 1) * 516],
                                                 hb[:, n4 * 516 + 4:(n4 + 1) * 516],
                                                 A_big4[:, 8192 + n * T:8192 + (n + 1) * T],
                                                 OP.mult),
                                 inc="dve" if n4 == 3 else None)
                    ybank = ps[4 + g % 2]
                    P.w("pe", dve=dv)
                    if q == 0:
                        if g >= 2:
                            P.w("pe", dve=yfin[g - 2])
                        P.w("pe", act=ud_ac)
                        P.e("pe", lambda e, b=ybank: e.matmul(
                            b[:], s_id[:], ud_t[:], start=True, stop=False))
                    for n4 in range(4):
                        pe_c = P.e("pe", lambda e, b=ybank, d=dab, n4=n4,
                                   last=(q == 3 and n4 == 3):
                                   e.matmul(b[:], s_id[:],
                                            d[:, n4 * 516 + 4:(n4 + 1) * 516],
                                            start=False, stop=last),
                                   inc="pe" if n4 == 3 else None)
                    idmm_rel[q % 2] = pe_c
            P.w("dve", pe=idmm_rel[1])
            yfin[NG - 1] = P.e("dve", lambda e, b=ps[4 + (NG - 1) % 2]:
                               e.tensor_tensor(
                ysl(NG - 1), b[:], srsl(NG - 1), OP.mult), inc="dve")
            scan_done = P.barrier_snapshot()
            tap("y", A_xm[:, 0:8192], scan_done)

            P.wait_barrier("pe", scan_done)
            for j in range(2):
                slot, sidx, wl_t = op_slots[j]
                P.w("pe", **wl_t)
                for k8 in range(8):
                    kabs = j * 8 + k8
                    for mt in range(8):
                        pe_c = P.e("pe", lambda e, b=ps[mt], s=slot, k8=k8, mt=mt,
                                   ka=kabs, st=(kabs == 0), sp=(kabs == 15):
                                   e.matmul(b[:], s[:, k8 * 1024 + mt * 128:
                                                   k8 * 1024 + (mt + 1) * 128],
                                            ysl(ka), start=st, stop=sp),
                                   inc="pe" if (k8 == 7 and mt == 7) else None)
                release_slot(sidx)
            P.w("act", pe=pe_c)
            for mt in range(8):
                a_mt = P.e("act", lambda e, mt=mt: e.activation(
                    m_f32[:, mt * T:(mt + 1) * T], ps[mt][:], AF.Identity),
                    inc="act")
                P.w("dve", act=a_mt)
                P.e("dve", lambda e, mt=mt: e.tensor_copy(
                    xnsl(mt), ps[mt][:]), inc="dve" if mt == 7 else None)
            op_done = P.barrier_snapshot()
            fh["op_done"] = op_done
            tap("m", A_x[:].bitcast(F32), op_done)

            gel_ev = {}
            for j in range(4):
                slot, sidx, wl_t = w1_slots[j]
                for mt in range(8):
                    mabs = j * 8 + mt
                    bank = ps[mt % 2]
                    P.w("pe", **wl_t)
                    if mabs == 0:
                        P.wait_barrier("pe", op_done)
                    if mabs >= 2:
                        P.w("pe", act=gel_ev[mabs - 2])
                    for k in range(NKD):
                        pe_c = P.e("pe", lambda e, b=bank, s=slot, k=k, mt=mt:
                                   e.matmul(b[:], s[:, k * 1024 + mt * 128:
                                                   k * 1024 + (mt + 1) * 128],
                                            xnsl(k), start=(k == 0), stop=(k == 7)),
                                   inc="pe" if k == 7 else None)
                    P.w("act", pe=pe_c)
                    gel_ev[mabs] = P.e("act", lambda e, b=bank, mabs=mabs:
                                       e.activation(A_big4[:, mabs * T:(mabs + 1) * T],
                                                    b[:], AF.Gelu,
                                                    bias=s_b1[:, mabs:mabs + 1]),
                                       inc="act")
                release_slot(sidx)

            if c + 1 < NCH:
                front_head(c + 1, False)

            for j in range(4):
                slot, sidx, wl_t = w2_slots[j]
                P.w("pe", **wl_t)
                if j == 0:
                    P.w("pe", act=gel_ev[31])
                    P.w("pe", dve=fh["murow"])
                for k8 in range(8):
                    kabs = j * 8 + k8
                    for mt in range(8):
                        pe_c = P.e("pe", lambda e, b=ps[mt], s=slot, k8=k8, mt=mt,
                                   ka=kabs, st=(kabs == 0), sp=(kabs == 31):
                                   e.matmul(b[:], s[:, k8 * 1024 + mt * 128:
                                                   k8 * 1024 + (mt + 1) * 128],
                                            A_big4[:, ka * T:(ka + 1) * T],
                                            start=st, stop=sp),
                                   inc="pe" if (k8 == 7 and mt == 7) else None)
                release_slot(sidx)
            P.w("dve", pe=pe_c)
            cdone = None
            for mt in range(8):
                cdone = P.e("dve", lambda e, mt=mt, c=c: e.scalar_tensor_tensor(
                    c_f32[:, mt * T:(mt + 1) * T], ps[mt][:], s_b2[:, mt:mt + 1],
                    m_f32[:, mt * T:(mt + 1) * T], OP.add, OP.add,
                    accum_out=stats_c[:, c * 8 + mt:c * 8 + mt + 1]), inc="dve")
            P.w("act", dve=cdone)
            sq_done = None
            sqd = {}
            for mt in range(8):
                if mt >= 2:
                    P.w("act", dve=sqd[mt - 2])
                sa = P.e("act", lambda e, mt=mt: e.activation(
                    A_tmp[:, (mt % 2) * T:(mt % 2 + 1) * T],
                    c_f32[:, mt * T:(mt + 1) * T], AF.Square), inc="act")
                P.w("dve", act=sa)
                sq_done = P.e("dve", lambda e, mt=mt, c=c: e.tensor_reduce(
                    stats_c2[:, c * 8 + mt:c * 8 + mt + 1],
                    A_tmp[:, (mt % 2) * T:(mt % 2 + 1) * T],
                    axis=mybir.AxisListType.X, op=OP.add), inc="dve")
                sqd[mt] = sq_done
            P.w("sync", dve=cdone)
            cstore_sl[0] = P.e("sync", lambda e, c=c: e.dma_start(
                out=c_scr[:, c * T:(c + 1) * T].rearrange("(m p) t -> p m t", p=128),
                in_=c_f32[:, 0:4096].rearrange("p (m t) -> p m t", m=8)),
                inc="sl", n=16)
            ctap = P.barrier_snapshot()
            tap("c", c_f32[:, 0:4096], ctap)
        snapA = P.barrier_snapshot()
        P.wait_barrier("dve", snapA)

        def dstep(fn):
            v = P.e("dve", fn, inc="dve")
            P.w("dve", dve=v)
            return v

        dstep(lambda e: e.tensor_reduce(sc_red[:, 0:1], stats_c[:],
                                        axis=mybir.AxisListType.X, op=OP.add))
        dstep(lambda e: e.tensor_reduce(sc_red[:, 1:2], stats_c2[:],
                                        axis=mybir.AxisListType.X, op=OP.add))
        rdve = dstep(lambda e: e.tensor_copy(sc_red_r[:], sc_red[:]))
        P.w("pe", dve=rdve)
        P.e("pe", lambda e: e.matmul(ps[1][0:1, 0:2], s_oneskr[:], sc_red_r[:],
                                     start=True, stop=True), inc="pe")
        pe_fin = P.cnt["pe"]
        P.w("dve", pe=pe_fin)
        NEL = float(L * DM)
        dstep(lambda e: e.tensor_scalar(t_s1[0:1, 0:2], ps[1][0:1, 0:2],
                                        1.0 / NEL, None, op0=OP.mult))
        dstep(lambda e: e.tensor_tensor(t_s1[0:1, 2:3], t_s1[0:1, 0:1],
                                        t_s1[0:1, 0:1], OP.mult))
        vdve = dstep(lambda e: e.tensor_tensor(t_s1[0:1, 3:4], t_s1[0:1, 1:2],
                                               t_s1[0:1, 2:3], OP.subtract))
        P.w("act", dve=vdve)
        aa = P.e("act", lambda e: e.activation(t_s1[0:1, 4:5], t_s1[0:1, 3:4],
                                               AF.Ln, bias=s_cst[0:1, 2:3]),
                 inc="act")
        P.w("act", act=aa)
        adone = P.e("act", lambda e: e.activation(t_s1[0:1, 5:6], t_s1[0:1, 4:5],
                                                  AF.Exp, scale=s_cst[0:1, 1:2]),
                    inc="act")
        P.w("dve", act=adone)
        dstep(lambda e: e.tensor_tensor(t_s1[0:1, 6:7], t_s1[0:1, 0:1],
                                        t_s1[0:1, 5:6], OP.mult))
        dstep(lambda e: e.tensor_scalar(t_s1[0:1, 6:7], t_s1[0:1, 6:7], -1.0,
                                        None, op0=OP.mult))
        rdve2 = dstep(lambda e: e.tensor_copy(t_s1r[0:1, 0:2], t_s1[0:1, 5:7]))
        P.w("pe", dve=rdve2)
        P.e("pe", lambda e: e.matmul(ps[2][:, 0:2], s_ones1r[:], t_s1r[0:1, 0:2],
                                     start=True, stop=True), inc="pe")
        pe_b = P.cnt["pe"]
        P.w("dve", pe=pe_b)
        dstep(lambda e: e.tensor_copy(rstd_col[:], ps[2][:, 0:1]))
        dstep(lambda e: e.tensor_copy(nmr_col[:], ps[2][:, 1:2]))
        statsB = P.barrier_snapshot()
        tap("slab", t_s1[0:1, 0:8], statsB)

        ostore_sl = [0]
        ping_f32 = [x2_r, A_xm[:].bitcast(F32)[:, 0:4096]]
        pb = {"pe_stats": P.cnt["pe"], "pe_wfc": P.cnt["pe"],
              "st_done": P.cnt["dve"], "hq31": P.cnt["act"], "x2f": {}}

        def pb_load(c):
            xl = P.e("gp", lambda e, c=c: e.dma_start(
                out=A_x[:].rearrange("p (k t) -> p k t", k=NKD),
                in_=x_t[:, c * T:(c + 1) * T].rearrange("(k p) t -> p k t", p=128)),
                inc="xv", n=16)
            cl = P.e("sync", lambda e, c=c: e.dma_start(
                out=c_f32[:, 0:4096].rearrange("p (m t) -> p m t", m=8),
                in_=c_scr[:, c * T:(c + 1) * T].rearrange("(m p) t -> p m t", p=128)),
                inc="cl", n=16)
            pb[("xl", c)], pb[("cl", c)] = xl, cl

        def pb_front1(c):
            if ("xl", c) not in pb:
                snapB = P.barrier_snapshot()
                P.wait_barrier("gp", snapB)
                P.wait_barrier("sync", snapB)
                pb_load(c)
            xl, cl = pb.pop(("xl", c)), pb.pop(("cl", c))
            fc_slots = []
            for j in range(4):
                srcw = wfct[:, j * 1024:(j + 1) * 1024].rearrange(
                    "(k p) m -> p k m", p=128)
                fc_slots.append(fill_slot(srcw))

            x2f = ping_f32[c % 2]
            pb["x2f"][c] = x2f
            P.w("dve", xv=xl, cl=cl, sl=ostore_sl[0], act=pb["hq31"])
            x2d = None
            for mt in range(8):
                P.e("dve", lambda e, x2f=x2f, mt=mt: e.tensor_scalar(
                    A_tmp[:, 0:T], c_f32[:, mt * T:(mt + 1) * T],
                    rstd_col[:, 0:1], nmr_col[:, 0:1], op0=OP.mult, op1=OP.add))
                x2d = P.e("dve", lambda e, x2f=x2f, mt=mt: e.tensor_tensor(
                    x2f[:, mt * T:(mt + 1) * T], A_tmp[:, 0:T],
                    A_x[:, mt * T:(mt + 1) * T], OP.add),
                    inc="dve" if mt == 7 else None)
            if c + 1 < NCH:
                P.w("gp", dve=x2d)
                P.w("sync", dve=x2d)
                pb_load(c + 1)

            def x2sl(k, x2f=x2f):
                return x2f[:, k * T:(k + 1) * T]
            P.w("dve", pe=pb["pe_stats"])
            for mt in range(8):
                P.e("dve", lambda e, x2f=x2f, mt=mt: e.tensor_copy(
                    A_dA[:, mt * T:(mt + 1) * T],
                    x2f[:, mt * T:(mt + 1) * T]), inc="dve" if mt == 7 else None)

            def x2b(k):
                return A_dA[:, k * T:(k + 1) * T]
            ln_stats(P, ps, x2b, A_h[:, 0:2 * T], ra, rb, rrb,
                     A_tmp[0:1, 0:T], s_oneskb, s_ones1b, s_cst, None,
                     P.barrier_snapshot(),
                     snap_act={"pe": pb["pe_stats"], "dve": P.cnt["dve"],
                               "act": 0})
            pb["pe_stats"] = P.cnt["pe"]
            P.w("dve", pe=pb["pe_wfc"])
            for k in range(NKD):
                P.e("dve", lambda e, k=k, x2sl=x2sl: e.tensor_tensor(
                    A_tmp[:, 0:T], x2sl(k), ps[6][:], OP.subtract))
                P.e("dve", lambda e, k=k: e.tensor_tensor(
                    xnsl(k), A_tmp[:, 0:T], ps[7][:], OP.mult), inc="dve")
            ln2_done = P.barrier_snapshot()
            tap("x2", x2f[:, 0:4096], ln2_done)

            hq_ev = {}
            for j in range(4):
                slot, sidx, wl_t = fc_slots[j]
                for mt in range(8):
                    mabs = j * 8 + mt
                    bank = ps[mt % 2]
                    P.w("pe", **wl_t)
                    if mabs == 0:
                        P.wait_barrier("pe", ln2_done)
                    if mabs >= 2:
                        P.w("pe", act=hq_ev[mabs - 2])
                    for k in range(NKD):
                        pe_c = P.e("pe", lambda e, b=bank, s=slot, k=k, mt=mt:
                                   e.matmul(b[:], s[:, k * 1024 + mt * 128:
                                                   k * 1024 + (mt + 1) * 128],
                                            xnsl(k), start=(k == 0), stop=(k == 7)),
                                   inc="pe" if k == 7 else None)
                    P.w("act", pe=pe_c)
                    P.e("act", lambda e, b=bank, mabs=mabs: e.activation(
                        A_tmp[:, (mabs % 2) * T:(mabs % 2 + 1) * T], b[:], AF.Relu))
                    hq_ev[mabs] = P.e("act", lambda e, mabs=mabs: e.activation(
                        A_big4[:, mabs * T:(mabs + 1) * T],
                        A_tmp[:, (mabs % 2) * T:(mabs % 2 + 1) * T], AF.Square),
                        inc="act")
                release_slot(sidx)
            pb["pe_wfc"] = P.cnt["pe"]
            pb["hq31"] = hq_ev[31]
            return hq_ev[31]

        def pb_front2(c, hq31):
            pj_slots = []
            for j in range(4):
                srcw = wprojt[j * 1024:(j + 1) * 1024, :].rearrange(
                    "(k p) m -> p k m", p=128)
                pj_slots.append(fill_slot(srcw))
            pe_c = None
            for j in range(4):
                slot, sidx, wl_t = pj_slots[j]
                P.w("pe", **wl_t)
                if j == 0:
                    P.w("pe", act=hq31)
                for k8 in range(8):
                    kabs = j * 8 + k8
                    for mt in range(8):
                        pe_c = P.e("pe", lambda e, b=ps[mt], s=slot, k8=k8, mt=mt,
                                   st=(kabs == 0), sp=(kabs == 31), ka=kabs:
                                   e.matmul(b[:], s[:, k8 * 1024 + mt * 128:
                                                   k8 * 1024 + (mt + 1) * 128],
                                            A_big4[:, ka * T:(ka + 1) * T],
                                            start=st, stop=sp),
                                   inc="pe" if (k8 == 7 and mt == 7) else None)
                release_slot(sidx)
            P.w("act", pe=pe_c, dve=pb["st_done"])
            ysv = None
            for mt in range(8):
                ysv = P.e("act", lambda e, mt=mt: e.activation(
                    A_delta[:, mt * T:(mt + 1) * T], ps[mt][:], AF.Identity),
                    inc="act" if mt == 7 else None)
            ys_done = P.barrier_snapshot()
            tap("ys", A_delta[:, 0:4096], ys_done)
            pb[("ysv", c)] = ysv

        def pb_spike(c):
            ysv = pb.pop(("ysv", c))
            x2f = pb["x2f"].pop(c)
            P.w("dve", act=ysv)
            st_done = None
            for mt in range(8):
                y_ = A_delta[:, mt * T:(mt + 1) * T]
                spk = A_dBu[:, mt * T:(mt + 1) * T]
                mem = A_delta[:, 4096 + mt * T:4096 + (mt + 1) * T]
                o_ = x2f[:, mt * T:(mt + 1) * T]
                P.e("dve", lambda e, y_=y_, spk=spk: e.tensor_scalar(
                    spk, y_, 1.0, None, op0=OP.is_gt))
                P.e("dve", lambda e, o_=o_, spk=spk: e.scalar_tensor_tensor(
                    o_, spk, 0.2, o_, OP.mult, OP.add))
                P.e("dve", lambda e, y_=y_, mem=mem: e.tensor_scalar(
                    mem, y_, 1.9, None, op0=OP.mult))
                P.e("dve", lambda e, mem=mem, spk=spk: e.tensor_tensor(
                    mem, mem, spk, OP.subtract))
                for it in range(3):
                    P.e("dve", lambda e, mem=mem, spk=spk: e.tensor_scalar(
                        spk, mem, 1.0, None, op0=OP.is_gt))
                    P.e("dve", lambda e, o_=o_, spk=spk: e.scalar_tensor_tensor(
                        o_, spk, 0.2, o_, OP.mult, OP.add))
                    P.e("dve", lambda e, mem=mem: e.tensor_scalar(
                        mem, mem, 0.9, None, op0=OP.mult))
                    P.e("dve", lambda e, mem=mem, y_=y_: e.tensor_tensor(
                        mem, mem, y_, OP.add))
                    P.e("dve", lambda e, mem=mem, spk=spk: e.tensor_tensor(
                        mem, mem, spk, OP.subtract))
                P.e("dve", lambda e, mem=mem, spk=spk: e.tensor_scalar(
                    spk, mem, 1.0, None, op0=OP.is_gt))
                st_done = P.e("dve", lambda e, o_=o_, spk=spk:
                              e.scalar_tensor_tensor(
                    o_, spk, 0.2, o_, OP.mult, OP.add), inc="dve")
            pb["st_done"] = st_done
            P.w("sync", dve=st_done)
            ostore_sl[0] = P.e("sync", lambda e, c=c, x2f=x2f: e.dma_start(
                out=out_t[:, c * T:(c + 1) * T].rearrange("(m p) t -> p m t", p=128),
                in_=x2f[:, 0:4096].rearrange("p (m t) -> p m t", m=8)),
                inc="sl", n=16)

        for c in range(NCH):
            hq31 = pb_front1(c)
            if c > 0:
                pb_spike(c - 1)
            pb_front2(c, hq31)
        pb_spike(NCH - 1)

        P.w("sync", sl=P.cnt["sl"], xv=P.cnt["xv"], cl=P.cnt["cl"],
            ws0=P.cnt["ws0"], ws1=P.cnt["ws1"])

        @block.sync
        def _(e):
            for fn in P.q["sync"]:
                fn(e)

        @block.vector
        def _(e):
            for fn in P.q["dve"]:
                fn(e)

        @block.scalar
        def _(e):
            for fn in P.q["act"]:
                fn(e)

        @block.tensor
        def _(e):
            for fn in P.q["pe"]:
                fn(e)

        @block.gpsimd
        def _(e):
            for fn in P.q["gp"]:
                fn(e)

    return nc


def ln_stats(P, ps, xslfn, sqbuf, ra, rb, rrb, rc,
             s_oneskb, s_ones1b, s_cst, xl, snap, snap_act=None):
    if xl is not None:
        P.w("pe", xv=xl)
        P.w("act", xv=xl)
    P.wait_barrier("pe", snap)
    P.wait_barrier("act", snap_act if snap_act is not None else snap)
    sq_pe = {}
    for k in range(NKD):
        P.e("pe", lambda e, k=k: e.matmul(ps[6][0:1, :], s_oneskb[:], xslfn(k),
                                          start=(k == 0), stop=(k == NKD - 1)),
            inc="pe" if k == NKD - 1 else None)
    for k in range(NKD):
        buf = sqbuf[:, (k % 2) * T:(k % 2 + 1) * T]
        if k >= 2:
            P.w("act", pe=sq_pe[k - 2])
        a_k = P.e("act", lambda e, b=buf, k=k: e.activation(
            b, xslfn(k), AF.Square), inc="act")
        P.w("pe", act=a_k)
        sq_pe[k] = P.e("pe", lambda e, b=buf, k=k: e.matmul(
            ps[7][0:1, :], s_oneskb[:], b,
            start=(k == 0), stop=(k == NKD - 1)), inc="pe")
    pe_c = P.cnt["pe"]
    P.w("dve", pe=pe_c)
    P.e("dve", lambda e: e.tensor_copy(ra[:], ps[6][0:1, :]))
    P.e("dve", lambda e: e.tensor_copy(rb[:], ps[7][0:1, :]))
    P.e("dve", lambda e: e.tensor_scalar(ra[:], ra[:], 1.0 / DM, None,
                                         op0=OP.mult))
    P.e("dve", lambda e: e.tensor_scalar(rb[:], rb[:], 1.0 / DM, None,
                                         op0=OP.mult))
    P.e("dve", lambda e: e.tensor_tensor(rc, ra[:], ra[:], OP.mult))
    vd = P.e("dve", lambda e: e.tensor_tensor(rb[:], rb[:], rc, OP.subtract),
             inc="dve")
    P.w("act", dve=vd)
    P.e("act", lambda e: e.activation(rc, rb[:], AF.Ln, bias=s_cst[0:1, 2:3]))
    ad = P.e("act", lambda e: e.activation(rb[:], rc, AF.Exp,
                                           scale=s_cst[0:1, 1:2]), inc="act")
    P.w("dve", act=ad)
    P.e("dve", lambda e: e.tensor_copy(rrb[0:1, 0:T], ra[:]))
    cd = P.e("dve", lambda e: e.tensor_copy(rrb[0:1, T:2 * T], rb[:]), inc="dve")
    P.w("pe", dve=cd)
    P.e("pe", lambda e: e.matmul(ps[6][:], s_ones1b[:], rrb[0:1, 0:T],
                                 start=True, stop=True))
    bc = P.e("pe", lambda e: e.matmul(ps[7][:], s_ones1b[:], rrb[0:1, T:2 * T],
                                      start=True, stop=True), inc="pe")
    P.w("dve", pe=bc)



L_FULL = 8192


def kernel(**inputs):
    maps = host_prep(inputs, L_FULL)
    nc = build(L_FULL)
    from concourse.bass_utils import run_bass_kernel_spmd
    res = run_bass_kernel_spmd(nc, maps, core_ids=list(range(8)))
    out = np.stack([np.ascontiguousarray(res.results[b]["out_t"].T)
                    for b in range(8)], axis=0)
    return out


# version 09 (final)
# speedup vs baseline: 1.0175x; 1.0118x over previous
import contextlib
from collections import defaultdict
import numpy as np
import ml_dtypes
import concourse.bass as bass
import concourse.mybir as mybir

F32 = mybir.dt.float32
BF16 = mybir.dt.bfloat16
F32R = mybir.dt.float32r
AF = mybir.ActivationFunctionType
OP = mybir.AluOpType

DM, DI, NS, DC, DTR = 1024, 2048, 16, 4, 64
EPS = 1e-5
NKD = DM // 128
NG = DI // 128
T = 512
NSLOT = 2

ENGS = ("sync", "dve", "act", "pe", "gp")


class Prog:

    def __init__(self, nc, sems):
        self.nc = nc
        self.sems = sems
        self.q = {e: [] for e in ENGS}
        self.cnt = defaultdict(int)
        self.trace = []

    def w(self, eng, **waits):
        for s, v in waits.items():
            if v <= 0:
                continue
            sem = self.sems[s]
            self.trace.append(("wait", eng, s, v))
            self.q[eng].append(lambda e, sem=sem, v=v: e.wait_ge(sem, v))

    def w_lazy(self, eng, s, cell):
        sem = self.sems[s]
        self.trace.append(("waitlazy", eng, s, cell))
        def fn(e, sem=sem, cell=cell):
            if cell[0] > 0:
                e.wait_ge(sem, cell[0])
        self.q[eng].append(fn)

    def e(self, eng, fn, inc=None, n=1):
        if inc is None:
            self.trace.append(("op", eng, None, 0))
            self.q[eng].append(lambda e, fn=fn: fn(e))
            return None
        sem = self.sems[inc]
        self.cnt[inc] += n
        self.trace.append(("op", eng, inc, n))
        self.q[eng].append(lambda e, fn=fn, sem=sem, n=n: fn(e).then_inc(sem, n))
        return self.cnt[inc]

    def barrier_snapshot(self):
        return {k: self.cnt[k] for k in ("pe", "dve", "act")}

    def wait_barrier(self, eng, snap):
        self.w(eng, **{k: v for k, v in snap.items() if k != _ENG2SEM.get(eng)})


_ENG2SEM = {"pe": "pe", "dve": "dve", "act": "act"}


def host_prep(inp, L):
    bf = ml_dtypes.bfloat16
    f32 = np.float32
    c = np.ascontiguousarray
    w = {}
    w["win_t"] = c(inp["L0_attn_in_proj_w"].T)
    cw = inp["L0_attn_conv_w"]
    CW = np.zeros((128, DC * NG * 128), f32)
    ar = np.arange(128)
    for k in range(DC):
        for g in range(NG):
            CW[ar, (k * NG + g) * 128 + ar] = cw[g * 128:(g + 1) * 128, 0, k]
    w["cdiag"] = CW
    w["convb"] = c(inp["L0_attn_conv_b"].reshape(NG, 128).T)
    w["xprojt"] = c(inp["L0_attn_x_proj_w"].T.reshape(NG, 128, 96)
                    .transpose(1, 0, 2).reshape(128, NG * 96))
    dtp = inp["L0_attn_dt_proj_w"].T
    w["dtprojt"] = np.concatenate([dtp[:, :1024], dtp[:, 1024:]], 0)
    w["dtb"] = c(inp["L0_attn_dt_proj_b"].reshape(NG, 128).T)
    w["alogt"] = c(inp["L0_attn_A_log"].reshape(NG, 128, NS)
                   .transpose(1, 0, 2).reshape(128, NG * NS))
    w["dcol"] = c(inp["L0_attn_D"].reshape(NG, 128).T)
    w["outprojt"] = c(inp["L0_attn_out_proj_w"].T)
    w["w1t"] = c(inp["L0_attn_ffn_w1"].T)
    w["b1c"] = c(inp["L0_attn_ffn_b1"].reshape(32, 128).T)
    w["w2t"] = c(inp["L0_attn_ffn_w2"].T)
    w["b2c"] = c(inp["L0_attn_ffn_b2"].reshape(8, 128).T)
    w["wfct"] = c(inp["L0_ffn_W_fc"].T)
    w["wprojt"] = c(inp["L0_ffn_W_proj"].T)
    sel = np.zeros((32, 32 * 128), bf)
    for j in range(32):
        sel[j, j * 128:(j + 1) * 128] = 1
    w["selbc"] = sel
    w["id128"] = np.eye(128, dtype=bf)
    w["ones1"] = np.ones((1, 128), f32)
    w["onesk"] = np.ones((128, 1), f32)
    cst = np.zeros((128, 4), f32)
    cst[:, 0] = -1.0
    cst[:, 1] = -0.5
    cst[:, 2] = EPS
    w["cst"] = cst
    maps = []
    x = inp["L0_attn_seq_embeddings"]
    for b in range(8):
        m = dict(w)
        m["x_t"] = c(x[b, :L, :].T)
        maps.append(m)
    return maps


def build(L, taps=()):
    NCH = L // T
    nc = bass.Bass("TRN2", target_bir_lowering=False)
    D = {}

    def din(name, shape, dt=F32):
        D[name] = nc.dram_tensor(name, list(shape), dt, kind="ExternalInput").ap()
        return D[name]

    x_t = din("x_t", (DM, L))
    win_t = din("win_t", (DM, 4096))
    cdiag = din("cdiag", (128, 8192))
    convb = din("convb", (128, NG))
    xprojt = din("xprojt", (128, NG * 96))
    dtprojt = din("dtprojt", (128, 1024))
    dtb = din("dtb", (128, NG))
    alogt = din("alogt", (128, NG * NS))
    dcol = din("dcol", (128, NG))
    outprojt = din("outprojt", (DI, DM))
    w1t = din("w1t", (DM, 4096))
    b1c = din("b1c", (128, 32))
    w2t = din("w2t", (4096, DM))
    b2c = din("b2c", (128, 8))
    wfct = din("wfct", (DM, 4096))
    wprojt = din("wprojt", (4096, DM))
    selbc_d = din("selbc", (32, 4096), BF16)
    id128_d = din("id128", (128, 128), BF16)
    ones1_d = din("ones1", (1, 128))
    onesk_d = din("onesk", (128, 1))
    cst_d = din("cst", (128, 4))

    out_t = nc.dram_tensor("out_t", [DM, L], F32, kind="ExternalOutput").ap()
    c_scr = nc.dram_tensor("c_scr", [DM, L], F32).ap()
    x_bf = nc.dram_tensor("x_bf", [DM, L], BF16).ap()
    tap_d = {}
    for tspec in taps:
        tname, tshape = tspec[0], tspec[1]
        tdt = tspec[2] if len(tspec) > 2 else F32
        tap_d[tname] = nc.dram_tensor("tap_" + tname, list(tshape), tdt,
                                      kind="ExternalOutput").ap()

    es = contextlib.ExitStack()
    with es:
        def sb(name, shape, dt):
            return es.enter_context(nc.sbuf_tensor(name, list(shape), dt))

        ring = [sb(f"ring{i}", (128, 8192), BF16) for i in range(NSLOT)]
        A_x = sb("A_x", (128, 4096), F32R)
        A_xn = sb("A_xn", (128, 4096), BF16)
        A_xm = sb("A_xm", (128, NG * 516), BF16)
        A_sres = sb("A_sres", (128, 8192), BF16)
        A_big4 = sb("A_big4", (128, 16384), BF16)
        A_u = sb("A_u", (128, 8192), BF16)
        A_delta = sb("A_delta", (128, 8192), BF16)
        A_dA = sb("A_dA", (128, 4128), BF16)
        A_dBu = sb("A_dBu", (128, 4128), BF16)
        A_h = sb("A_h", (128, 2064), BF16)
        A_tmp = sb("A_tmp", (128, 2560), F32)
        dt_t = sb("dt_t", (128, T), BF16)
        bcb = sb("bcb", (32, T), BF16)
        carry = sb("carry", (128, NG * NS), BF16)
        xm_tail = sb("xm_tail", (128, NG * 3), BF16)
        fx1 = sb("fx1", (128, 4), F32)
        fx2 = sb("fx2", (128, 4), F32)
        stats_c = sb("stats_c", (128, NCH * 8), F32)
        stats_c2 = sb("stats_c2", (128, NCH * 8), F32)
        ra = A_tmp[0:1, T:2 * T]
        rb = A_tmp[0:1, 2 * T:3 * T]
        sc_red = sb("sc_red", (128, 2), F32)
        sc_red_r = sb("sc_red_r", (128, 2), F32R)
        t_s1 = sb("t_s1", (1, 8), F32)
        t_s1r = sb("t_s1r", (1, 2), F32R)
        rstd_col = sb("rstd_col", (128, 1), F32)
        nmr_col = sb("nmr_col", (128, 1), F32)
        s_xproj = sb("s_xproj", (128, NG * 96), BF16)
        s_dtproj = sb("s_dtproj", (128, 1024), BF16)
        s_dcol = sb("s_dcol", (128, NG), F32)
        ud_t = sb("ud_t", (128, T), BF16)
        du_t = sb("du_t", (128, T), BF16)
        s_selbc = sb("s_selbc", (32, 4096), BF16)
        s_id = sb("s_id", (128, 128), BF16)
        s_ones1r = sb("s_ones1r", (1, 128), F32R)
        s_oneskr = sb("s_oneskr", (128, 1), F32R)
        s_asc = sb("s_asc", (128, NG * NS), F32)
        s_convb = sb("s_convb", (128, NG), F32)
        s_dtb = sb("s_dtb", (128, NG), F32)
        s_b1 = sb("s_b1", (128, 32), F32)
        s_b2 = sb("s_b2", (128, 8), F32)
        s_cst = sb("s_cst", (128, 4), F32)
        s_oneskb = sb("s_oneskb", (128, 1), BF16)
        s_ones1b = sb("s_ones1b", (1, 128), BF16)
        rrb = A_tmp[:].bitcast(BF16)[0:1, 0:2 * T]

        ps = [es.enter_context(nc.psum_tensor(f"ps{i}", [128, T], F32))
              for i in range(8)]

        sems = {n: es.enter_context(nc.semaphore("sem_" + n))
                for n in ("ws0", "ws1", "xv", "cl", "sl", "pe", "dve", "act")}
        block = es.enter_context(nc.Block())
        P = Prog(nc, sems)

        du_bf = None
        m_f32 = A_x[:].bitcast(F32)
        c_f32 = A_u[:].bitcast(F32)
        x2_r = A_sres[:].bitcast(F32)
        ys_f32 = A_delta[:].bitcast(F32)
        mem_f32 = A_xm[:, 0:8192].bitcast(F32)

        def xsl(k):
            return A_delta[:, k * T:(k + 1) * T]

        def xnsl(k):
            return A_xn[:, k * T:(k + 1) * T]

        def usl(g):
            return A_u[:, g * T:(g + 1) * T]

        def dsl(g):
            return A_delta[:, g * T:(g + 1) * T]

        def ysl(g):
            return A_xm[:, g * T:(g + 1) * T]

        def srsl(g):
            return A_sres[:, g * T:(g + 1) * T]

        ring_state = {"fill": 0, "relcell": [[0] for _ in range(NSLOT)]}

        def fill_slot(src_ap):
            s = ring_state["fill"] % NSLOT
            ring_state["fill"] += 1
            P.w_lazy("gp", "pe", ring_state["relcell"][s])
            newcell = [0]
            ring_state["relcell"][s] = newcell
            sname = f"ws{s}"
            wt = P.e("gp", lambda e, dst=ring[s], src=src_ap:
                     e.dma_start(out=dst[:].rearrange("p (k m) -> p k m",
                                                      k=src.shape[1]),
                                 in_=src), inc=sname, n=16)
            return ring[s], newcell, {sname: wt}

        def release_slot(cell):
            cell[0] = P.cnt["pe"]

        P.e("gp", lambda e: e.dma_start(out=s_xproj[:], in_=xprojt[:]), inc="xv", n=16)
        P.e("gp", lambda e: e.dma_start(out=s_dtproj[:], in_=dtprojt[:]), inc="xv", n=16)
        P.e("gp", lambda e: e.dma_start(out=s_dcol[:], in_=dcol[:]), inc="xv", n=16)
        P.e("gp", lambda e: e.dma_start(out=s_ones1r[:], in_=ones1_d[:]), inc="xv", n=16)
        P.e("gp", lambda e: e.dma_start(out=s_oneskr[:], in_=onesk_d[:]), inc="xv", n=16)
        P.e("gp", lambda e: e.dma_start(out=s_ones1b[:], in_=ones1_d[:]), inc="xv", n=16)
        P.e("gp", lambda e: e.dma_start(out=s_oneskb[:], in_=onesk_d[:]), inc="xv", n=16)
        P.e("sync", lambda e: e.dma_start(out=s_selbc[:], in_=selbc_d[:]), inc="sl", n=16)
        P.e("sync", lambda e: e.dma_start(out=s_id[:], in_=id128_d[:]), inc="sl", n=16)
        P.e("gp", lambda e: e.dma_start(out=A_x[:, 0:NG * NS], in_=alogt[:]), inc="xv", n=16)
        P.e("sync", lambda e: e.dma_start(out=s_convb[:], in_=convb[:]), inc="sl", n=16)
        P.e("sync", lambda e: e.dma_start(out=s_dtb[:], in_=dtb[:]), inc="sl", n=16)
        P.e("sync", lambda e: e.dma_start(out=s_b1[:], in_=b1c[:]), inc="sl", n=16)
        P.e("sync", lambda e: e.dma_start(out=s_b2[:], in_=b2c[:]), inc="sl", n=16)
        P.e("sync", lambda e: e.dma_start(out=s_cst[:], in_=cst_d[:]), inc="sl", n=16)
        xbf_done = P.e("gp", lambda e: e.dma_start(
            out=x_bf.rearrange("(k p) t -> p k t", p=128),
            in_=x_t[:].rearrange("(k p) t -> p k t", p=128)), inc="xv", n=16)
        pro_wl, pro_sl = P.cnt["xv"], P.cnt["sl"]
        P.w("gp", xv=pro_wl)

        P.w("act", sl=pro_sl, xv=pro_wl)
        P.e("act", lambda e: e.activation(s_asc[:], A_x[:, 0:NG * NS], AF.Exp),
            inc="act")
        P.w("dve", act=P.cnt["act"])
        P.e("dve", lambda e: e.tensor_scalar(s_asc[:], s_asc[:], -1.0, None,
                                             op0=OP.mult), inc="dve")
        P.e("dve", lambda e: e.memset(A_xm[:], 0.0), inc="dve")
        P.e("dve", lambda e: e.memset(carry[:], 0.0), inc="dve")
        mfc = P.e("dve", lambda e: e.memset(A_dA[:], 1.0), inc="dve")
        P.w("dve", dve=mfc)
        P.e("dve", lambda e: e.memset(A_dA[:, 0:4128:516], 0.0))
        P.e("dve", lambda e: e.memset(A_dBu[:], 0.0), inc="dve")

        def tap(name, tile_ap, wait_snap):
            if name in tap_d:
                P.wait_barrier("sync", wait_snap)
                tv = P.e("sync", lambda e, t=tile_ap, o=tap_d[name]:
                         e.dma_start(out=o[:], in_=t), inc="sl", n=16)
                for eng in ("dve", "act", "pe", "gp"):
                    P.w(eng, sl=tv)

        cstore_sl = [0]
        fh = {}

        def front_head(c, first):
            if first:
                snapH = P.barrier_snapshot()
                P.wait_barrier("sync", snapH)
                P.w("sync", xv=xbf_done)
                snapL = snapH
            else:
                P.w("sync", dve=fh["du15"], act=fh["exp15"])
                snapL = fh["op_done"]
            xl = P.e("sync", lambda e, c=c: e.dma_start(
                out=A_delta[:, 0:4096].rearrange("p (k t) -> p k t", k=NKD),
                in_=x_bf[:, c * T:(c + 1) * T].rearrange("(k p) t -> p k t", p=128)),
                inc="cl", n=16)
            ln_stats(P, ps, xsl, A_h[:, 0:2 * T], ra, rb, rrb,
                     A_tmp[0:1, 0:T], s_oneskb, s_ones1b, s_cst, xl, snapL,
                     xsem="cl")
            mu_sb = A_tmp[:, 0:T]
            rstd_sb = A_tmp[:, T:2 * T]
            P.e("dve", lambda e: e.tensor_copy(mu_sb, ps[6][:]))
            fh["murow"] = P.e("dve", lambda e: e.tensor_copy(rstd_sb, ps[7][:]),
                              inc="dve")
            if not first:
                P.w("dve", pe=P.cnt["pe"])
            xd = None
            for k in range(NKD):
                P.e("dve", lambda e, k=k, m_=mu_sb: e.tensor_tensor(
                    xnsl(k), xsl(k), m_, OP.subtract))
                xd = P.e("dve", lambda e, k=k, r_=rstd_sb: e.tensor_tensor(
                    xnsl(k), xnsl(k), r_, OP.mult),
                    inc="dve" if k == NKD - 1 else None)
            fh["xn_done"] = xd
            tap("xn", A_xn[:], P.barrier_snapshot())

        for c in range(NCH):
            if c == NCH - 1:
                tap("carry", carry[:], P.barrier_snapshot())
            if tap_d:
                for eng in ("dve", "act", "pe", "gp", "sync"):
                    P.w(eng, sl=P.cnt["sl"], wl=P.cnt["wl"])
            if c == 0:
                front_head(0, True)
            ip_slots = []
            for j in range(4):
                srcw = win_t[:, j * 1024:(j + 1) * 1024].rearrange(
                    "(k p) m -> p k m", p=128)
                ip_slots.append(fill_slot(srcw))
            cd_slot = fill_slot(cdiag[:].rearrange("p (k m) -> p k m", k=1))
            op_slots = []
            for j in range(2):
                srcw = outprojt[j * 1024:(j + 1) * 1024, :].rearrange(
                    "(k p) m -> p k m", p=128)
                op_slots.append(fill_slot(srcw))
            w1_slots = []
            for j in range(4):
                srcw = w1t[:, j * 1024:(j + 1) * 1024].rearrange(
                    "(k p) m -> p k m", p=128)
                w1_slots.append(fill_slot(srcw))
            w2_slots = []
            for j in range(4):
                srcw = w2t[j * 1024:(j + 1) * 1024, :].rearrange(
                    "(k p) m -> p k m", p=128)
                w2_slots.append(fill_slot(srcw))
            if c > 0:
                for g in range(NG):
                    P.e("dve", lambda e, g=g: e.tensor_copy(
                        A_xm[:, g * 516:g * 516 + 3],
                        xm_tail[:, g * 3:(g + 1) * 3]), inc="dve")

            evac_cnt = {}
            for j in range(4):
                slot, sidx, wl_t = ip_slots[j]
                for mt in range(8):
                    mabs = j * 8 + mt
                    bank = ps[mt % 2]
                    P.w("pe", dve=P.cnt["dve"] if mabs == 0 else 0, **wl_t)
                    if mabs >= 2:
                        P.w("pe", **{evac_cnt[mabs - 2][0]: evac_cnt[mabs - 2][1]})
                    for k in range(NKD):
                        pe_c = P.e("pe", lambda e, b=bank, s=slot, k=k, mt=mt:
                                   e.matmul(b[:], s[:, k * 1024 + mt * 128:
                                                   k * 1024 + (mt + 1) * 128],
                                            xnsl(k), start=(k == 0), stop=(k == 7)),
                                   inc="pe" if k == 7 else None)
                    if mabs < NG:
                        P.w("dve", pe=pe_c)
                        dc = P.e("dve", lambda e, b=bank, g=mabs: e.tensor_copy(
                            A_xm[:, g * 516 + 3:g * 516 + 515], b[:]), inc="dve")
                        evac_cnt[mabs] = ("dve", dc)
                    else:
                        P.w("act", pe=pe_c)
                        ac = P.e("act", lambda e, b=bank, g=mabs - NG: e.activation(
                            srsl(g), b[:], AF.Silu), inc="act")
                        evac_cnt[mabs] = ("act", ac)
                release_slot(sidx)

            cslot, cs_idx, cwl = cd_slot
            P.w("act", sl=cstore_sl[0])
            prev_u = {}
            for g in range(NG):
                bank = ps[2 + g % 2]
                P.w("pe", **cwl)
                if g == 0:
                    P.w("pe", **{evac_cnt[15][0]: evac_cnt[15][1]})
                    P.w("pe", dve=P.cnt["dve"])
                if g >= 2:
                    P.w("pe", act=prev_u[g - 2])
                for k in range(DC):
                    pe_c = P.e("pe", lambda e, b=bank, s=cslot, k=k, g=g:
                               e.matmul(b[:], s[:, (k * NG + g) * 128:
                                               (k * NG + g + 1) * 128],
                                        A_xm[:, g * 516 + k:g * 516 + k + T],
                                        start=(k == 0), stop=(k == DC - 1)),
                               inc="pe" if k == DC - 1 else None)
                P.w("act", pe=pe_c)
                prev_u[g] = P.e("act", lambda e, b=bank, g=g: e.activation(
                    usl(g), b[:], AF.Silu, bias=s_convb[:, g:g + 1]), inc="act")
            release_slot(cs_idx)
            for g in range(NG):
                P.e("dve", lambda e, g=g: e.tensor_copy(
                    xm_tail[:, g * 3:(g + 1) * 3],
                    A_xm[:, g * 516 + 512:g * 516 + 515]), inc="dve")
            conv_done = P.barrier_snapshot()
            tap("u", A_u[:], conv_done)
            tap("xm", A_xm[:], conv_done)
            tap("sres", A_sres[:], conv_done)
            tap("wslot", ring[0][:], conv_done)

            P.w("pe", act=prev_u[NG - 1])
            for g in range(NG):
                pe_c = P.e("pe", lambda e, g=g: e.matmul(
                    ps[4][0:96, :], s_xproj[:, g * 96:(g + 1) * 96], usl(g),
                    start=(g == 0), stop=(g == NG - 1)),
                    inc="pe" if g == NG - 1 else None)
            P.w("dve", pe=pe_c)
            P.e("dve", lambda e: e.tensor_copy(dt_t[0:64, :], ps[4][0:64, :]))
            P.e("dve", lambda e: e.tensor_copy(dt_t[64:128, :], ps[4][0:64, :]))
            bc_dve = P.e("dve", lambda e: e.tensor_copy(bcb[:], ps[4][64:96, :]),
                         inc="dve")

            tap("bcb", bcb[:], P.barrier_snapshot())
            del_done = {}
            for g in range(NG):
                bank = ps[2 + g % 2]
                P.w("pe", dve=bc_dve)
                if g >= 2:
                    P.w("pe", dve=del_done[g - 2])
                dtw = (s_dtproj[0:64, g * 128:(g + 1) * 128] if g < 8 else
                       s_dtproj[64:128, (g - 8) * 128:(g - 7) * 128])
                dtr = dt_t[0:64, :] if g < 8 else dt_t[64:128, :]
                pe_c = P.e("pe", lambda e, b=bank, w_=dtw, r_=dtr: e.matmul(
                    b[:], w_, r_, start=True, stop=True), inc="pe")
                P.w("act", pe=pe_c)
                if g == 0:
                    P.w("act", dve=fh["xn_done"])
                if g >= 2:
                    P.w("act", dve=del_done[g - 2])
                rt = (0 if g % 2 == 0 else 3) * T
                at = rt + T
                P.e("act", lambda e, b=bank, g=g, rt=rt: e.activation(
                    A_tmp[:, rt:rt + T], b[:], AF.Relu, bias=s_dtb[:, g:g + 1]))
                P.e("act", lambda e, b=bank, g=g, at=at: e.activation(
                    A_tmp[:, at:at + T], b[:], AF.Abs, bias=s_dtb[:, g:g + 1]))
                P.e("act", lambda e, at=at: e.activation(
                    A_tmp[:, 2 * T:3 * T], A_tmp[:, at:at + T], AF.Exp,
                    scale=s_cst[:, 0:1]))
                ac = P.e("act", lambda e, at=at: e.activation(
                    A_tmp[:, at:at + T], A_tmp[:, 2 * T:3 * T], AF.Ln, bias=1.0),
                    inc="act")
                P.w("dve", act=ac)
                del_done[g] = P.e("dve", lambda e, g=g, rt=rt, at=at:
                                  e.tensor_tensor(
                    dsl(g), A_tmp[:, rt:rt + T], A_tmp[:, at:at + T], OP.add),
                    inc="dve")
            dt_done = P.barrier_snapshot()
            tap("delta", A_delta[:], dt_done)

            bc_ev = {}
            for j in range(32):
                bank = ps[4 + j % 2]
                P.w("pe", dve=bc_dve)
                if j >= 2:
                    P.w("pe", act=bc_ev[j - 2])
                pe_c = P.e("pe", lambda e, b=bank, j=j: e.matmul(
                    b[:], s_selbc[:, j * 128:(j + 1) * 128], bcb[:],
                    start=True, stop=True), inc="pe")
                P.w("act", pe=pe_c)
                bc_ev[j] = P.e("act", lambda e, b=bank, j=j: e.activation(
                    A_big4[:, j * T:(j + 1) * T], b[:], AF.Identity), inc="act")

            scan_pre = P.barrier_snapshot()
            idmm_rel = {0: 0, 1: 0}
            yfin = {}
            QT = 4 * T
            for g in range(NG):
                P.w("dve", **{k: v for k, v in scan_pre.items() if k != "dve"})
                du_c = P.e("dve", lambda e, g=g: e.tensor_tensor(
                    du_t[:], dsl(g), usl(g), OP.mult), inc="dve")
                if g == NG - 1:
                    fh["du15"] = du_c
                if g >= 2:
                    P.w("act", pe=idmm_rel[0])
                ud_ac = P.e("act", lambda e, g=g: e.activation(
                    ud_t[:], usl(g), AF.Identity, scale=s_dcol[:, g:g + 1]),
                    inc="act")
                for q in range(4):
                    dab = A_dA[:, (q % 2) * 2064:(q % 2 + 1) * 2064]
                    dbb = A_dBu[:, (q % 2) * 2064:(q % 2 + 1) * 2064]
                    hb = A_h[:, 0:2064]
                    P.w("act", pe=idmm_rel[q % 2])
                    P.w("act", dve=scan_pre["dve"])
                    for n4 in range(4):
                        n = 4 * q + n4
                        ac = P.e("act", lambda e, d=dab, n4=n4, g=g, n=n:
                                 e.activation(d[:, n4 * 516 + 4:(n4 + 1) * 516],
                                              dsl(g), AF.Exp,
                                              scale=s_asc[:, g * NS + n:g * NS + n + 1]),
                                 inc="act" if n4 == 3 else None)
                    if g == NG - 1 and q == 3:
                        fh["exp15"] = ac
                    P.e("dve", lambda e, d=dbb, g=g, q=q: e.tensor_copy(
                        d[:, 0:2064:516],
                        carry[:, g * NS + 4 * q:g * NS + 4 * q + 4]))
                    for n4 in range(4):
                        n = 4 * q + n4
                        P.e("dve", lambda e, d=dbb, n4=n4, n=n:
                            e.tensor_tensor(d[:, n4 * 516 + 4:(n4 + 1) * 516],
                                            du_t[:],
                                            A_big4[:, n * T:(n + 1) * T], OP.mult))
                    if q == 0 and g >= 1:
                        P.w("dve", pe=idmm_rel[1])
                        yfin[g - 1] = P.e("dve", lambda e, b=ps[4 + (g - 1) % 2],
                                          g1=g - 1: e.tensor_tensor(
                            ysl(g1), b[:], srsl(g1), OP.mult), inc="dve")
                    P.w("dve", act=ac)
                    ts = P.e("dve", lambda e, da=dab, db=dbb, hb=hb:
                             e.tensor_tensor_scan(hb, da, db, 0.0, OP.mult, OP.add),
                             inc="dve")
                    P.w("dve", dve=ts)
                    P.e("dve", lambda e, hb=hb, g=g, q=q: e.tensor_copy(
                        carry[:, g * NS + 4 * q:g * NS + 4 * q + 4],
                        hb[:, 515:2064:516]))
                    for n4 in range(4):
                        n = 4 * q + n4
                        dv = P.e("dve", lambda e, hb=hb, d=dab, n4=n4, n=n:
                                 e.tensor_tensor(d[:, n4 * 516 + 4:(n4 + 1) * 516],
                                                 hb[:, n4 * 516 + 4:(n4 + 1) * 516],
                                                 A_big4[:, 8192 + n * T:8192 + (n + 1) * T],
                                                 OP.mult),
                                 inc="dve" if n4 == 3 else None)
                    ybank = ps[4 + g % 2]
                    P.w("pe", dve=dv)
                    if q == 0:
                        if g >= 2:
                            P.w("pe", dve=yfin[g - 2])
                        P.w("pe", act=ud_ac)
                        P.e("pe", lambda e, b=ybank: e.matmul(
                            b[:], s_id[:], ud_t[:], start=True, stop=False))
                    for n4 in range(4):
                        pe_c = P.e("pe", lambda e, b=ybank, d=dab, n4=n4,
                                   last=(q == 3 and n4 == 3):
                                   e.matmul(b[:], s_id[:],
                                            d[:, n4 * 516 + 4:(n4 + 1) * 516],
                                            start=False, stop=last),
                                   inc="pe" if n4 == 3 else None)
                    idmm_rel[q % 2] = pe_c
            P.w("dve", pe=idmm_rel[1])
            yfin[NG - 1] = P.e("dve", lambda e, b=ps[4 + (NG - 1) % 2]:
                               e.tensor_tensor(
                ysl(NG - 1), b[:], srsl(NG - 1), OP.mult), inc="dve")
            scan_done = P.barrier_snapshot()
            tap("y", A_xm[:, 0:8192], scan_done)

            P.wait_barrier("pe", scan_done)
            for j in range(2):
                slot, sidx, wl_t = op_slots[j]
                P.w("pe", **wl_t)
                for k8 in range(8):
                    kabs = j * 8 + k8
                    for mt in range(8):
                        pe_c = P.e("pe", lambda e, b=ps[mt], s=slot, k8=k8, mt=mt,
                                   ka=kabs, st=(kabs == 0), sp=(kabs == 15):
                                   e.matmul(b[:], s[:, k8 * 1024 + mt * 128:
                                                   k8 * 1024 + (mt + 1) * 128],
                                            ysl(ka), start=st, stop=sp),
                                   inc="pe" if (k8 == 7 and mt == 7) else None)
                release_slot(sidx)
            P.w("act", pe=pe_c)
            for mt in range(8):
                a_mt = P.e("act", lambda e, mt=mt: e.activation(
                    m_f32[:, mt * T:(mt + 1) * T], ps[mt][:], AF.Identity),
                    inc="act")
                P.w("dve", act=a_mt)
                P.e("dve", lambda e, mt=mt: e.tensor_copy(
                    xnsl(mt), ps[mt][:]), inc="dve" if mt == 7 else None)
            op_done = P.barrier_snapshot()
            fh["op_done"] = op_done
            tap("m", A_x[:].bitcast(F32), op_done)

            gel_ev = {}
            for j in range(4):
                slot, sidx, wl_t = w1_slots[j]
                for mt in range(8):
                    mabs = j * 8 + mt
                    bank = ps[mt % 2]
                    P.w("pe", **wl_t)
                    if mabs == 0:
                        P.wait_barrier("pe", op_done)
                    if mabs >= 2:
                        P.w("pe", act=gel_ev[mabs - 2])
                    for k in range(NKD):
                        pe_c = P.e("pe", lambda e, b=bank, s=slot, k=k, mt=mt:
                                   e.matmul(b[:], s[:, k * 1024 + mt * 128:
                                                   k * 1024 + (mt + 1) * 128],
                                            xnsl(k), start=(k == 0), stop=(k == 7)),
                                   inc="pe" if k == 7 else None)
                    P.w("act", pe=pe_c)
                    gel_ev[mabs] = P.e("act", lambda e, b=bank, mabs=mabs:
                                       e.activation(A_big4[:, mabs * T:(mabs + 1) * T],
                                                    b[:], AF.Gelu,
                                                    bias=s_b1[:, mabs:mabs + 1]),
                                       inc="act")
                release_slot(sidx)

            if c + 1 < NCH:
                front_head(c + 1, False)

            for j in range(4):
                slot, sidx, wl_t = w2_slots[j]
                P.w("pe", **wl_t)
                if j == 0:
                    P.w("pe", act=gel_ev[31])
                    P.w("pe", dve=fh["murow"])
                for k8 in range(8):
                    kabs = j * 8 + k8
                    for mt in range(8):
                        pe_c = P.e("pe", lambda e, b=ps[mt], s=slot, k8=k8, mt=mt,
                                   ka=kabs, st=(kabs == 0), sp=(kabs == 31):
                                   e.matmul(b[:], s[:, k8 * 1024 + mt * 128:
                                                   k8 * 1024 + (mt + 1) * 128],
                                            A_big4[:, ka * T:(ka + 1) * T],
                                            start=st, stop=sp),
                                   inc="pe" if (k8 == 7 and mt == 7) else None)
                release_slot(sidx)
            P.w("dve", pe=pe_c)
            cdone = None
            for mt in range(8):
                cdone = P.e("dve", lambda e, mt=mt, c=c: e.scalar_tensor_tensor(
                    c_f32[:, mt * T:(mt + 1) * T], ps[mt][:], s_b2[:, mt:mt + 1],
                    m_f32[:, mt * T:(mt + 1) * T], OP.add, OP.add,
                    accum_out=stats_c[:, c * 8 + mt:c * 8 + mt + 1]), inc="dve")
            P.w("act", dve=cdone)
            sq_done = None
            sqd = {}
            for mt in range(8):
                if mt >= 2:
                    P.w("act", dve=sqd[mt - 2])
                sa = P.e("act", lambda e, mt=mt: e.activation(
                    A_tmp[:, (mt % 2) * T:(mt % 2 + 1) * T],
                    c_f32[:, mt * T:(mt + 1) * T], AF.Square), inc="act")
                P.w("dve", act=sa)
                sq_done = P.e("dve", lambda e, mt=mt, c=c: e.tensor_reduce(
                    stats_c2[:, c * 8 + mt:c * 8 + mt + 1],
                    A_tmp[:, (mt % 2) * T:(mt % 2 + 1) * T],
                    axis=mybir.AxisListType.X, op=OP.add), inc="dve")
                sqd[mt] = sq_done
            P.w("sync", dve=cdone)
            cstore_sl[0] = P.e("sync", lambda e, c=c: e.dma_start(
                out=c_scr[:, c * T:(c + 1) * T].rearrange("(m p) t -> p m t", p=128),
                in_=c_f32[:, 0:4096].rearrange("p (m t) -> p m t", m=8)),
                inc="sl", n=16)
            ctap = P.barrier_snapshot()
            tap("c", c_f32[:, 0:4096], ctap)
        snapA = P.barrier_snapshot()
        P.wait_barrier("dve", snapA)

        def dstep(fn):
            v = P.e("dve", fn, inc="dve")
            P.w("dve", dve=v)
            return v

        dstep(lambda e: e.tensor_reduce(sc_red[:, 0:1], stats_c[:],
                                        axis=mybir.AxisListType.X, op=OP.add))
        dstep(lambda e: e.tensor_reduce(sc_red[:, 1:2], stats_c2[:],
                                        axis=mybir.AxisListType.X, op=OP.add))
        rdve = dstep(lambda e: e.tensor_copy(sc_red_r[:], sc_red[:]))
        P.w("pe", dve=rdve)
        P.e("pe", lambda e: e.matmul(ps[1][0:1, 0:2], s_oneskr[:], sc_red_r[:],
                                     start=True, stop=True), inc="pe")
        pe_fin = P.cnt["pe"]
        P.w("dve", pe=pe_fin)
        NEL = float(L * DM)
        dstep(lambda e: e.tensor_scalar(t_s1[0:1, 0:2], ps[1][0:1, 0:2],
                                        1.0 / NEL, None, op0=OP.mult))
        dstep(lambda e: e.tensor_tensor(t_s1[0:1, 2:3], t_s1[0:1, 0:1],
                                        t_s1[0:1, 0:1], OP.mult))
        vdve = dstep(lambda e: e.tensor_tensor(t_s1[0:1, 3:4], t_s1[0:1, 1:2],
                                               t_s1[0:1, 2:3], OP.subtract))
        P.w("act", dve=vdve)
        aa = P.e("act", lambda e: e.activation(t_s1[0:1, 4:5], t_s1[0:1, 3:4],
                                               AF.Ln, bias=s_cst[0:1, 2:3]),
                 inc="act")
        P.w("act", act=aa)
        adone = P.e("act", lambda e: e.activation(t_s1[0:1, 5:6], t_s1[0:1, 4:5],
                                                  AF.Exp, scale=s_cst[0:1, 1:2]),
                    inc="act")
        P.w("dve", act=adone)
        dstep(lambda e: e.tensor_tensor(t_s1[0:1, 6:7], t_s1[0:1, 0:1],
                                        t_s1[0:1, 5:6], OP.mult))
        dstep(lambda e: e.tensor_scalar(t_s1[0:1, 6:7], t_s1[0:1, 6:7], -1.0,
                                        None, op0=OP.mult))
        rdve2 = dstep(lambda e: e.tensor_copy(t_s1r[0:1, 0:2], t_s1[0:1, 5:7]))
        P.w("pe", dve=rdve2)
        P.e("pe", lambda e: e.matmul(ps[2][:, 0:2], s_ones1r[:], t_s1r[0:1, 0:2],
                                     start=True, stop=True), inc="pe")
        pe_b = P.cnt["pe"]
        P.w("dve", pe=pe_b)
        dstep(lambda e: e.tensor_copy(rstd_col[:], ps[2][:, 0:1]))
        dstep(lambda e: e.tensor_copy(nmr_col[:], ps[2][:, 1:2]))
        statsB = P.barrier_snapshot()
        tap("slab", t_s1[0:1, 0:8], statsB)

        ostore_sl = [0]
        ping_f32 = [x2_r, A_xm[:].bitcast(F32)[:, 0:4096]]
        pb = {"pe_stats": P.cnt["pe"], "pe_wfc": P.cnt["pe"],
              "st_done": P.cnt["dve"], "hq31": P.cnt["act"], "x2f": {}}

        def pb_load(c):
            xl = P.e("gp", lambda e, c=c: e.dma_start(
                out=A_x[:].rearrange("p (k t) -> p k t", k=NKD),
                in_=x_t[:, c * T:(c + 1) * T].rearrange("(k p) t -> p k t", p=128)),
                inc="xv", n=16)
            cl = P.e("sync", lambda e, c=c: e.dma_start(
                out=c_f32[:, 0:4096].rearrange("p (m t) -> p m t", m=8),
                in_=c_scr[:, c * T:(c + 1) * T].rearrange("(m p) t -> p m t", p=128)),
                inc="cl", n=16)
            pb[("xl", c)], pb[("cl", c)] = xl, cl

        def pb_front1(c):
            if ("xl", c) not in pb:
                snapB = P.barrier_snapshot()
                P.wait_barrier("gp", snapB)
                P.wait_barrier("sync", snapB)
                pb_load(c)
            xl, cl = pb.pop(("xl", c)), pb.pop(("cl", c))
            fc_slots = []
            for j in range(4):
                srcw = wfct[:, j * 1024:(j + 1) * 1024].rearrange(
                    "(k p) m -> p k m", p=128)
                fc_slots.append(fill_slot(srcw))

            x2f = ping_f32[c % 2]
            pb["x2f"][c] = x2f
            P.w("dve", xv=xl, cl=cl, sl=ostore_sl[0], act=pb["hq31"])
            x2d = None
            for mt in range(8):
                P.e("dve", lambda e, x2f=x2f, mt=mt: e.tensor_scalar(
                    A_tmp[:, 0:T], c_f32[:, mt * T:(mt + 1) * T],
                    rstd_col[:, 0:1], nmr_col[:, 0:1], op0=OP.mult, op1=OP.add))
                x2d = P.e("dve", lambda e, x2f=x2f, mt=mt: e.tensor_tensor(
                    x2f[:, mt * T:(mt + 1) * T], A_tmp[:, 0:T],
                    A_x[:, mt * T:(mt + 1) * T], OP.add),
                    inc="dve" if mt == 7 else None)
            if c + 1 < NCH:
                P.w("gp", dve=x2d)
                P.w("sync", dve=x2d)
                pb_load(c + 1)

            def x2sl(k, x2f=x2f):
                return x2f[:, k * T:(k + 1) * T]
            P.w("dve", pe=pb["pe_stats"])
            for mt in range(8):
                P.e("dve", lambda e, x2f=x2f, mt=mt: e.tensor_copy(
                    A_dA[:, mt * T:(mt + 1) * T],
                    x2f[:, mt * T:(mt + 1) * T]), inc="dve" if mt == 7 else None)

            def x2b(k):
                return A_dA[:, k * T:(k + 1) * T]
            ln_stats(P, ps, x2b, A_h[:, 0:2 * T], ra, rb, rrb,
                     A_tmp[0:1, 0:T], s_oneskb, s_ones1b, s_cst, None,
                     P.barrier_snapshot(),
                     snap_act={"pe": pb["pe_stats"], "dve": P.cnt["dve"],
                               "act": 0})
            pb["pe_stats"] = P.cnt["pe"]
            P.w("dve", pe=pb["pe_wfc"])
            for k in range(NKD):
                P.e("dve", lambda e, k=k, x2sl=x2sl: e.tensor_tensor(
                    A_tmp[:, 0:T], x2sl(k), ps[6][:], OP.subtract))
                P.e("dve", lambda e, k=k: e.tensor_tensor(
                    xnsl(k), A_tmp[:, 0:T], ps[7][:], OP.mult), inc="dve")
            ln2_done = P.barrier_snapshot()
            tap("x2", x2f[:, 0:4096], ln2_done)

            hq_ev = {}
            for j in range(4):
                slot, sidx, wl_t = fc_slots[j]
                for mt in range(8):
                    mabs = j * 8 + mt
                    bank = ps[mt % 2]
                    P.w("pe", **wl_t)
                    if mabs == 0:
                        P.wait_barrier("pe", ln2_done)
                    if mabs >= 2:
                        P.w("pe", act=hq_ev[mabs - 2])
                    for k in range(NKD):
                        pe_c = P.e("pe", lambda e, b=bank, s=slot, k=k, mt=mt:
                                   e.matmul(b[:], s[:, k * 1024 + mt * 128:
                                                   k * 1024 + (mt + 1) * 128],
                                            xnsl(k), start=(k == 0), stop=(k == 7)),
                                   inc="pe" if k == 7 else None)
                    P.w("act", pe=pe_c)
                    P.e("act", lambda e, b=bank, mabs=mabs: e.activation(
                        A_tmp[:, (mabs % 2) * T:(mabs % 2 + 1) * T], b[:], AF.Relu))
                    hq_ev[mabs] = P.e("act", lambda e, mabs=mabs: e.activation(
                        A_big4[:, mabs * T:(mabs + 1) * T],
                        A_tmp[:, (mabs % 2) * T:(mabs % 2 + 1) * T], AF.Square),
                        inc="act")
                release_slot(sidx)
            pb["pe_wfc"] = P.cnt["pe"]
            pb["hq31"] = hq_ev[31]
            return hq_ev[31]

        def pb_front2(c, hq31):
            pj_slots = []
            for j in range(4):
                srcw = wprojt[j * 1024:(j + 1) * 1024, :].rearrange(
                    "(k p) m -> p k m", p=128)
                pj_slots.append(fill_slot(srcw))
            pe_c = None
            for j in range(4):
                slot, sidx, wl_t = pj_slots[j]
                P.w("pe", **wl_t)
                if j == 0:
                    P.w("pe", act=hq31)
                for k8 in range(8):
                    kabs = j * 8 + k8
                    for mt in range(8):
                        pe_c = P.e("pe", lambda e, b=ps[mt], s=slot, k8=k8, mt=mt,
                                   st=(kabs == 0), sp=(kabs == 31), ka=kabs:
                                   e.matmul(b[:], s[:, k8 * 1024 + mt * 128:
                                                   k8 * 1024 + (mt + 1) * 128],
                                            A_big4[:, ka * T:(ka + 1) * T],
                                            start=st, stop=sp),
                                   inc="pe" if (k8 == 7 and mt == 7) else None)
                release_slot(sidx)
            P.w("act", pe=pe_c, dve=pb["st_done"])
            ysv = None
            for mt in range(8):
                ysv = P.e("act", lambda e, mt=mt: e.activation(
                    A_delta[:, mt * T:(mt + 1) * T], ps[mt][:], AF.Identity),
                    inc="act" if mt == 7 else None)
            ys_done = P.barrier_snapshot()
            tap("ys", A_delta[:, 0:4096], ys_done)
            pb[("ysv", c)] = ysv

        def pb_spike(c):
            ysv = pb.pop(("ysv", c))
            x2f = pb["x2f"].pop(c)
            P.w("dve", act=ysv)
            st_done = None
            for mt in range(8):
                y_ = A_delta[:, mt * T:(mt + 1) * T]
                spk = A_dBu[:, mt * T:(mt + 1) * T]
                mem = A_delta[:, 4096 + mt * T:4096 + (mt + 1) * T]
                o_ = x2f[:, mt * T:(mt + 1) * T]
                P.e("dve", lambda e, y_=y_, spk=spk: e.tensor_scalar(
                    spk, y_, 1.0, None, op0=OP.is_gt))
                P.e("dve", lambda e, o_=o_, spk=spk: e.scalar_tensor_tensor(
                    o_, spk, 0.2, o_, OP.mult, OP.add))
                P.e("dve", lambda e, y_=y_, mem=mem: e.tensor_scalar(
                    mem, y_, 1.9, None, op0=OP.mult))
                P.e("dve", lambda e, mem=mem, spk=spk: e.tensor_tensor(
                    mem, mem, spk, OP.subtract))
                for it in range(3):
                    P.e("dve", lambda e, mem=mem, spk=spk: e.tensor_scalar(
                        spk, mem, 1.0, None, op0=OP.is_gt))
                    P.e("dve", lambda e, o_=o_, spk=spk: e.scalar_tensor_tensor(
                        o_, spk, 0.2, o_, OP.mult, OP.add))
                    P.e("dve", lambda e, mem=mem: e.tensor_scalar(
                        mem, mem, 0.9, None, op0=OP.mult))
                    P.e("dve", lambda e, mem=mem, y_=y_: e.tensor_tensor(
                        mem, mem, y_, OP.add))
                    P.e("dve", lambda e, mem=mem, spk=spk: e.tensor_tensor(
                        mem, mem, spk, OP.subtract))
                P.e("dve", lambda e, mem=mem, spk=spk: e.tensor_scalar(
                    spk, mem, 1.0, None, op0=OP.is_gt))
                st_done = P.e("dve", lambda e, o_=o_, spk=spk:
                              e.scalar_tensor_tensor(
                    o_, spk, 0.2, o_, OP.mult, OP.add), inc="dve")
            pb["st_done"] = st_done
            P.w("sync", dve=st_done)
            ostore_sl[0] = P.e("sync", lambda e, c=c, x2f=x2f: e.dma_start(
                out=out_t[:, c * T:(c + 1) * T].rearrange("(m p) t -> p m t", p=128),
                in_=x2f[:, 0:4096].rearrange("p (m t) -> p m t", m=8)),
                inc="sl", n=16)

        for c in range(NCH):
            hq31 = pb_front1(c)
            if c > 0:
                pb_spike(c - 1)
            pb_front2(c, hq31)
        pb_spike(NCH - 1)

        P.w("sync", sl=P.cnt["sl"], xv=P.cnt["xv"], cl=P.cnt["cl"],
            ws0=P.cnt["ws0"], ws1=P.cnt["ws1"])

        @block.sync
        def _(e):
            for fn in P.q["sync"]:
                fn(e)

        @block.vector
        def _(e):
            for fn in P.q["dve"]:
                fn(e)

        @block.scalar
        def _(e):
            for fn in P.q["act"]:
                fn(e)

        @block.tensor
        def _(e):
            for fn in P.q["pe"]:
                fn(e)

        @block.gpsimd
        def _(e):
            for fn in P.q["gp"]:
                fn(e)

    return nc


def ln_stats(P, ps, xslfn, sqbuf, ra, rb, rrb, rc,
             s_oneskb, s_ones1b, s_cst, xl, snap, snap_act=None, xsem="xv"):
    if xl is not None:
        P.w("pe", **{xsem: xl})
        P.w("act", **{xsem: xl})
    P.wait_barrier("pe", snap)
    P.wait_barrier("act", snap_act if snap_act is not None else snap)
    sq_pe = {}
    for k in range(NKD):
        P.e("pe", lambda e, k=k: e.matmul(ps[6][0:1, :], s_oneskb[:], xslfn(k),
                                          start=(k == 0), stop=(k == NKD - 1)),
            inc="pe" if k == NKD - 1 else None)
    for k in range(NKD):
        buf = sqbuf[:, (k % 2) * T:(k % 2 + 1) * T]
        if k >= 2:
            P.w("act", pe=sq_pe[k - 2])
        a_k = P.e("act", lambda e, b=buf, k=k: e.activation(
            b, xslfn(k), AF.Square), inc="act")
        P.w("pe", act=a_k)
        sq_pe[k] = P.e("pe", lambda e, b=buf, k=k: e.matmul(
            ps[7][0:1, :], s_oneskb[:], b,
            start=(k == 0), stop=(k == NKD - 1)), inc="pe")
    pe_c = P.cnt["pe"]
    P.w("dve", pe=pe_c)
    P.e("dve", lambda e: e.tensor_copy(ra[:], ps[6][0:1, :]))
    P.e("dve", lambda e: e.tensor_copy(rb[:], ps[7][0:1, :]))
    P.e("dve", lambda e: e.tensor_scalar(ra[:], ra[:], 1.0 / DM, None,
                                         op0=OP.mult))
    P.e("dve", lambda e: e.tensor_scalar(rb[:], rb[:], 1.0 / DM, None,
                                         op0=OP.mult))
    P.e("dve", lambda e: e.tensor_tensor(rc, ra[:], ra[:], OP.mult))
    vd = P.e("dve", lambda e: e.tensor_tensor(rb[:], rb[:], rc, OP.subtract),
             inc="dve")
    P.w("act", dve=vd)
    P.e("act", lambda e: e.activation(rc, rb[:], AF.Ln, bias=s_cst[0:1, 2:3]))
    ad = P.e("act", lambda e: e.activation(rb[:], rc, AF.Exp,
                                           scale=s_cst[0:1, 1:2]), inc="act")
    P.w("dve", act=ad)
    P.e("dve", lambda e: e.tensor_copy(rrb[0:1, 0:T], ra[:]))
    cd = P.e("dve", lambda e: e.tensor_copy(rrb[0:1, T:2 * T], rb[:]), inc="dve")
    P.w("pe", dve=cd)
    P.e("pe", lambda e: e.matmul(ps[6][:], s_ones1b[:], rrb[0:1, 0:T],
                                 start=True, stop=True))
    bc = P.e("pe", lambda e: e.matmul(ps[7][:], s_ones1b[:], rrb[0:1, T:2 * T],
                                      start=True, stop=True), inc="pe")
    P.w("dve", pe=bc)



L_FULL = 8192


def kernel(**inputs):
    maps = host_prep(inputs, L_FULL)
    nc = build(L_FULL)
    from concourse.bass_utils import run_bass_kernel_spmd
    res = run_bass_kernel_spmd(nc, maps, core_ids=list(range(8)))
    out = np.stack([np.ascontiguousarray(res.results[b]["out_t"].T)
                    for b in range(8)], axis=0)
    return out


# version 10
# speedup vs baseline: 1.0196x; 1.0021x over previous
import contextlib
from collections import defaultdict
import numpy as np
import ml_dtypes
import concourse.bass as bass
import concourse.mybir as mybir

F32 = mybir.dt.float32
BF16 = mybir.dt.bfloat16
F32R = mybir.dt.float32r
AF = mybir.ActivationFunctionType
OP = mybir.AluOpType

DM, DI, NS, DC, DTR = 1024, 2048, 16, 4, 64
EPS = 1e-5
NKD = DM // 128
NG = DI // 128
T = 512
NSLOT = 2

ENGS = ("sync", "dve", "act", "pe", "gp")


class Prog:

    def __init__(self, nc, sems):
        self.nc = nc
        self.sems = sems
        self.q = {e: [] for e in ENGS}
        self.cnt = defaultdict(int)
        self.trace = []

    def w(self, eng, **waits):
        for s, v in waits.items():
            if v <= 0:
                continue
            sem = self.sems[s]
            self.trace.append(("wait", eng, s, v))
            self.q[eng].append(lambda e, sem=sem, v=v: e.wait_ge(sem, v))

    def w_lazy(self, eng, s, cell):
        sem = self.sems[s]
        self.trace.append(("waitlazy", eng, s, cell))
        def fn(e, sem=sem, cell=cell):
            if cell[0] > 0:
                e.wait_ge(sem, cell[0])
        self.q[eng].append(fn)

    def e(self, eng, fn, inc=None, n=1):
        if inc is None:
            self.trace.append(("op", eng, None, 0))
            self.q[eng].append(lambda e, fn=fn: fn(e))
            return None
        sem = self.sems[inc]
        self.cnt[inc] += n
        self.trace.append(("op", eng, inc, n))
        self.q[eng].append(lambda e, fn=fn, sem=sem, n=n: fn(e).then_inc(sem, n))
        return self.cnt[inc]

    def barrier_snapshot(self):
        return {k: self.cnt[k] for k in ("pe", "dve", "act")}

    def wait_barrier(self, eng, snap):
        self.w(eng, **{k: v for k, v in snap.items() if k != _ENG2SEM.get(eng)})


_ENG2SEM = {"pe": "pe", "dve": "dve", "act": "act"}


def host_prep(inp, L):
    bf = ml_dtypes.bfloat16
    f32 = np.float32
    c = np.ascontiguousarray
    w = {}
    w["win_t"] = c(inp["L0_attn_in_proj_w"].T)
    cw = inp["L0_attn_conv_w"]
    CW = np.zeros((128, DC * NG * 128), f32)
    ar = np.arange(128)
    for k in range(DC):
        for g in range(NG):
            CW[ar, (k * NG + g) * 128 + ar] = cw[g * 128:(g + 1) * 128, 0, k]
    w["cdiag"] = CW
    w["convb"] = c(inp["L0_attn_conv_b"].reshape(NG, 128).T)
    w["xprojt"] = c(inp["L0_attn_x_proj_w"].T.reshape(NG, 128, 96)
                    .transpose(1, 0, 2).reshape(128, NG * 96))
    dtp = inp["L0_attn_dt_proj_w"].T
    w["dtprojt"] = np.concatenate([dtp[:, :1024], dtp[:, 1024:]], 0)
    w["dtb"] = c(inp["L0_attn_dt_proj_b"].reshape(NG, 128).T)
    w["alogt"] = c(inp["L0_attn_A_log"].reshape(NG, 128, NS)
                   .transpose(1, 0, 2).reshape(128, NG * NS))
    w["dcol"] = c(inp["L0_attn_D"].reshape(NG, 128).T)
    w["outprojt"] = c(inp["L0_attn_out_proj_w"].T)
    w["w1t"] = c(inp["L0_attn_ffn_w1"].T)
    w["b1c"] = c(inp["L0_attn_ffn_b1"].reshape(32, 128).T)
    w["w2t"] = c(inp["L0_attn_ffn_w2"].T)
    w["b2c"] = c(inp["L0_attn_ffn_b2"].reshape(8, 128).T)
    w["wfct"] = c(inp["L0_ffn_W_fc"].T)
    w["wprojt"] = c(inp["L0_ffn_W_proj"].T)
    sel = np.zeros((32, 32 * 128), bf)
    for j in range(32):
        sel[j, j * 128:(j + 1) * 128] = 1
    w["selbc"] = sel
    w["id128"] = np.eye(128, dtype=bf)
    w["ones1"] = np.ones((1, 128), f32)
    w["onesk"] = np.ones((128, 1), f32)
    cst = np.zeros((128, 4), f32)
    cst[:, 0] = -1.0
    cst[:, 1] = -0.5
    cst[:, 2] = EPS
    w["cst"] = cst
    maps = []
    x = inp["L0_attn_seq_embeddings"]
    for b in range(8):
        m = dict(w)
        m["x_t"] = c(x[b, :L, :].T)
        maps.append(m)
    return maps


def build(L, taps=()):
    NCH = L // T
    nc = bass.Bass("TRN2", target_bir_lowering=False)
    D = {}

    def din(name, shape, dt=F32):
        D[name] = nc.dram_tensor(name, list(shape), dt, kind="ExternalInput").ap()
        return D[name]

    x_t = din("x_t", (DM, L))
    win_t = din("win_t", (DM, 4096))
    cdiag = din("cdiag", (128, 8192))
    convb = din("convb", (128, NG))
    xprojt = din("xprojt", (128, NG * 96))
    dtprojt = din("dtprojt", (128, 1024))
    dtb = din("dtb", (128, NG))
    alogt = din("alogt", (128, NG * NS))
    dcol = din("dcol", (128, NG))
    outprojt = din("outprojt", (DI, DM))
    w1t = din("w1t", (DM, 4096))
    b1c = din("b1c", (128, 32))
    w2t = din("w2t", (4096, DM))
    b2c = din("b2c", (128, 8))
    wfct = din("wfct", (DM, 4096))
    wprojt = din("wprojt", (4096, DM))
    selbc_d = din("selbc", (32, 4096), BF16)
    id128_d = din("id128", (128, 128), BF16)
    ones1_d = din("ones1", (1, 128))
    onesk_d = din("onesk", (128, 1))
    cst_d = din("cst", (128, 4))

    out_t = nc.dram_tensor("out_t", [DM, L], F32, kind="ExternalOutput").ap()
    c_scr = nc.dram_tensor("c_scr", [DM, L], F32).ap()
    x_bf = nc.dram_tensor("x_bf", [DM, L], BF16).ap()
    tap_d = {}
    for tspec in taps:
        tname, tshape = tspec[0], tspec[1]
        tdt = tspec[2] if len(tspec) > 2 else F32
        tap_d[tname] = nc.dram_tensor("tap_" + tname, list(tshape), tdt,
                                      kind="ExternalOutput").ap()

    es = contextlib.ExitStack()
    with es:
        def sb(name, shape, dt):
            return es.enter_context(nc.sbuf_tensor(name, list(shape), dt))

        ring = [sb(f"ring{i}", (128, 8192), BF16) for i in range(NSLOT)]
        A_x = sb("A_x", (128, 4096), F32R)
        A_xn = sb("A_xn", (128, 4096), BF16)
        A_xm = sb("A_xm", (128, NG * 516), BF16)
        A_sres = sb("A_sres", (128, 8192), BF16)
        A_big4 = sb("A_big4", (128, 16384), BF16)
        A_u = sb("A_u", (128, 8192), BF16)
        A_delta = sb("A_delta", (128, 8192), BF16)
        A_dA = sb("A_dA", (128, 4128), BF16)
        A_dBu = sb("A_dBu", (128, 4128), BF16)
        A_h = sb("A_h", (128, 2064), BF16)
        A_tmp = sb("A_tmp", (128, 2560), F32)
        dt_t = sb("dt_t", (128, T), BF16)
        bcb = sb("bcb", (32, T), BF16)
        carry = sb("carry", (128, NG * NS), BF16)
        xm_tail = sb("xm_tail", (128, NG * 3), BF16)
        fx1 = sb("fx1", (128, 4), F32)
        fx2 = sb("fx2", (128, 4), F32)
        stats_c = sb("stats_c", (128, NCH * 8), F32)
        stats_c2 = sb("stats_c2", (128, NCH * 8), F32)
        ra = A_tmp[0:1, T:2 * T]
        rb = A_tmp[0:1, 2 * T:3 * T]
        sc_red = sb("sc_red", (128, 2), F32)
        sc_red_r = sb("sc_red_r", (128, 2), F32R)
        t_s1 = sb("t_s1", (1, 8), F32)
        t_s1r = sb("t_s1r", (1, 2), F32R)
        rstd_col = sb("rstd_col", (128, 1), F32)
        nmr_col = sb("nmr_col", (128, 1), F32)
        s_xproj = sb("s_xproj", (128, NG * 96), BF16)
        s_dtproj = sb("s_dtproj", (128, 1024), BF16)
        s_dcol = sb("s_dcol", (128, NG), F32)
        ud_t = sb("ud_t", (128, T), BF16)
        du_t = sb("du_t", (128, T), BF16)
        s_selbc = sb("s_selbc", (32, 4096), BF16)
        s_id = sb("s_id", (128, 128), BF16)
        s_ones1r = sb("s_ones1r", (1, 128), F32R)
        s_oneskr = sb("s_oneskr", (128, 1), F32R)
        s_asc = sb("s_asc", (128, NG * NS), F32)
        s_convb = sb("s_convb", (128, NG), F32)
        s_dtb = sb("s_dtb", (128, NG), F32)
        s_b1 = sb("s_b1", (128, 32), F32)
        s_b2 = sb("s_b2", (128, 8), F32)
        s_cst = sb("s_cst", (128, 4), F32)
        s_oneskb = sb("s_oneskb", (128, 1), BF16)
        s_ones1b = sb("s_ones1b", (1, 128), BF16)
        rrb = A_tmp[:].bitcast(BF16)[0:1, 0:2 * T]

        ps = [es.enter_context(nc.psum_tensor(f"ps{i}", [128, T], F32))
              for i in range(8)]

        sems = {n: es.enter_context(nc.semaphore("sem_" + n))
                for n in ("ws0", "ws1", "xv", "cl", "sl", "pe", "dve", "act")}
        block = es.enter_context(nc.Block())
        P = Prog(nc, sems)

        du_bf = None
        m_f32 = A_x[:].bitcast(F32)
        c_f32 = A_u[:].bitcast(F32)
        x2_r = A_sres[:].bitcast(F32)
        ys_f32 = A_delta[:].bitcast(F32)
        mem_f32 = A_xm[:, 0:8192].bitcast(F32)

        def xsl(k):
            return A_delta[:, k * T:(k + 1) * T]

        def xnsl(k):
            return A_xn[:, k * T:(k + 1) * T]

        def usl(g):
            return A_u[:, g * T:(g + 1) * T]

        def dsl(g):
            return A_delta[:, g * T:(g + 1) * T]

        def ysl(g):
            return A_xm[:, g * T:(g + 1) * T]

        def srsl(g):
            return A_sres[:, g * T:(g + 1) * T]

        ring_state = {"fill": 0, "relcell": [[0] for _ in range(NSLOT)]}

        def fill_slot(src_ap):
            s = ring_state["fill"] % NSLOT
            ring_state["fill"] += 1
            P.w_lazy("gp", "pe", ring_state["relcell"][s])
            newcell = [0]
            ring_state["relcell"][s] = newcell
            sname = f"ws{s}"
            wt = P.e("gp", lambda e, dst=ring[s], src=src_ap:
                     e.dma_start(out=dst[:].rearrange("p (k m) -> p k m",
                                                      k=src.shape[1]),
                                 in_=src), inc=sname, n=16)
            return ring[s], newcell, {sname: wt}

        def release_slot(cell):
            cell[0] = P.cnt["pe"]

        P.e("gp", lambda e: e.dma_start(out=s_xproj[:], in_=xprojt[:]), inc="xv", n=16)
        P.e("gp", lambda e: e.dma_start(out=s_dtproj[:], in_=dtprojt[:]), inc="xv", n=16)
        P.e("gp", lambda e: e.dma_start(out=s_dcol[:], in_=dcol[:]), inc="xv", n=16)
        P.e("gp", lambda e: e.dma_start(out=s_ones1r[:], in_=ones1_d[:]), inc="xv", n=16)
        P.e("gp", lambda e: e.dma_start(out=s_oneskr[:], in_=onesk_d[:]), inc="xv", n=16)
        P.e("gp", lambda e: e.dma_start(out=s_ones1b[:], in_=ones1_d[:]), inc="xv", n=16)
        P.e("gp", lambda e: e.dma_start(out=s_oneskb[:], in_=onesk_d[:]), inc="xv", n=16)
        P.e("sync", lambda e: e.dma_start(out=s_selbc[:], in_=selbc_d[:]), inc="sl", n=16)
        P.e("sync", lambda e: e.dma_start(out=s_id[:], in_=id128_d[:]), inc="sl", n=16)
        P.e("gp", lambda e: e.dma_start(out=A_x[:, 0:NG * NS], in_=alogt[:]), inc="xv", n=16)
        P.e("sync", lambda e: e.dma_start(out=s_convb[:], in_=convb[:]), inc="sl", n=16)
        P.e("sync", lambda e: e.dma_start(out=s_dtb[:], in_=dtb[:]), inc="sl", n=16)
        P.e("sync", lambda e: e.dma_start(out=s_b1[:], in_=b1c[:]), inc="sl", n=16)
        P.e("sync", lambda e: e.dma_start(out=s_b2[:], in_=b2c[:]), inc="sl", n=16)
        P.e("sync", lambda e: e.dma_start(out=s_cst[:], in_=cst_d[:]), inc="sl", n=16)
        xbf_done = P.e("gp", lambda e: e.dma_start(
            out=x_bf.rearrange("(k p) t -> p k t", p=128),
            in_=x_t[:].rearrange("(k p) t -> p k t", p=128)), inc="xv", n=16)
        pro_wl, pro_sl = P.cnt["xv"], P.cnt["sl"]
        P.w("gp", xv=pro_wl)

        P.w("act", sl=pro_sl, xv=pro_wl)
        P.e("act", lambda e: e.activation(s_asc[:], A_x[:, 0:NG * NS], AF.Exp),
            inc="act")
        P.w("dve", act=P.cnt["act"])
        P.e("dve", lambda e: e.tensor_scalar(s_asc[:], s_asc[:], -1.0, None,
                                             op0=OP.mult), inc="dve")
        P.e("dve", lambda e: e.memset(A_xm[:], 0.0), inc="dve")
        P.e("dve", lambda e: e.memset(carry[:], 0.0), inc="dve")
        mfc = P.e("dve", lambda e: e.memset(A_dA[:], 1.0), inc="dve")
        P.w("dve", dve=mfc)
        P.e("dve", lambda e: e.memset(A_dA[:, 0:4128:516], 0.0))
        P.e("dve", lambda e: e.memset(A_dBu[:], 0.0), inc="dve")

        def tap(name, tile_ap, wait_snap):
            if name in tap_d:
                P.wait_barrier("sync", wait_snap)
                tv = P.e("sync", lambda e, t=tile_ap, o=tap_d[name]:
                         e.dma_start(out=o[:], in_=t), inc="sl", n=16)
                for eng in ("dve", "act", "pe", "gp"):
                    P.w(eng, sl=tv)

        cstore_sl = [0]
        fh = {}

        def front_head(c, first):
            if first:
                snapH = P.barrier_snapshot()
                P.wait_barrier("sync", snapH)
                P.w("sync", xv=xbf_done)
                snapL = snapH
            else:
                P.w("sync", dve=fh["du15"], act=fh["exp15"])
                snapL = fh["op_done"]
            xl = P.e("sync", lambda e, c=c: e.dma_start(
                out=A_delta[:, 0:4096].rearrange("p (k t) -> p k t", k=NKD),
                in_=x_bf[:, c * T:(c + 1) * T].rearrange("(k p) t -> p k t", p=128)),
                inc="cl", n=16)
            ln_stats(P, ps, xsl, A_h[:, 0:2 * T], ra, rb, rrb,
                     A_tmp[0:1, 0:T], s_oneskb, s_ones1b, s_cst, xl, snapL,
                     xsem="cl")
            mu_sb = A_tmp[:, 0:T]
            rstd_sb = A_tmp[:, T:2 * T]
            P.e("dve", lambda e: e.tensor_copy(mu_sb, ps[6][:]))
            fh["murow"] = P.e("dve", lambda e: e.tensor_copy(rstd_sb, ps[7][:]),
                              inc="dve")
            if not first:
                P.w("dve", pe=P.cnt["pe"])
            xd = None
            for k in range(NKD):
                P.e("dve", lambda e, k=k, m_=mu_sb: e.tensor_tensor(
                    xnsl(k), xsl(k), m_, OP.subtract))
                xd = P.e("dve", lambda e, k=k, r_=rstd_sb: e.tensor_tensor(
                    xnsl(k), xnsl(k), r_, OP.mult),
                    inc="dve" if k == NKD - 1 else None)
            fh["xn_done"] = xd
            tap("xn", A_xn[:], P.barrier_snapshot())

        for c in range(NCH):
            if c == NCH - 1:
                tap("carry", carry[:], P.barrier_snapshot())
            if tap_d:
                for eng in ("dve", "act", "pe", "gp", "sync"):
                    P.w(eng, sl=P.cnt["sl"], wl=P.cnt["wl"])
            if c == 0:
                front_head(0, True)
            ip_slots = []
            for j in range(4):
                srcw = win_t[:, j * 1024:(j + 1) * 1024].rearrange(
                    "(k p) m -> p k m", p=128)
                ip_slots.append(fill_slot(srcw))
            cd_slot = fill_slot(cdiag[:].rearrange("p (k m) -> p k m", k=1))
            op_slots = []
            for j in range(2):
                srcw = outprojt[j * 1024:(j + 1) * 1024, :].rearrange(
                    "(k p) m -> p k m", p=128)
                op_slots.append(fill_slot(srcw))
            w1_slots = []
            for j in range(4):
                srcw = w1t[:, j * 1024:(j + 1) * 1024].rearrange(
                    "(k p) m -> p k m", p=128)
                w1_slots.append(fill_slot(srcw))
            w2_slots = []
            for j in range(4):
                srcw = w2t[j * 1024:(j + 1) * 1024, :].rearrange(
                    "(k p) m -> p k m", p=128)
                w2_slots.append(fill_slot(srcw))
            if c > 0:
                for g in range(NG):
                    P.e("dve", lambda e, g=g: e.tensor_copy(
                        A_xm[:, g * 516:g * 516 + 3],
                        xm_tail[:, g * 3:(g + 1) * 3]), inc="dve")

            evac_cnt = {}
            for j in range(4):
                slot, sidx, wl_t = ip_slots[j]
                for mt in range(8):
                    mabs = j * 8 + mt
                    bank = ps[mt % 2]
                    P.w("pe", dve=(fh.get("cdone2", P.cnt["dve"]) if mabs == 0
                                   else 0), **wl_t)
                    if mabs >= 2:
                        P.w("pe", **{evac_cnt[mabs - 2][0]: evac_cnt[mabs - 2][1]})
                    for k in range(NKD):
                        pe_c = P.e("pe", lambda e, b=bank, s=slot, k=k, mt=mt:
                                   e.matmul(b[:], s[:, k * 1024 + mt * 128:
                                                   k * 1024 + (mt + 1) * 128],
                                            xnsl(k), start=(k == 0), stop=(k == 7)),
                                   inc="pe" if k == 7 else None)
                    if mabs < NG:
                        P.w("dve", pe=pe_c)
                        dc = P.e("dve", lambda e, b=bank, g=mabs: e.tensor_copy(
                            A_xm[:, g * 516 + 3:g * 516 + 515], b[:]), inc="dve")
                        evac_cnt[mabs] = ("dve", dc)
                    else:
                        P.w("act", pe=pe_c)
                        ac = P.e("act", lambda e, b=bank, g=mabs - NG: e.activation(
                            srsl(g), b[:], AF.Silu), inc="act")
                        evac_cnt[mabs] = ("act", ac)
                release_slot(sidx)

            cslot, cs_idx, cwl = cd_slot
            P.w("act", sl=cstore_sl[0])
            prev_u = {}
            for g in range(NG):
                bank = ps[2 + g % 2]
                P.w("pe", **cwl)
                if g == 0:
                    P.w("pe", **{evac_cnt[15][0]: evac_cnt[15][1]})
                    P.w("pe", dve=P.cnt["dve"])
                if g >= 2:
                    P.w("pe", act=prev_u[g - 2])
                for k in range(DC):
                    pe_c = P.e("pe", lambda e, b=bank, s=cslot, k=k, g=g:
                               e.matmul(b[:], s[:, (k * NG + g) * 128:
                                               (k * NG + g + 1) * 128],
                                        A_xm[:, g * 516 + k:g * 516 + k + T],
                                        start=(k == 0), stop=(k == DC - 1)),
                               inc="pe" if k == DC - 1 else None)
                P.w("act", pe=pe_c)
                prev_u[g] = P.e("act", lambda e, b=bank, g=g: e.activation(
                    usl(g), b[:], AF.Silu, bias=s_convb[:, g:g + 1]), inc="act")
            release_slot(cs_idx)
            for g in range(NG):
                P.e("dve", lambda e, g=g: e.tensor_copy(
                    xm_tail[:, g * 3:(g + 1) * 3],
                    A_xm[:, g * 516 + 512:g * 516 + 515]), inc="dve")
            conv_done = P.barrier_snapshot()
            tap("u", A_u[:], conv_done)
            tap("xm", A_xm[:], conv_done)
            tap("sres", A_sres[:], conv_done)
            tap("wslot", ring[0][:], conv_done)

            P.w("pe", act=prev_u[NG - 1])
            for g in range(NG):
                pe_c = P.e("pe", lambda e, g=g: e.matmul(
                    ps[4][0:96, :], s_xproj[:, g * 96:(g + 1) * 96], usl(g),
                    start=(g == 0), stop=(g == NG - 1)),
                    inc="pe" if g == NG - 1 else None)
            P.w("dve", pe=pe_c)
            P.e("dve", lambda e: e.tensor_copy(dt_t[0:64, :], ps[4][0:64, :]))
            P.e("dve", lambda e: e.tensor_copy(dt_t[64:128, :], ps[4][0:64, :]))
            bc_dve = P.e("dve", lambda e: e.tensor_copy(bcb[:], ps[4][64:96, :]),
                         inc="dve")

            tap("bcb", bcb[:], P.barrier_snapshot())
            del_done = {}
            for g in range(NG):
                bank = ps[2 + g % 2]
                P.w("pe", dve=bc_dve)
                if g >= 2:
                    P.w("pe", dve=del_done[g - 2])
                dtw = (s_dtproj[0:64, g * 128:(g + 1) * 128] if g < 8 else
                       s_dtproj[64:128, (g - 8) * 128:(g - 7) * 128])
                dtr = dt_t[0:64, :] if g < 8 else dt_t[64:128, :]
                pe_c = P.e("pe", lambda e, b=bank, w_=dtw, r_=dtr: e.matmul(
                    b[:], w_, r_, start=True, stop=True), inc="pe")
                P.w("act", pe=pe_c)
                if g == 0:
                    P.w("act", dve=fh["xn_done"])
                if g >= 2:
                    P.w("act", dve=del_done[g - 2])
                rt = (0 if g % 2 == 0 else 3) * T
                at = rt + T
                P.e("act", lambda e, b=bank, g=g, rt=rt: e.activation(
                    A_tmp[:, rt:rt + T], b[:], AF.Relu, bias=s_dtb[:, g:g + 1]))
                P.e("act", lambda e, b=bank, g=g, at=at: e.activation(
                    A_tmp[:, at:at + T], b[:], AF.Abs, bias=s_dtb[:, g:g + 1]))
                P.e("act", lambda e, at=at: e.activation(
                    A_tmp[:, 2 * T:3 * T], A_tmp[:, at:at + T], AF.Exp,
                    scale=s_cst[:, 0:1]))
                ac = P.e("act", lambda e, at=at: e.activation(
                    A_tmp[:, at:at + T], A_tmp[:, 2 * T:3 * T], AF.Ln, bias=1.0),
                    inc="act")
                P.w("dve", act=ac)
                del_done[g] = P.e("dve", lambda e, g=g, rt=rt, at=at:
                                  e.tensor_tensor(
                    dsl(g), A_tmp[:, rt:rt + T], A_tmp[:, at:at + T], OP.add),
                    inc="dve")
            dt_done = P.barrier_snapshot()
            tap("delta", A_delta[:], dt_done)

            bc_ev = {}
            for j in range(32):
                bank = ps[4 + j % 2]
                P.w("pe", dve=bc_dve)
                if j >= 2:
                    P.w("pe", act=bc_ev[j - 2])
                pe_c = P.e("pe", lambda e, b=bank, j=j: e.matmul(
                    b[:], s_selbc[:, j * 128:(j + 1) * 128], bcb[:],
                    start=True, stop=True), inc="pe")
                P.w("act", pe=pe_c)
                bc_ev[j] = P.e("act", lambda e, b=bank, j=j: e.activation(
                    A_big4[:, j * T:(j + 1) * T], b[:], AF.Identity), inc="act")

            scan_pre = P.barrier_snapshot()
            idmm_rel = {0: 0, 1: 0}
            yfin = {}
            QT = 4 * T
            for g in range(NG):
                P.w("dve", **{k: v for k, v in scan_pre.items() if k != "dve"})
                du_c = P.e("dve", lambda e, g=g: e.tensor_tensor(
                    du_t[:], dsl(g), usl(g), OP.mult), inc="dve")
                if g == NG - 1:
                    fh["du15"] = du_c
                if g >= 2:
                    P.w("act", pe=idmm_rel[0])
                ud_ac = P.e("act", lambda e, g=g: e.activation(
                    ud_t[:], usl(g), AF.Identity, scale=s_dcol[:, g:g + 1]),
                    inc="act")
                for q in range(4):
                    dab = A_dA[:, (q % 2) * 2064:(q % 2 + 1) * 2064]
                    dbb = A_dBu[:, (q % 2) * 2064:(q % 2 + 1) * 2064]
                    hb = A_h[:, 0:2064]
                    P.w("act", pe=idmm_rel[q % 2])
                    P.w("act", dve=scan_pre["dve"])
                    for n4 in range(4):
                        n = 4 * q + n4
                        ac = P.e("act", lambda e, d=dab, n4=n4, g=g, n=n:
                                 e.activation(d[:, n4 * 516 + 4:(n4 + 1) * 516],
                                              dsl(g), AF.Exp,
                                              scale=s_asc[:, g * NS + n:g * NS + n + 1]),
                                 inc="act" if n4 == 3 else None)
                    if g == NG - 1 and q == 3:
                        fh["exp15"] = ac
                    P.e("dve", lambda e, d=dbb, g=g, q=q: e.tensor_copy(
                        d[:, 0:2064:516],
                        carry[:, g * NS + 4 * q:g * NS + 4 * q + 4]))
                    for n4 in range(4):
                        n = 4 * q + n4
                        P.e("dve", lambda e, d=dbb, n4=n4, n=n:
                            e.tensor_tensor(d[:, n4 * 516 + 4:(n4 + 1) * 516],
                                            du_t[:],
                                            A_big4[:, n * T:(n + 1) * T], OP.mult))
                    if q == 0 and g >= 1:
                        P.w("dve", pe=idmm_rel[1])
                        yfin[g - 1] = P.e("dve", lambda e, b=ps[4 + (g - 1) % 2],
                                          g1=g - 1: e.tensor_tensor(
                            ysl(g1), b[:], srsl(g1), OP.mult), inc="dve")
                    P.w("dve", act=ac)
                    ts = P.e("dve", lambda e, da=dab, db=dbb, hb=hb:
                             e.tensor_tensor_scan(hb, da, db, 0.0, OP.mult, OP.add),
                             inc="dve")
                    P.w("dve", dve=ts)
                    P.e("dve", lambda e, hb=hb, g=g, q=q: e.tensor_copy(
                        carry[:, g * NS + 4 * q:g * NS + 4 * q + 4],
                        hb[:, 515:2064:516]))
                    for n4 in range(4):
                        n = 4 * q + n4
                        dv = P.e("dve", lambda e, hb=hb, d=dab, n4=n4, n=n:
                                 e.tensor_tensor(d[:, n4 * 516 + 4:(n4 + 1) * 516],
                                                 hb[:, n4 * 516 + 4:(n4 + 1) * 516],
                                                 A_big4[:, 8192 + n * T:8192 + (n + 1) * T],
                                                 OP.mult),
                                 inc="dve" if n4 == 3 else None)
                    ybank = ps[4 + g % 2]
                    P.w("pe", dve=dv)
                    if q == 0:
                        if g >= 2:
                            P.w("pe", dve=yfin[g - 2])
                        P.w("pe", act=ud_ac)
                        P.e("pe", lambda e, b=ybank: e.matmul(
                            b[:], s_id[:], ud_t[:], start=True, stop=False))
                    for n4 in range(4):
                        pe_c = P.e("pe", lambda e, b=ybank, d=dab, n4=n4,
                                   last=(q == 3 and n4 == 3):
                                   e.matmul(b[:], s_id[:],
                                            d[:, n4 * 516 + 4:(n4 + 1) * 516],
                                            start=False, stop=last),
                                   inc="pe" if n4 == 3 else None)
                    idmm_rel[q % 2] = pe_c
            P.w("dve", pe=idmm_rel[1])
            yfin[NG - 1] = P.e("dve", lambda e, b=ps[4 + (NG - 1) % 2]:
                               e.tensor_tensor(
                ysl(NG - 1), b[:], srsl(NG - 1), OP.mult), inc="dve")
            scan_done = P.barrier_snapshot()
            tap("y", A_xm[:, 0:8192], scan_done)

            P.wait_barrier("pe", scan_done)
            for j in range(2):
                slot, sidx, wl_t = op_slots[j]
                P.w("pe", **wl_t)
                for k8 in range(8):
                    kabs = j * 8 + k8
                    for mt in range(8):
                        pe_c = P.e("pe", lambda e, b=ps[mt], s=slot, k8=k8, mt=mt,
                                   ka=kabs, st=(kabs == 0), sp=(kabs == 15):
                                   e.matmul(b[:], s[:, k8 * 1024 + mt * 128:
                                                   k8 * 1024 + (mt + 1) * 128],
                                            ysl(ka), start=st, stop=sp),
                                   inc="pe" if (k8 == 7 and mt == 7) else None)
                release_slot(sidx)
            P.w("act", pe=pe_c)
            for mt in range(8):
                a_mt = P.e("act", lambda e, mt=mt: e.activation(
                    m_f32[:, mt * T:(mt + 1) * T], ps[mt][:], AF.Identity),
                    inc="act")
                P.w("dve", act=a_mt)
                P.e("dve", lambda e, mt=mt: e.tensor_copy(
                    xnsl(mt), ps[mt][:]), inc="dve" if mt == 7 else None)
            op_done = P.barrier_snapshot()
            fh["op_done"] = op_done
            tap("m", A_x[:].bitcast(F32), op_done)

            gel_ev = {}
            for j in range(4):
                slot, sidx, wl_t = w1_slots[j]
                for mt in range(8):
                    mabs = j * 8 + mt
                    bank = ps[mt % 2]
                    P.w("pe", **wl_t)
                    if mabs == 0:
                        P.wait_barrier("pe", op_done)
                    if mabs >= 2:
                        P.w("pe", act=gel_ev[mabs - 2])
                    for k in range(NKD):
                        pe_c = P.e("pe", lambda e, b=bank, s=slot, k=k, mt=mt:
                                   e.matmul(b[:], s[:, k * 1024 + mt * 128:
                                                   k * 1024 + (mt + 1) * 128],
                                            xnsl(k), start=(k == 0), stop=(k == 7)),
                                   inc="pe" if k == 7 else None)
                    P.w("act", pe=pe_c)
                    gel_ev[mabs] = P.e("act", lambda e, b=bank, mabs=mabs:
                                       e.activation(A_big4[:, mabs * T:(mabs + 1) * T],
                                                    b[:], AF.Gelu,
                                                    bias=s_b1[:, mabs:mabs + 1]),
                                       inc="act")
                release_slot(sidx)

            if c + 1 < NCH:
                front_head(c + 1, False)

            for j in range(4):
                slot, sidx, wl_t = w2_slots[j]
                P.w("pe", **wl_t)
                if j == 0:
                    P.w("pe", act=gel_ev[31])
                    P.w("pe", dve=fh["murow"])
                for k8 in range(8):
                    kabs = j * 8 + k8
                    for mt in range(8):
                        pe_c = P.e("pe", lambda e, b=ps[mt], s=slot, k8=k8, mt=mt,
                                   ka=kabs, st=(kabs == 0), sp=(kabs == 31):
                                   e.matmul(b[:], s[:, k8 * 1024 + mt * 128:
                                                   k8 * 1024 + (mt + 1) * 128],
                                            A_big4[:, ka * T:(ka + 1) * T],
                                            start=st, stop=sp),
                                   inc="pe" if (k8 == 7 and mt == 7) else None)
                release_slot(sidx)
            P.w("dve", pe=pe_c)
            cdone = None
            for mt in range(8):
                cdone = P.e("dve", lambda e, mt=mt, c=c: e.scalar_tensor_tensor(
                    c_f32[:, mt * T:(mt + 1) * T], ps[mt][:], s_b2[:, mt:mt + 1],
                    m_f32[:, mt * T:(mt + 1) * T], OP.add, OP.add,
                    accum_out=stats_c[:, c * 8 + mt:c * 8 + mt + 1]), inc="dve")
                if mt == 1:
                    fh["cdone2"] = cdone
            P.w("act", dve=cdone)
            sq_done = None
            sqd = {}
            for mt in range(8):
                if mt >= 2:
                    P.w("act", dve=sqd[mt - 2])
                sa = P.e("act", lambda e, mt=mt: e.activation(
                    A_tmp[:, (mt % 2) * T:(mt % 2 + 1) * T],
                    c_f32[:, mt * T:(mt + 1) * T], AF.Square), inc="act")
                P.w("dve", act=sa)
                sq_done = P.e("dve", lambda e, mt=mt, c=c: e.tensor_reduce(
                    stats_c2[:, c * 8 + mt:c * 8 + mt + 1],
                    A_tmp[:, (mt % 2) * T:(mt % 2 + 1) * T],
                    axis=mybir.AxisListType.X, op=OP.add), inc="dve")
                sqd[mt] = sq_done
            P.w("sync", dve=cdone)
            cstore_sl[0] = P.e("sync", lambda e, c=c: e.dma_start(
                out=c_scr[:, c * T:(c + 1) * T].rearrange("(m p) t -> p m t", p=128),
                in_=c_f32[:, 0:4096].rearrange("p (m t) -> p m t", m=8)),
                inc="sl", n=16)
            ctap = P.barrier_snapshot()
            tap("c", c_f32[:, 0:4096], ctap)
        snapA = P.barrier_snapshot()
        P.wait_barrier("dve", snapA)

        def dstep(fn):
            v = P.e("dve", fn, inc="dve")
            P.w("dve", dve=v)
            return v

        dstep(lambda e: e.tensor_reduce(sc_red[:, 0:1], stats_c[:],
                                        axis=mybir.AxisListType.X, op=OP.add))
        dstep(lambda e: e.tensor_reduce(sc_red[:, 1:2], stats_c2[:],
                                        axis=mybir.AxisListType.X, op=OP.add))
        rdve = dstep(lambda e: e.tensor_copy(sc_red_r[:], sc_red[:]))
        P.w("pe", dve=rdve)
        P.e("pe", lambda e: e.matmul(ps[1][0:1, 0:2], s_oneskr[:], sc_red_r[:],
                                     start=True, stop=True), inc="pe")
        pe_fin = P.cnt["pe"]
        P.w("dve", pe=pe_fin)
        NEL = float(L * DM)
        dstep(lambda e: e.tensor_scalar(t_s1[0:1, 0:2], ps[1][0:1, 0:2],
                                        1.0 / NEL, None, op0=OP.mult))
        dstep(lambda e: e.tensor_tensor(t_s1[0:1, 2:3], t_s1[0:1, 0:1],
                                        t_s1[0:1, 0:1], OP.mult))
        vdve = dstep(lambda e: e.tensor_tensor(t_s1[0:1, 3:4], t_s1[0:1, 1:2],
                                               t_s1[0:1, 2:3], OP.subtract))
        P.w("act", dve=vdve)
        aa = P.e("act", lambda e: e.activation(t_s1[0:1, 4:5], t_s1[0:1, 3:4],
                                               AF.Ln, bias=s_cst[0:1, 2:3]),
                 inc="act")
        P.w("act", act=aa)
        adone = P.e("act", lambda e: e.activation(t_s1[0:1, 5:6], t_s1[0:1, 4:5],
                                                  AF.Exp, scale=s_cst[0:1, 1:2]),
                    inc="act")
        P.w("dve", act=adone)
        dstep(lambda e: e.tensor_tensor(t_s1[0:1, 6:7], t_s1[0:1, 0:1],
                                        t_s1[0:1, 5:6], OP.mult))
        dstep(lambda e: e.tensor_scalar(t_s1[0:1, 6:7], t_s1[0:1, 6:7], -1.0,
                                        None, op0=OP.mult))
        rdve2 = dstep(lambda e: e.tensor_copy(t_s1r[0:1, 0:2], t_s1[0:1, 5:7]))
        P.w("pe", dve=rdve2)
        P.e("pe", lambda e: e.matmul(ps[2][:, 0:2], s_ones1r[:], t_s1r[0:1, 0:2],
                                     start=True, stop=True), inc="pe")
        pe_b = P.cnt["pe"]
        P.w("dve", pe=pe_b)
        dstep(lambda e: e.tensor_copy(rstd_col[:], ps[2][:, 0:1]))
        dstep(lambda e: e.tensor_copy(nmr_col[:], ps[2][:, 1:2]))
        statsB = P.barrier_snapshot()
        tap("slab", t_s1[0:1, 0:8], statsB)

        ostore_sl = [0]
        ping_f32 = [x2_r, A_xm[:].bitcast(F32)[:, 0:4096]]
        pb = {"pe_stats": P.cnt["pe"], "pe_wfc": P.cnt["pe"],
              "st_done": P.cnt["dve"], "hq31": P.cnt["act"], "x2f": {}}

        def pb_load(c):
            xl = P.e("gp", lambda e, c=c: e.dma_start(
                out=A_x[:].rearrange("p (k t) -> p k t", k=NKD),
                in_=x_t[:, c * T:(c + 1) * T].rearrange("(k p) t -> p k t", p=128)),
                inc="xv", n=16)
            cl = P.e("sync", lambda e, c=c: e.dma_start(
                out=c_f32[:, 0:4096].rearrange("p (m t) -> p m t", m=8),
                in_=c_scr[:, c * T:(c + 1) * T].rearrange("(m p) t -> p m t", p=128)),
                inc="cl", n=16)
            pb[("xl", c)], pb[("cl", c)] = xl, cl

        def pb_front1(c):
            if ("xl", c) not in pb:
                snapB = P.barrier_snapshot()
                P.wait_barrier("gp", snapB)
                P.wait_barrier("sync", snapB)
                pb_load(c)
            xl, cl = pb.pop(("xl", c)), pb.pop(("cl", c))
            fc_slots = []
            for j in range(4):
                srcw = wfct[:, j * 1024:(j + 1) * 1024].rearrange(
                    "(k p) m -> p k m", p=128)
                fc_slots.append(fill_slot(srcw))

            x2f = ping_f32[c % 2]
            pb["x2f"][c] = x2f
            P.w("dve", xv=xl, cl=cl, sl=ostore_sl[0], act=pb["hq31"])
            x2d = None
            for mt in range(8):
                P.e("dve", lambda e, x2f=x2f, mt=mt: e.tensor_scalar(
                    A_tmp[:, 0:T], c_f32[:, mt * T:(mt + 1) * T],
                    rstd_col[:, 0:1], nmr_col[:, 0:1], op0=OP.mult, op1=OP.add))
                x2d = P.e("dve", lambda e, x2f=x2f, mt=mt: e.tensor_tensor(
                    x2f[:, mt * T:(mt + 1) * T], A_tmp[:, 0:T],
                    A_x[:, mt * T:(mt + 1) * T], OP.add),
                    inc="dve" if mt == 7 else None)
            if c + 1 < NCH:
                P.w("gp", dve=x2d)
                P.w("sync", dve=x2d)
                pb_load(c + 1)

            def x2sl(k, x2f=x2f):
                return x2f[:, k * T:(k + 1) * T]
            P.w("dve", pe=pb["pe_stats"])
            for mt in range(8):
                P.e("dve", lambda e, x2f=x2f, mt=mt: e.tensor_copy(
                    A_dA[:, mt * T:(mt + 1) * T],
                    x2f[:, mt * T:(mt + 1) * T]), inc="dve" if mt == 7 else None)

            def x2b(k):
                return A_dA[:, k * T:(k + 1) * T]
            ln_stats(P, ps, x2b, A_h[:, 0:2 * T], ra, rb, rrb,
                     A_tmp[0:1, 0:T], s_oneskb, s_ones1b, s_cst, None,
                     P.barrier_snapshot(),
                     snap_act={"pe": pb["pe_stats"], "dve": P.cnt["dve"],
                               "act": 0})
            pb["pe_stats"] = P.cnt["pe"]
            P.w("dve", pe=pb["pe_wfc"])
            for k in range(NKD):
                P.e("dve", lambda e, k=k, x2sl=x2sl: e.tensor_tensor(
                    A_tmp[:, 0:T], x2sl(k), ps[6][:], OP.subtract))
                P.e("dve", lambda e, k=k: e.tensor_tensor(
                    xnsl(k), A_tmp[:, 0:T], ps[7][:], OP.mult), inc="dve")
            ln2_done = P.barrier_snapshot()
            tap("x2", x2f[:, 0:4096], ln2_done)

            hq_ev = {}
            for j in range(4):
                slot, sidx, wl_t = fc_slots[j]
                for mt in range(8):
                    mabs = j * 8 + mt
                    bank = ps[mt % 2]
                    P.w("pe", **wl_t)
                    if mabs == 0:
                        P.wait_barrier("pe", ln2_done)
                    if mabs >= 2:
                        P.w("pe", act=hq_ev[mabs - 2])
                    for k in range(NKD):
                        pe_c = P.e("pe", lambda e, b=bank, s=slot, k=k, mt=mt:
                                   e.matmul(b[:], s[:, k * 1024 + mt * 128:
                                                   k * 1024 + (mt + 1) * 128],
                                            xnsl(k), start=(k == 0), stop=(k == 7)),
                                   inc="pe" if k == 7 else None)
                    P.w("act", pe=pe_c)
                    P.e("act", lambda e, b=bank, mabs=mabs: e.activation(
                        A_tmp[:, (mabs % 2) * T:(mabs % 2 + 1) * T], b[:], AF.Relu))
                    hq_ev[mabs] = P.e("act", lambda e, mabs=mabs: e.activation(
                        A_big4[:, mabs * T:(mabs + 1) * T],
                        A_tmp[:, (mabs % 2) * T:(mabs % 2 + 1) * T], AF.Square),
                        inc="act")
                release_slot(sidx)
            pb["pe_wfc"] = P.cnt["pe"]
            pb["hq31"] = hq_ev[31]
            return hq_ev[31]

        def pb_front2(c, hq31):
            pj_slots = []
            for j in range(4):
                srcw = wprojt[j * 1024:(j + 1) * 1024, :].rearrange(
                    "(k p) m -> p k m", p=128)
                pj_slots.append(fill_slot(srcw))
            pe_c = None
            for j in range(4):
                slot, sidx, wl_t = pj_slots[j]
                P.w("pe", **wl_t)
                if j == 0:
                    P.w("pe", act=hq31)
                for k8 in range(8):
                    kabs = j * 8 + k8
                    for mt in range(8):
                        pe_c = P.e("pe", lambda e, b=ps[mt], s=slot, k8=k8, mt=mt,
                                   st=(kabs == 0), sp=(kabs == 31), ka=kabs:
                                   e.matmul(b[:], s[:, k8 * 1024 + mt * 128:
                                                   k8 * 1024 + (mt + 1) * 128],
                                            A_big4[:, ka * T:(ka + 1) * T],
                                            start=st, stop=sp),
                                   inc="pe" if (k8 == 7 and mt == 7) else None)
                release_slot(sidx)
            P.w("act", pe=pe_c, dve=pb["st_done"])
            ysv = None
            for mt in range(8):
                ysv = P.e("act", lambda e, mt=mt: e.activation(
                    A_delta[:, mt * T:(mt + 1) * T], ps[mt][:], AF.Identity),
                    inc="act" if mt == 7 else None)
            ys_done = P.barrier_snapshot()
            tap("ys", A_delta[:, 0:4096], ys_done)
            pb[("ysv", c)] = ysv

        def pb_spike(c):
            ysv = pb.pop(("ysv", c))
            x2f = pb["x2f"].pop(c)
            P.w("dve", act=ysv)
            st_done = None
            for mt in range(8):
                y_ = A_delta[:, mt * T:(mt + 1) * T]
                spk = A_dBu[:, mt * T:(mt + 1) * T]
                mem = A_delta[:, 4096 + mt * T:4096 + (mt + 1) * T]
                o_ = x2f[:, mt * T:(mt + 1) * T]
                P.e("dve", lambda e, y_=y_, spk=spk: e.tensor_scalar(
                    spk, y_, 1.0, None, op0=OP.is_gt))
                P.e("dve", lambda e, o_=o_, spk=spk: e.scalar_tensor_tensor(
                    o_, spk, 0.2, o_, OP.mult, OP.add))
                P.e("dve", lambda e, y_=y_, mem=mem: e.tensor_scalar(
                    mem, y_, 1.9, None, op0=OP.mult))
                P.e("dve", lambda e, mem=mem, spk=spk: e.tensor_tensor(
                    mem, mem, spk, OP.subtract))
                for it in range(3):
                    P.e("dve", lambda e, mem=mem, spk=spk: e.tensor_scalar(
                        spk, mem, 1.0, None, op0=OP.is_gt))
                    P.e("dve", lambda e, o_=o_, spk=spk: e.scalar_tensor_tensor(
                        o_, spk, 0.2, o_, OP.mult, OP.add))
                    P.e("dve", lambda e, mem=mem: e.tensor_scalar(
                        mem, mem, 0.9, None, op0=OP.mult))
                    P.e("dve", lambda e, mem=mem, y_=y_: e.tensor_tensor(
                        mem, mem, y_, OP.add))
                    P.e("dve", lambda e, mem=mem, spk=spk: e.tensor_tensor(
                        mem, mem, spk, OP.subtract))
                P.e("dve", lambda e, mem=mem, spk=spk: e.tensor_scalar(
                    spk, mem, 1.0, None, op0=OP.is_gt))
                st_done = P.e("dve", lambda e, o_=o_, spk=spk:
                              e.scalar_tensor_tensor(
                    o_, spk, 0.2, o_, OP.mult, OP.add), inc="dve")
            pb["st_done"] = st_done
            P.w("sync", dve=st_done)
            ostore_sl[0] = P.e("sync", lambda e, c=c, x2f=x2f: e.dma_start(
                out=out_t[:, c * T:(c + 1) * T].rearrange("(m p) t -> p m t", p=128),
                in_=x2f[:, 0:4096].rearrange("p (m t) -> p m t", m=8)),
                inc="sl", n=16)

        for c in range(NCH):
            hq31 = pb_front1(c)
            if c > 0:
                pb_spike(c - 1)
            pb_front2(c, hq31)
        pb_spike(NCH - 1)

        P.w("sync", sl=P.cnt["sl"], xv=P.cnt["xv"], cl=P.cnt["cl"],
            ws0=P.cnt["ws0"], ws1=P.cnt["ws1"])

        @block.sync
        def _(e):
            for fn in P.q["sync"]:
                fn(e)

        @block.vector
        def _(e):
            for fn in P.q["dve"]:
                fn(e)

        @block.scalar
        def _(e):
            for fn in P.q["act"]:
                fn(e)

        @block.tensor
        def _(e):
            for fn in P.q["pe"]:
                fn(e)

        @block.gpsimd
        def _(e):
            for fn in P.q["gp"]:
                fn(e)

    return nc


def ln_stats(P, ps, xslfn, sqbuf, ra, rb, rrb, rc,
             s_oneskb, s_ones1b, s_cst, xl, snap, snap_act=None, xsem="xv"):
    if xl is not None:
        P.w("pe", **{xsem: xl})
        P.w("act", **{xsem: xl})
    P.wait_barrier("pe", snap)
    P.wait_barrier("act", snap_act if snap_act is not None else snap)
    sq_pe = {}
    for k in range(NKD):
        P.e("pe", lambda e, k=k: e.matmul(ps[6][0:1, :], s_oneskb[:], xslfn(k),
                                          start=(k == 0), stop=(k == NKD - 1)),
            inc="pe" if k == NKD - 1 else None)
    for k in range(NKD):
        buf = sqbuf[:, (k % 2) * T:(k % 2 + 1) * T]
        if k >= 2:
            P.w("act", pe=sq_pe[k - 2])
        a_k = P.e("act", lambda e, b=buf, k=k: e.activation(
            b, xslfn(k), AF.Square), inc="act")
        P.w("pe", act=a_k)
        sq_pe[k] = P.e("pe", lambda e, b=buf, k=k: e.matmul(
            ps[7][0:1, :], s_oneskb[:], b,
            start=(k == 0), stop=(k == NKD - 1)), inc="pe")
    pe_c = P.cnt["pe"]
    P.w("dve", pe=pe_c)
    P.e("dve", lambda e: e.tensor_copy(ra[:], ps[6][0:1, :]))
    P.e("dve", lambda e: e.tensor_copy(rb[:], ps[7][0:1, :]))
    P.e("dve", lambda e: e.tensor_scalar(ra[:], ra[:], 1.0 / DM, None,
                                         op0=OP.mult))
    P.e("dve", lambda e: e.tensor_scalar(rb[:], rb[:], 1.0 / DM, None,
                                         op0=OP.mult))
    P.e("dve", lambda e: e.tensor_tensor(rc, ra[:], ra[:], OP.mult))
    vd = P.e("dve", lambda e: e.tensor_tensor(rb[:], rb[:], rc, OP.subtract),
             inc="dve")
    P.w("act", dve=vd)
    P.e("act", lambda e: e.activation(rc, rb[:], AF.Ln, bias=s_cst[0:1, 2:3]))
    ad = P.e("act", lambda e: e.activation(rb[:], rc, AF.Exp,
                                           scale=s_cst[0:1, 1:2]), inc="act")
    P.w("dve", act=ad)
    P.e("dve", lambda e: e.tensor_copy(rrb[0:1, 0:T], ra[:]))
    cd = P.e("dve", lambda e: e.tensor_copy(rrb[0:1, T:2 * T], rb[:]), inc="dve")
    P.w("pe", dve=cd)
    P.e("pe", lambda e: e.matmul(ps[6][:], s_ones1b[:], rrb[0:1, 0:T],
                                 start=True, stop=True))
    bc = P.e("pe", lambda e: e.matmul(ps[7][:], s_ones1b[:], rrb[0:1, T:2 * T],
                                      start=True, stop=True), inc="pe")
    P.w("dve", pe=bc)



L_FULL = 8192


def kernel(**inputs):
    maps = host_prep(inputs, L_FULL)
    nc = build(L_FULL)
    from concourse.bass_utils import run_bass_kernel_spmd
    res = run_bass_kernel_spmd(nc, maps, core_ids=list(range(8)))
    out = np.stack([np.ascontiguousarray(res.results[b]["out_t"].T)
                    for b in range(8)], axis=0)
    return out
